# version 15
# baseline (speedup 1.0000x reference)
import numpy as np
import concourse.bass as bass
import concourse.mybir as mybir
from contextlib import ExitStack
from concourse.bass_utils import run_bass_kernel_spmd

F32 = mybir.dt.float32
BF16 = mybir.dt.bfloat16
AF = mybir.ActivationFunctionType
ALU = mybir.AluOpType
AX = mybir.AxisListType

D = 1024
KD = 8
E = 32
FS = 512
NB = 2
EPS = 1e-5
NEG = -1.0e30

ENGINES = ("pe", "dve", "act", "pool", "sp")


class _Chan:
    __slots__ = ("name", "h", "n")

    def __init__(self, name):
        self.name = name
        self.h = None
        self.n = 0


class _Op:
    __slots__ = ("eng", "deps", "fn", "chan", "amt")

    def __init__(self, eng, deps, fn, chan, amt):
        self.eng = eng
        self.deps = deps
        self.fn = fn
        self.chan = chan
        self.amt = amt


class _Buf:
    __slots__ = ("w", "r")

    def __init__(self, guard=()):
        self.w = list(guard)
        self.r = []


def _dedup(toks):
    best = {}
    for (c, v) in toks:
        if v > best.get(c, 0):
            best[c] = v
    return tuple(best.items())


class _Builder:
    def __init__(self):
        self.ops = []
        self.chans = []
        self.ech = {e: self.chan("c_" + e) for e in ENGINES}

    def chan(self, name):
        c = _Chan(name)
        self.chans.append(c)
        return c

    def op(self, eng, fn, deps=(), inc=None, reads=(), writes=(), partial=()):
        d = list(deps)
        for b in reads:
            d += b.w
        for b in writes:
            d += b.w + b.r
        for b in partial:
            d += b.r
        if inc is None:
            cc, amt = self.ech[eng], 1
        else:
            cc, amt = inc
        cc.n += amt
        tok = (cc, cc.n)
        self.ops.append(_Op(eng, _dedup(d), fn, cc, amt))
        for b in reads:
            b.r.append(tok)
        for b in writes:
            b.w = [tok]
            b.r = []
        for b in partial:
            b.w.append(tok)
        return tok

    def all_toks(self):
        return [(c, c.n) for c in self.chans if c.n > 0]

    def replay(self, nc, exit_stack):
        for c in self.chans:
            c.h = exit_stack.enter_context(nc.semaphore(c.name))
        hwm = {}
        per_eng = {e: [] for e in ENGINES}
        for o in self.ops:
            per_eng[o.eng].append(o)
        block = exit_stack.enter_context(nc.Block())
        engmap = {"pe": block.tensor, "dve": block.vector, "act": block.scalar,
                  "pool": block.gpsimd, "sp": block.sync}

        def make_body(ename, ops):
            def body(e):
                for o in ops:
                    for (cc, v) in o.deps:
                        if v > 0 and hwm.get((ename, cc), 0) < v:
                            e.wait_ge(cc.h, v)
                            hwm[(ename, cc)] = v
                    o.fn(e).then_inc(o.chan.h, o.amt)
            return body

        for ename in ENGINES:
            if per_eng[ename]:
                engmap[ename](make_body(ename, per_eng[ename]))


def _gp(first, *bufs):
    return (bufs, ()) if first else ((), bufs)


DBG = 0


def build_program(S):
    F = E * 128
    N = NB * S
    NT = N // 128
    TPB = S // 128

    nc = bass.Bass("TRN2", target_bir_lowering=False, debug=False)
    es = ExitStack()

    dr = lambda name, shape, dt=F32, kind="ExternalInput": \
        nc.dram_tensor(name, list(shape), dt, kind=kind).ap()
    x_in = dr("x", (NB, S, D))
    xT_in = dr("xT", (NB, KD, 128, S))
    wq_in = dr("wq", (128, KD, D))
    wk_in = dr("wk", (128, KD, D))
    wv_in = dr("wv", (128, KD, D))
    gw_in = dr("gw", (128, KD, F))
    uw_in = dr("uw", (128, KD, F))
    dw_in = dr("dw", (128, E, D))
    sgw_in = dr("sgw", (128, KD, FS))
    suw_in = dr("suw", (128, KD, FS))
    sdw_in = dr("sdw", (128, 4, D))
    rw_in = dr("rw", (128, KD, E))
    rb_in = dr("rb", (1, E))
    idn_in = dr("idn", (128, 128))
    cm_in = dr("cm", (128, 128))
    y_out = dr("y", (N, D), kind="ExternalOutput")
    if DBG:
        dbg_xatt = dr("dbg_xatt", (N, D), kind="ExternalOutput")
        dbg_li = dr("dbg_li", (NT, 128, 1), kind="ExternalOutput")

    x1_scr = dr("x1s", (N, D), kind="Internal")
    x1t_scr = dr("x1ts", (KD, 128, N), BF16, kind="Internal")
    hp_scr = dr("hps", (E, 128, N), BF16, kind="Internal")
    sa_scr = dr("sas", (4, 128, N), BF16, kind="Internal")
    w_scr = dr("ws", (E, N), kind="Internal")

    sb = lambda name, shape, dt=F32: es.enter_context(
        nc.sbuf_tensor(name, list(shape), dt))

    A1_COLS = max(8 * S + 8 * S, E * D + 2 * E * 512)
    arena1 = sb("arena1", (128, A1_COLS // 2), F32)
    a1 = arena1.bitcast(BF16)
    VB = 8 * S
    HPB = E * D

    A2_COLS = 18944
    arena2 = sb("arena2", (128, A2_COLS), F32)
    a2f = arena2
    a2 = arena2.bitcast(BF16)

    def bfv(c0f, c1f):
        return a2[:, 2 * c0f: 2 * c1f]

    def f32v(c0f, c1f):
        return a2f[:, c0f:c1f]

    idb = sb("idb", (128, 128), BF16)
    cmf = sb("cmf", (128, 128), F32)
    rbb = sb("rbb", (128, E), F32)
    rwb = sb("rwb", (128, KD * E), BF16)
    mxs = sb("mxs", (128, 16), F32)
    nmx = sb("nmx", (128, 16), F32)
    sums = sb("sums", (128, 16), F32)
    al = sb("al", (128, 16), F32)
    al2 = sb("al2", (128, 16), F32)
    gm = sb("gm", (128, 2), F32)
    gmn = sb("gmn", (128, 2), F32)
    lv = sb("lv", (128, 2), F32)
    linv = sb("linv", (128, 2), F32)
    bnb = sb("bnb", (128, 12), F32)
    bna = sb("bna", (128, 2), F32)
    sx2 = sb("sx2", (128, 2), F32)
    sq2 = sb("sq2", (128, 2), F32)
    mn1 = sb("mn1", (128, 1), F32)
    m2t = sb("m2t", (128, 1), F32)
    rstd = sb("rstd", (128, 1), F32)
    lt8 = sb("lt8", (128, 8 * E), F32)
    et8 = sb("et8", (128, 8 * E), F32)
    ms8 = sb("ms8", (128, 8 * E), F32)
    mk8 = sb("mk8", (128, 8), F32)
    m18 = sb("m18", (128, 8), F32)
    ri8 = sb("ri8", (128, 8), F32)

    ppair = [es.enter_context(nc.psum_tensor(f"pp{i}", [128, 1024], F32))
             for i in range(3)]
    p = [ppair[i // 2][:, (i % 2) * 512:(i % 2) * 512 + 512] for i in range(6)]
    p6 = es.enter_context(nc.psum_tensor("p6", [128, 1024], BF16))
    p7 = es.enter_context(nc.psum_tensor("p7", [128, 1024], BF16))
    ptr = [p6, p7]

    B = _Builder()
    ch = B.chan

    idbB, cmfB, rbbB, rwbB = _Buf(), _Buf(), _Buf(), _Buf()
    B.op("pool", lambda e: e.dma_start(out=idb[:, :], in_=idn_in),
         inc=(ch("d_idb"), 16), writes=[idbB])
    B.op("pool", lambda e: e.dma_start(
        out=rwb[:, :].rearrange("p (k e) -> p k e", k=KD), in_=rw_in),
        inc=(ch("d_rwb"), 16), writes=[rwbB])
    B.op("sp", lambda e: e.dma_start(out=cmf[:, :], in_=cm_in),
         inc=(ch("d_cmf"), 16), writes=[cmfB])
    B.op("sp", lambda e: e.dma_start(out=rbb[:, :],
                                     in_=rb_in.partition_broadcast(128)),
         inc=(ch("d_rbb"), 16), writes=[rbbB])

    ch_wk, ch_wv, ch_wq = ch("d_wk"), ch("d_wv"), ch("d_wq")
    ch_xtq = [ch("d_xtq0"), ch("d_xtq1"), ch("d_xtq2")]
    ch_xtqB2 = [ch("d_xtqB0"), ch("d_xtqB1")]
    ch_xr = [ch("d_xr0"), ch("d_xr1")]
    ch_x1w, ch_x1tw, ch_ws = ch("d_x1w"), ch("d_x1tw"), ch("d_ws")
    ch_x1tr = [ch("d_x1tr0"), ch("d_x1tr1")]
    ch_gw = [ch("d_gw0"), ch("d_gw1")]
    ch_uw = [ch("d_uw0"), ch("d_uw1")]
    ch_wf = [ch("d_wf0"), ch("d_wf1")]
    ch_hb = [ch("d_hb0"), ch("d_hb1")]
    ch_dw, ch_sdw = ch("d_dw"), ch("d_sdw")
    ch_hp = [ch("d_hp0"), ch("d_hp1")]
    ch_sab = [ch("d_sab0"), ch("d_sab1")]
    ch_xrc = [ch("d_xrc0"), ch("d_xrc1")]
    ch_out = [ch("d_out0"), ch("d_out1")]

    dbg_toks = []
    ch_dbg, ch_dbg2 = ch("d_dbg"), ch("d_dbg2")
    pB = [_Buf() for _ in range(6)]
    p6B = _Buf()
    p7B = _Buf()
    ptrB = [p6B, p7B]
    x1sB, x1tsB, wscrB = _Buf(), _Buf(), _Buf()
    hpsB = _Buf()
    sasB = _Buf()
    out_toks = []

    prev_wqbB = None
    for b in range(NB):
        guard = B.all_toks()
        if prev_wqbB is not None:
            wkbB, wvbB = _Buf(prev_wqbB.w + prev_wqbB.r), _Buf(guard)
        else:
            wkbB, wvbB = _Buf(guard), _Buf(guard)
        xtqB_ = [_Buf(guard), _Buf(guard), _Buf(guard)]
        KTB = [_Buf(guard) for _ in range(KD)]
        VBuf = _Buf(guard)

        wkb = bfv(0, 4096)
        wvb = bfv(4096, 8192)
        xtqA = [bfv(8192, 10240), bfv(10240, 12288), bfv(12288, 14336)]

        B.op("pool", lambda e: e.dma_start(
            out=wkb.rearrange("p (k d) -> p k d", k=KD), in_=wk_in),
            inc=(ch_wk, 16), writes=[wkbB])
        B.op("pool", lambda e: e.dma_start(
            out=wvb.rearrange("p (k d) -> p k d", k=KD), in_=wv_in),
            inc=(ch_wv, 16), writes=[wvbB])

        def xtq_sl(slot, k, lo, hi):
            return xtqA[slot][:, k * 512 + lo: k * 512 + hi]

        def load_xtq(cidx):
            slot = cidx % 3
            B.op("pool", lambda e, s=slot, c=cidx, b=b: e.dma_start(
                out=xtqA[s].rearrange("p (k t) -> p k t", k=KD),
                in_=xT_in[b][:, :, c * 512:(c + 1) * 512].rearrange("k p t -> p k t")),
                inc=(ch_xtq[slot], 16), writes=[xtqB_[slot]])

        nchunks = S // 512
        load_xtq(0)
        load_xtq(1)
        for i in range(nchunks // 2):
            s0, s1 = (2 * i) % 3, (2 * i + 1) % 3
            if 2 * i + 2 < nchunks:
                load_xtq(2 * i + 2)
            for m in range(KD):
                bA, bBk = (0, 1) if m % 2 == 0 else (2, 3)

                def kmm(e, m=m, bA=bA, bBk=bBk, s0=s0, s1=s1):
                    inst = None
                    for k in range(KD):
                        st = wkb[:, k * D + m * 128: k * D + (m + 1) * 128]
                        inst = e.matmul(p[bA][:, :], st, xtq_sl(s0, k, 0, 512),
                                        start=(k == 0), stop=(k == KD - 1))
                        inst = e.matmul(p[bBk][:, :], st, xtq_sl(s1, k, 0, 512),
                                        start=(k == 0), stop=(k == KD - 1))
                    return inst
                B.op("pe", kmm, reads=[wkbB, xtqB_[s0], xtqB_[s1]],
                     writes=[pB[bA], pB[bBk]])
                for (bk, half) in ((bA, 0), (bBk, 1)):
                    c0 = (2 * i + half) * 512
                    B.op("dve", lambda e, bk=bk, m=m, c0=c0: e.tensor_copy(
                        a1[:, m * S + c0: m * S + c0 + 512], p[bk][:, :]),
                        reads=[pB[bk]], partial=[KTB[m]])
            for ci in (2 * i, 2 * i + 1):
                slot = ci % 3
                if ci == 2 * i + 1 and 2 * i + 3 < nchunks:
                    load_xtq(2 * i + 3)
                for j in range(4):
                    tt = ci * 4 + j
                    b0, b1 = (4, 5) if tt % 2 == 0 else (0, 1)

                    def vmm(e, slot=slot, j=j, b0=b0, b1=b1):
                        inst = None
                        for k in range(KD):
                            st = xtq_sl(slot, k, j * 128, (j + 1) * 128)
                            inst = e.matmul(p[b0][:, :], st,
                                            wvb[:, k * D: k * D + 512],
                                            start=(k == 0), stop=(k == KD - 1))
                            inst = e.matmul(p[b1][:, :], st,
                                            wvb[:, k * D + 512: (k + 1) * D],
                                            start=(k == 0), stop=(k == KD - 1))
                        return inst
                    B.op("pe", vmm, reads=[wvbB, xtqB_[slot]],
                         writes=[pB[b0], pB[b1]])
                    for (bk, off) in ((b0, 0), (b1, 512)):
                        B.op("dve", lambda e, bk=bk, tt=tt, off=off: e.tensor_copy(
                            a1[:, VB + tt * 1024 + off: VB + tt * 1024 + off + 512],
                            p[bk][:, :]), reads=[pB[bk]], partial=[VBuf])

        guard2 = B.all_toks()
        wqbB = _Buf(guard2)
        xtqBB2 = [_Buf(guard2), _Buf(guard2)]
        QTB_ = _Buf(guard2)
        PB_ = _Buf(guard2)
        PTB_ = _Buf(guard2)
        xresB = [_Buf(guard2), _Buf(guard2)]
        x1fB, x1cB, x1TB, xtqBB = (_Buf(guard2), _Buf(guard2), _Buf(guard2),
                                   _Buf(guard2))
        mxsB = [_Buf(), _Buf()]
        nmxB = [_Buf(), _Buf()]
        sumsB = [_Buf(), _Buf()]
        alB = [_Buf(), _Buf()]
        al2B = [_Buf(), _Buf()]
        gmB = [_Buf(), _Buf()]
        gmnB = [_Buf(), _Buf()]
        lvB = [_Buf(), _Buf()]
        linvB = [_Buf(), _Buf()]
        PBs = [PB_, _Buf(guard2)]
        bnbB, bnaB, rstdB = _Buf(), _Buf(), _Buf()
        sx2B, sq2B, mn1B, m2tB = _Buf(), _Buf(), _Buf(), _Buf()

        wqb = bfv(0, 4096)
        QT = bfv(4096, 6144)
        Pbs = [bfv(6144, 8192), bfv(16896, 18944)]
        PT = bfv(8192, 8704)
        xres = [f32v(8704, 9728), f32v(9728, 10752)]
        x1tF = f32v(10752, 11776)
        x1tC = bfv(11776, 12288)
        x1TC = bfv(12288, 12800)
        xtqBs = [bfv(12800, 14848), bfv(14848, 16896)]

        B.op("pool", lambda e: e.dma_start(
            out=wqb.rearrange("p (k d) -> p k d", k=KD), in_=wq_in),
            inc=(ch_wq, 16), writes=[wqbB])
        prev_wqbB = wqbB

        pend_x1 = [None]

        def flush_x1():
            if pend_x1[0] is not None:
                pend_x1[0]()
                pend_x1[0] = None

        def load_xtqB(g):
            sl = g % 2
            B.op("pool", lambda e, g=g, b=b, sl=sl: e.dma_start(
                out=xtqBs[sl].rearrange("p (k t) -> p k t", k=KD),
                in_=xT_in[b][:, :, g * 512:(g + 1) * 512].rearrange("k p t -> p k t")),
                inc=(ch_xtqB2[sl], 16), writes=[xtqBB2[sl]])

        def emit_Q(g):
            sl = g % 2
            for m in range(KD):
                bk = m % 2

                def qmm(e, m=m, bk=bk, sl=sl):
                    inst = None
                    for k in range(KD):
                        st = wqb[:, k * D + m * 128: k * D + (m + 1) * 128]
                        inst = e.matmul(p[bk][:, :], st,
                                        xtqBs[sl][:, k * 512:(k + 1) * 512],
                                        start=(k == 0), stop=(k == KD - 1))
                    return inst
                B.op("pe", qmm, reads=[wqbB, xtqBB2[sl]], writes=[pB[bk]])
                w_, pt_ = _gp(m == 0, QTB_)
                B.op("act", lambda e, m=m, bk=bk: e.activation(
                    out=QT[:, m * 512:(m + 1) * 512], in_=p[bk][:, :],
                    func=AF.Copy, scale=1.0 / 32.0),
                    reads=[pB[bk]], writes=w_, partial=pt_)

        def emit_S(T):
            q = T % 2
            j = T % 4
            t_len = 128 * (T + 1)
            npc = (t_len + 511) // 512
            Pq = Pbs[q]
            B.op("sp", lambda e, T=T, b=b: e.dma_start(
                out=xres[q], in_=x_in[b, T * 128:(T + 1) * 128, :]),
                inc=(ch_xr[q], 16), writes=[xresB[q]])
            npc = (t_len + 1023) // 1024
            for c in range(npc):
                w_c = min(1024, t_len - 1024 * c)
                pr = c % 2
                bk0, bk1 = (0, 1) if pr == 0 else (2, 3)
                nsub = (w_c + 511) // 512

                def smm(e, j=j, c=c, w_c=w_c, pr=pr, nsub=nsub):
                    inst = None
                    for k in range(KD):
                        st = QT[:, k * 512 + j * 128: k * 512 + (j + 1) * 128]
                        for sbi in range(nsub):
                            wseg = min(512, w_c - 512 * sbi)
                            inst = e.matmul(
                                ppair[pr][:, 512 * sbi: 512 * sbi + wseg], st,
                                a1[:, k * S + 1024 * c + 512 * sbi:
                                   k * S + 1024 * c + 512 * sbi + wseg],
                                start=(k == 0), stop=(k == KD - 1))
                    return inst
                B.op("pe", smm, reads=[QTB_] + KTB, writes=[pB[bk0], pB[bk1]])
                if c == npc - 1:
                    B.op("dve", lambda e, pr=pr, w_c=w_c: e.tensor_tensor(
                        out=ppair[pr][:, w_c - 128:w_c],
                        in0=ppair[pr][:, w_c - 128:w_c],
                        in1=cmf[:, :], op=ALU.add),
                        reads=[cmfB], writes=[pB[bk0], pB[bk1]])
                w_, pt_ = _gp(c == 0, mxsB[q])
                B.op("dve", lambda e, pr=pr, c=c, w_c=w_c, q=q: e.reduce_max(
                    mxs[:, q * 8 + c:q * 8 + c + 1], ppair[pr][:, 0:w_c],
                    axis=AX.X),
                    reads=[pB[bk0], pB[bk1]], writes=w_, partial=pt_)
                w_, pt_ = _gp(c == 0, nmxB[q])
                B.op("dve", lambda e, c=c, q=q: e.tensor_scalar(
                    out=nmx[:, q * 8 + c:q * 8 + c + 1],
                    in0=mxs[:, q * 8 + c:q * 8 + c + 1], scalar1=-1.0,
                    scalar2=None, op0=ALU.mult),
                    reads=[mxsB[q]], writes=w_, partial=pt_)
                w_, pt_ = _gp(c == 0, PBs[q], sumsB[q])
                B.op("act", lambda e, pr=pr, c=c, w_c=w_c, q=q, Pq=Pq: e.activation(
                    out=Pq[:, 1024 * c: 1024 * c + w_c], in_=ppair[pr][:, 0:w_c],
                    func=AF.Exp, bias=nmx[:, q * 8 + c:q * 8 + c + 1], scale=1.0,
                    accum_out=sums[:, q * 8 + c:q * 8 + c + 1]),
                    reads=[pB[bk0], pB[bk1], nmxB[q]], writes=w_, partial=pt_)

        def emit_C(T):
            q = T % 2
            t_len = 128 * (T + 1)
            npc = (t_len + 1023) // 1024
            Pq = Pbs[q]
            if npc > 1:
                B.op("dve", lambda e, npc=npc, q=q: e.reduce_max(
                    gm[:, q:q + 1], mxs[:, q * 8:q * 8 + npc], axis=AX.X),
                    reads=[mxsB[q]], writes=[gmB[q]])
                B.op("dve", lambda e, q=q: e.tensor_scalar(
                    out=gmn[:, q:q + 1], in0=gm[:, q:q + 1], scalar1=-1.0,
                    scalar2=None, op0=ALU.mult), reads=[gmB[q]], writes=[gmnB[q]])
                B.op("act", lambda e, npc=npc, q=q: e.activation(
                    out=al[:, q * 8:q * 8 + npc], in_=mxs[:, q * 8:q * 8 + npc],
                    func=AF.Exp, bias=gmn[:, q:q + 1], scale=1.0),
                    reads=[mxsB[q], gmnB[q]], writes=[alB[q]])
                B.op("dve", lambda e, npc=npc, q=q: e.tensor_tensor(
                    out=al2[:, q * 8:q * 8 + npc], in0=al[:, q * 8:q * 8 + npc],
                    in1=sums[:, q * 8:q * 8 + npc],
                    op=ALU.mult), reads=[alB[q], sumsB[q]], writes=[al2B[q]])
                B.op("dve", lambda e, npc=npc, q=q: e.reduce_sum(
                    lv[:, q:q + 1], al2[:, q * 8:q * 8 + npc], axis=AX.X),
                    reads=[al2B[q]], writes=[lvB[q]])
                B.op("dve", lambda e, q=q: e.reciprocal(
                    linv[:, q:q + 1], lv[:, q:q + 1]),
                    reads=[lvB[q]], writes=[linvB[q]])
                for c in range(npc):
                    w_c = min(1024, t_len - 1024 * c)
                    B.op("dve", lambda e, c=c, w_c=w_c, q=q, Pq=Pq: e.tensor_scalar(
                        out=Pq[:, 1024 * c:1024 * c + w_c],
                        in0=Pq[:, 1024 * c:1024 * c + w_c],
                        scalar1=al[:, q * 8 + c:q * 8 + c + 1], scalar2=None,
                        op0=ALU.mult),
                        reads=[alB[q], PBs[q]], partial=[PBs[q]])
            else:
                B.op("dve", lambda e, q=q: e.reciprocal(
                    linv[:, q:q + 1], sums[:, q * 8:q * 8 + 1]),
                    reads=[sumsB[q]], writes=[linvB[q]])

        def make_TA(T):
            q = T % 2
            t_len = 128 * (T + 1)
            Pq = Pbs[q]

            def emit():
                flush_x1()
                nblk = t_len // 128
                ngrp = (nblk + 7) // 8

                def emit_tr(grp):
                    blocks = list(range(grp * 8, min(grp * 8 + 8, nblk)))
                    bq = grp % 2

                    def tr(e, blocks=blocks, bq=bq):
                        inst = None
                        for tt in blocks:
                            inst = e.transpose(
                                ptr[bq][:, (tt % 8) * 128:(tt % 8 + 1) * 128],
                                Pq[:, tt * 128:(tt + 1) * 128], idb[:, :])
                        return inst
                    B.op("pe", tr, reads=[PBs[q], idbB], writes=[ptrB[bq]])

                def emit_ev(grp):
                    blocks = list(range(grp * 8, min(grp * 8 + 8, nblk)))
                    bq = grp % 2
                    nb_ = len(blocks)
                    B.op("act", lambda e, nb_=nb_, bq=bq: e.copy(
                        out=PT[:, 0:nb_ * 128], in_=ptr[bq][:, 0:nb_ * 128]),
                        reads=[ptrB[bq]], writes=[PTB_])

                def emit_av(grp):
                    blocks = list(range(grp * 8, min(grp * 8 + 8, nblk)))

                    def av(e, blocks=blocks, first=(grp == 0), nblk=nblk):
                        inst = None
                        for ii, tt in enumerate(blocks):
                            st = PT[:, ii * 128:(ii + 1) * 128]
                            fi = first and ii == 0
                            la = tt == nblk - 1
                            inst = e.matmul(
                                p[4][:, :], st,
                                a1[:, VB + tt * 1024: VB + tt * 1024 + 512],
                                start=fi, stop=la)
                            inst = e.matmul(
                                p[5][:, :], st,
                                a1[:, VB + tt * 1024 + 512: VB + (tt + 1) * 1024],
                                start=fi, stop=la)
                        return inst
                    w_, pt_ = _gp(grp == 0, pB[4], pB[5])
                    B.op("pe", av, reads=[PTB_, VBuf], writes=w_, partial=pt_)

                emit_tr(0)
                emit_ev(0)
                for grp in range(1, ngrp):
                    emit_tr(grp)
                    emit_av(grp - 1)
                    emit_ev(grp)
                emit_av(ngrp - 1)

                def res_add(e, q=q):
                    e.scalar_tensor_tensor(
                        out=x1tF[:, 0:512], in0=p[4][:, :],
                        scalar=linv[:, q:q + 1],
                        in1=xres[q][:, 0:512], op0=ALU.mult, op1=ALU.add,
                        accum_out=sx2[:, 0:1])
                    return e.scalar_tensor_tensor(
                        out=x1tF[:, 512:1024], in0=p[5][:, :],
                        scalar=linv[:, q:q + 1],
                        in1=xres[q][:, 512:1024], op0=ALU.mult, op1=ALU.add,
                        accum_out=sx2[:, 1:2])
                B.op("dve", res_add, reads=[pB[4], pB[5], linvB[q], xresB[q]],
                     writes=[x1fB, sx2B])
                if DBG:
                    gnt = b * TPB + T
                    dbg_toks.append(B.op("sp", lambda e, gnt=gnt: e.dma_start(
                        out=dbg_xatt[gnt * 128:(gnt + 1) * 128, :], in_=x1tF[:, :]),
                        inc=(ch_dbg, 16), reads=[x1fB]))
                    dbg_toks.append(B.op("sp", lambda e, gnt=gnt, q=q: e.dma_start(
                        out=dbg_li[gnt], in_=linv[:, q:q + 1]),
                        inc=(ch_dbg2, 16), reads=[linvB[q]]))

                def sqp(e):
                    e.activation(out=p[4][:, :], in_=x1tF[:, 0:512],
                                 func=AF.Square, accum_out=sq2[:, 0:1])
                    return e.activation(out=p[5][:, :], in_=x1tF[:, 512:1024],
                                        func=AF.Square, accum_out=sq2[:, 1:2])
                B.op("act", sqp, reads=[x1fB], writes=[sq2B, pB[4], pB[5]])
                B.op("dve", lambda e: e.reduce_sum(
                    mn1[:, :], sx2[:, :], axis=AX.X), reads=[sx2B], writes=[mn1B])
                B.op("dve", lambda e: e.tensor_scalar(
                    out=mn1[:, :], in0=mn1[:, :], scalar1=1.0 / D, scalar2=None,
                    op0=ALU.mult), reads=[mn1B], writes=[mn1B])
                B.op("dve", lambda e: e.tensor_tensor(
                    out=m2t[:, :], in0=mn1[:, :], in1=mn1[:, :], op=ALU.mult),
                    reads=[mn1B], writes=[m2tB])
                B.op("dve", lambda e: e.reduce_sum(
                    rstd[:, :], sq2[:, :], axis=AX.X), reads=[sq2B], writes=[rstdB])
                B.op("dve", lambda e: e.scalar_tensor_tensor(
                    out=rstd[:, :], in0=rstd[:, :], scalar=1.0 / D,
                    in1=m2t[:, :], op0=ALU.mult, op1=ALU.subtract),
                    reads=[m2tB], writes=[rstdB])
                B.op("dve", lambda e: e.tensor_scalar(
                    out=rstd[:, :], in0=rstd[:, :], scalar1=EPS, scalar2=None,
                    op0=ALU.add), reads=[rstdB], writes=[rstdB])
                B.op("act", lambda e: e.activation(
                    out=rstd[:, :], in_=rstd[:, :], func=AF.Ln),
                    reads=[rstdB], writes=[rstdB])
                B.op("act", lambda e: e.activation(
                    out=rstd[:, :], in_=rstd[:, :], func=AF.Exp, scale=-0.5),
                    reads=[rstdB], writes=[rstdB])
                B.op("dve", lambda e: e.tensor_scalar(
                    out=x1tF[:, :], in0=x1tF[:, :], scalar1=mn1[:, :],
                    scalar2=rstd[:, :], op0=ALU.subtract, op1=ALU.mult),
                    reads=[mn1B, rstdB], writes=[x1fB])
                n0 = (b * TPB + T) * 128
                B.op("sp", lambda e, n0=n0: e.dma_start(
                    out=x1_scr[n0:n0 + 128, :], in_=x1tF[:, :]),
                    inc=(ch_x1w, 16), reads=[x1fB], partial=[x1sB])

                def make_pend(n0=n0):
                    def emit2():
                        B.op("pool", lambda e: e.tensor_copy(x1tC[:, :], x1tF[:, :]),
                             reads=[x1fB], writes=[x1cB])

                        def xtr(e):
                            inst = None
                            for k in range(KD):
                                inst = e.transpose(
                                    p7[:, k * 128:(k + 1) * 128],
                                    x1tC[:, k * 128:(k + 1) * 128], idb[:, :])
                            return inst
                        B.op("pe", xtr, reads=[x1cB, idbB], writes=[p7B])
                        B.op("dve", lambda e: e.tensor_copy(x1TC[:, :], p7[:, :]),
                             reads=[p7B], writes=[x1TB])
                        B.op("sp", lambda e: e.dma_start(
                            out=x1t_scr[:, :, n0:n0 + 128].rearrange(
                                "k p j -> p k j"),
                            in_=x1TC[:, :].rearrange("p (k j) -> p k j", k=KD)),
                            inc=(ch_x1tw, 16), reads=[x1TB], partial=[x1tsB])
                    return emit2
                pend_x1[0] = make_pend()
            return emit

        pend_TA = [None]

        def flush_TA():
            if pend_TA[0] is not None:
                pend_TA[0]()
                pend_TA[0] = None

        load_xtqB(0)
        for T in range(TPB):
            g = T // 4
            if T % 4 == 0:
                if g + 1 < S // 512:
                    load_xtqB(g + 1)
                emit_Q(g)
            emit_S(T)
            emit_C(T)
            flush_TA()
            pend_TA[0] = make_TA(T)
        flush_TA()
        flush_x1()

    if DBG:
        B.op("sp", lambda e: e.nop(), deps=dbg_toks + B.all_toks())
        B.replay(nc, es)
        es.close()
        return nc

    NH = N // 2
    HREG = 8 * NH
    guard3 = B.all_toks()
    x1TrB = [_Buf(guard3), _Buf(guard3)]
    lt8B, et8B, ms8B, mk8B, m18B, ri8B = (_Buf(), _Buf(), _Buf(), _Buf(),
                                          _Buf(), _Buf())
    gwcB = [_Buf(guard3), _Buf(guard3)]
    uwcB = [_Buf(guard3), _Buf(guard3)]
    hbB_ = [_Buf(guard3), _Buf(guard3)]
    wfinB = _Buf(guard3)
    wfinbB = _Buf(guard3)
    t0B, t1B = _Buf(guard3), _Buf(guard3)
    wfulB = [_Buf(guard3), _Buf(guard3)]

    gwc = [bfv(0, 512), bfv(512, 1024)]
    uwc = [bfv(1024, 1536), bfv(1536, 2048)]
    hbuf = [bfv(2048, 4096), bfv(4096, 6144)]
    wfin = f32v(6144, 8192)
    t0 = f32v(8192, 9216)
    t1 = f32v(9216, 10240)
    wful = [f32v(10240, 11264), f32v(11264, 12288)]
    wfinb = bfv(12288, 13312)

    def x1tr_sl(h, k, lo, hi):
        base = h * HREG + k * NH
        return a1[:, base + lo: base + hi]

    for h in range(2):
        for k in range(KD):
            B.op("sp", lambda e, h=h, k=k: e.dma_start(
                out=x1tr_sl(h, k, 0, NH),
                in_=x1t_scr[k][:, h * NH:(h + 1) * NH]),
                inc=(ch_x1tr[h], 16), deps=guard3, reads=[x1tsB],
                partial=[x1TrB[h]])

    lt8_3 = lt8.reshape([128, 8, E])
    et8_3 = et8.reshape([128, 8, E])
    ms8_3 = ms8.reshape([128, 8, E])
    mk8_3 = mk8.reshape([128, 8, 1])
    m18_3 = m18.reshape([128, 8, 1])
    ri8_3 = ri8.reshape([128, 8, 1])
    NTH = NT // 2

    def emit_router(h):
        for nt in range(h * NTH, (h + 1) * NTH):
            ntl = nt - h * NTH

            lb = 4 + (nt % 2)

            def lmm(e, h=h, ntl=ntl, lb=lb):
                inst = None
                for k in range(KD):
                    st = x1tr_sl(h, k, ntl * 128, (ntl + 1) * 128)
                    inst = e.matmul(p[lb][:, 0:E], st, rwb[:, k * E:(k + 1) * E],
                                    start=(k == 0), stop=(k == KD - 1))
                return inst
            B.op("pe", lmm, reads=[x1TrB[h], rwbB], writes=[pB[lb]])
            w_, pt_ = _gp(nt == 0, wfinB)
            B.op("dve", lambda e, nt=nt, lb=lb: e.tensor_tensor(
                out=wfin[:, nt * E:(nt + 1) * E], in0=p[lb][:, 0:E], in1=rbb[:, :],
                op=ALU.add), reads=[pB[lb], rbbB], writes=w_, partial=pt_)
        for g8 in range(h * (NTH // 8), (h + 1) * (NTH // 8)):
            wf_sl = wfin[:, g8 * 8 * E:(g8 + 1) * 8 * E]
            wf3 = wf_sl.rearrange("p (s e) -> p s e", s=8)
            B.op("dve", lambda e, wf_sl=wf_sl: e.tensor_copy(lt8[:, :], wf_sl),
                 reads=[wfinB], writes=[lt8B])
            for it in range(4):
                B.op("dve", lambda e: e.reduce_max(
                    mk8_3[:, :, :], lt8_3[:, :, :], axis=AX.X),
                    reads=[lt8B], writes=[mk8B])
                if it == 0:
                    B.op("dve", lambda e: e.tensor_copy(m18[:, :], mk8[:, :]),
                         reads=[mk8B], writes=[m18B])
                if it < 3:
                    B.op("dve", lambda e: e.tensor_tensor(
                        out=ms8_3[:, :, :], in0=lt8_3[:, :, :],
                        in1=mk8_3.ap().to_broadcast([128, 8, E]), op=ALU.is_ge),
                        reads=[lt8B, mk8B], writes=[ms8B])
                    B.op("dve", lambda e: e.scalar_tensor_tensor(
                        out=lt8[:, :], in0=ms8[:, :], scalar=NEG, in1=lt8[:, :],
                        op0=ALU.mult, op1=ALU.add), reads=[ms8B], writes=[lt8B])
            B.op("dve", lambda e, wf3=wf3: e.tensor_tensor(
                out=lt8_3[:, :, :], in0=wf3,
                in1=m18_3.ap().to_broadcast([128, 8, E]), op=ALU.subtract),
                reads=[wfinB, m18B], writes=[lt8B])
            B.op("act", lambda e: e.activation(
                out=et8[:, :], in_=lt8[:, :], func=AF.Exp),
                reads=[lt8B], writes=[et8B])
            B.op("dve", lambda e, wf3=wf3: e.tensor_tensor(
                out=ms8_3[:, :, :], in0=wf3,
                in1=mk8_3.ap().to_broadcast([128, 8, E]), op=ALU.is_ge),
                reads=[wfinB, mk8B], writes=[ms8B])
            B.op("dve", lambda e: e.tensor_tensor(
                out=et8[:, :], in0=et8[:, :], in1=ms8[:, :], op=ALU.mult),
                reads=[ms8B], writes=[et8B])
            B.op("dve", lambda e: e.reduce_sum(
                ri8_3[:, :, :], et8_3[:, :, :], axis=AX.X),
                reads=[et8B], writes=[ri8B])
            B.op("dve", lambda e: e.reciprocal(ri8[:, :], ri8[:, :]),
                 reads=[ri8B], writes=[ri8B])
            B.op("dve", lambda e, wf3=wf3: e.tensor_tensor(
                out=wf3, in0=et8_3[:, :, :],
                in1=ri8_3.ap().to_broadcast([128, 8, E]), op=ALU.mult),
                reads=[et8B, ri8B], partial=[wfinB])
        for g8 in range(h * (NTH // 8), (h + 1) * (NTH // 8)):
            B.op("act", lambda e, g8=g8: e.copy(
                out=wfinb[:, g8 * 8 * E:(g8 + 1) * 8 * E],
                in_=wfin[:, g8 * 8 * E:(g8 + 1) * 8 * E]),
                reads=[wfinB], partial=[wfinbB])

            def wtr(e, g8=g8):
                inst = None
                for i in range(8):
                    inst = e.transpose(p6[0:E, i * 128:(i + 1) * 128],
                                       wfinb[:, (g8 * 8 + i) * E:(g8 * 8 + i + 1) * E],
                                       idb[:, :])
                return inst
            B.op("pe", wtr, reads=[wfinbB, idbB], writes=[p6B])
            B.op("dve", lambda e: e.tensor_copy(t0[0:E, 0:1024], p6[0:E, :]),
                 reads=[p6B], writes=[t0B])
            B.op("sp", lambda e, g8=g8: e.dma_start(
                out=w_scr[:, g8 * 1024:(g8 + 1) * 1024], in_=t0[0:E, 0:1024]),
                inc=(ch_ws, 16), reads=[t0B], partial=[wscrB])

    def emit_main_half(h, hook=None):
        NCHL = NH // 1024
        for ft in range(E + 4):
            if hook is not None and ft % 4 == 2:
                hook()
            shared = ft >= E
            slot = ft % 2
            if shared:
                cs = ft - E
                B.op("pool", lambda e, s=slot, cs=cs: e.dma_start(
                    out=gwc[s].rearrange("p (k f) -> p k f", k=KD),
                    in_=sgw_in[:, :, cs * 128:(cs + 1) * 128]),
                    inc=(ch_gw[slot], 16), writes=[gwcB[slot]])
                B.op("pool", lambda e, s=slot, cs=cs: e.dma_start(
                    out=uwc[s].rearrange("p (k f) -> p k f", k=KD),
                    in_=suw_in[:, :, cs * 128:(cs + 1) * 128]),
                    inc=(ch_uw[slot], 16), writes=[uwcB[slot]])
            else:
                B.op("pool", lambda e, s=slot, ft=ft: e.dma_start(
                    out=gwc[s].rearrange("p (k f) -> p k f", k=KD),
                    in_=gw_in[:, :, ft * 128:(ft + 1) * 128]),
                    inc=(ch_gw[slot], 16), writes=[gwcB[slot]])
                B.op("pool", lambda e, s=slot, ft=ft: e.dma_start(
                    out=uwc[s].rearrange("p (k f) -> p k f", k=KD),
                    in_=uw_in[:, :, ft * 128:(ft + 1) * 128]),
                    inc=(ch_uw[slot], 16), writes=[uwcB[slot]])
            for nc_ in range(NCHL):
                n0l = nc_ * 1024
                n0g = h * NH + n0l
                wslot = nc_ % 2
                if not shared:
                    B.op("pool", lambda e, ws_=wslot, ft=ft, n0g=n0g: e.dma_start(
                        out=wful[ws_][:, :],
                        in_=w_scr[ft:ft + 1, n0g:n0g + 1024].partition_broadcast(128)),
                        inc=(ch_wf[wslot], 16), reads=[wscrB], writes=[wfulB[wslot]])

                def gmm(e, s=slot, h=h, n0l=n0l):
                    inst = None
                    for k in range(KD):
                        st = gwc[s][:, k * 128:(k + 1) * 128]
                        inst = e.matmul(p[0][:, :], st, x1tr_sl(h, k, n0l, n0l + 512),
                                        start=(k == 0), stop=(k == KD - 1))
                        inst = e.matmul(p[1][:, :], st,
                                        x1tr_sl(h, k, n0l + 512, n0l + 1024),
                                        start=(k == 0), stop=(k == KD - 1))
                    return inst
                B.op("pe", gmm, reads=[gwcB[slot], x1TrB[h]], writes=[pB[0], pB[1]])

                def umm(e, s=slot, h=h, n0l=n0l):
                    inst = None
                    for k in range(KD):
                        st = uwc[s][:, k * 128:(k + 1) * 128]
                        inst = e.matmul(p[2][:, :], st, x1tr_sl(h, k, n0l, n0l + 512),
                                        start=(k == 0), stop=(k == KD - 1))
                        inst = e.matmul(p[3][:, :], st,
                                        x1tr_sl(h, k, n0l + 512, n0l + 1024),
                                        start=(k == 0), stop=(k == KD - 1))
                    return inst
                B.op("pe", umm, reads=[uwcB[slot], x1TrB[h]], writes=[pB[2], pB[3]])

                def silu_(e):
                    e.activation(out=t0[:, 0:512], in_=p[0][:, :], func=AF.Silu)
                    return e.activation(out=t0[:, 512:1024], in_=p[1][:, :],
                                        func=AF.Silu)
                B.op("act", silu_, reads=[pB[0], pB[1]], writes=[t0B])

                half_b = nc_ // (NCHL // 2) if NCHL >= 2 else 0
                pos = nc_ % max(NCHL // 2, 1)
                hb = hbuf[half_b]
                if shared:
                    def tt1s(e, hb=hb, pos=pos):
                        e.tensor_tensor(out=hb[:, pos * 1024: pos * 1024 + 512],
                                        in0=t0[:, 0:512], in1=p[2][:, :], op=ALU.mult)
                        return e.tensor_tensor(
                            out=hb[:, pos * 1024 + 512: (pos + 1) * 1024],
                            in0=t0[:, 512:1024], in1=p[3][:, :], op=ALU.mult)
                    B.op("dve", tt1s, reads=[t0B, pB[2], pB[3]], partial=[hbB_[half_b]])
                else:
                    def tt1(e):
                        e.tensor_tensor(out=t1[:, 0:512], in0=t0[:, 0:512],
                                        in1=p[2][:, :], op=ALU.mult)
                        return e.tensor_tensor(out=t1[:, 512:1024], in0=t0[:, 512:1024],
                                               in1=p[3][:, :], op=ALU.mult)
                    B.op("dve", tt1, reads=[t0B, pB[2], pB[3]], writes=[t1B])
                    B.op("dve", lambda e, hb=hb, pos=pos, ws_=wslot: e.tensor_tensor(
                        out=hb[:, pos * 1024:(pos + 1) * 1024], in0=t1[:, :],
                        in1=wful[ws_][:, :], op=ALU.mult),
                        reads=[t1B, wfulB[wslot]], partial=[hbB_[half_b]])
                if pos == max(NCHL // 2, 1) - 1:
                    width = (pos + 1) * 1024
                    base = h * NH + half_b * (NH // 2)
                    dst = (sa_scr[cs] if shared else hp_scr[ft])
                    scrb = sasB if shared else hpsB
                    B.op("sp", lambda e, hb=hb, width=width, base=base, dst=dst:
                         e.dma_start(out=dst[:, base:base + width],
                                     in_=hb[:, 0:width]),
                         inc=(ch_hb[half_b], 16), reads=[hbB_[half_b]], partial=[scrb])

    emit_router(0)
    emit_router(1)
    emit_main_half(0)

    def emit_dw_load(part=None, nparts=8):
        regs = [x1TrB[0]] + ([x1TrB[1]] if E * D > HREG else [])
        if part is None:
            B.op("pool", lambda e: e.dma_start(
                out=a1[:, 0:E * D].rearrange("p (c d) -> p c d", c=E), in_=dw_in),
                inc=(ch_dw, 16), writes=regs)
            return
        ecs = E // nparts
        B.op("pool", lambda e, part=part, ecs=ecs: e.dma_start(
            out=a1[:, part * ecs * D:(part + 1) * ecs * D].rearrange(
                "p (c d) -> p c d", c=ecs),
            in_=dw_in[:, part * ecs:(part + 1) * ecs, :]),
            inc=(ch_dw, 16),
            deps=[t for b_ in regs for t in (b_.w + b_.r)],
            partial=[regs[0]])

    if E * D <= HREG:
        _dwp = [0]

        def emit_dw_part():
            if _dwp[0] < 8:
                emit_dw_load(part=_dwp[0])
                _dwp[0] += 1
        emit_main_half(1, hook=emit_dw_part)
        while _dwp[0] < 8:
            emit_dw_load(part=_dwp[0])
            _dwp[0] += 1
    else:
        emit_main_half(1)
        emit_dw_load()
    dwbB = x1TrB[0]

    guard4 = B.all_toks()
    HPB2 = max(HREG, E * D)
    sdwbB = _Buf(guard4)
    hpbB = [_Buf(x1TrB[1].w + x1TrB[1].r), _Buf(x1TrB[1].w + x1TrB[1].r)]
    sabB = [_Buf(guard4), _Buf(guard4)]
    xrCB = [_Buf(guard4), _Buf(guard4)]
    outBB = [_Buf(guard4), _Buf(guard4)]
    t2B = _Buf(guard4)
    bnbB2, bnaB2, rstdB2 = _Buf(), _Buf(), _Buf()
    sx2B2, sq2B2, mn1B2, m2tB2 = _Buf(guard4), _Buf(guard4), _Buf(guard4), _Buf(guard4)

    sdwb = bfv(0, 2048)
    sab = [bfv(2048, 3072), bfv(3072, 4096)]
    xrC = [f32v(4096, 5120), f32v(5120, 6144)]
    outb = [f32v(6144, 7168), f32v(7168, 8192)]
    t2 = f32v(8192, 9216)

    B.op("pool", lambda e: e.dma_start(
        out=sdwb.rearrange("p (c d) -> p c d", c=4), in_=sdw_in),
        inc=(ch_sdw, 16), deps=guard4, writes=[sdwbB])

    def load_c2(nc_):
        n0 = nc_ * 512
        slot = nc_ % 2
        hpbase = HPB2 + slot * (E * 512)
        B.op("pool", lambda e, hpbase=hpbase, n0=n0: e.dma_start(
            out=a1[:, hpbase: hpbase + E * 512].rearrange("p (c n) -> p c n", c=E),
            in_=hp_scr[:, :, n0:n0 + 512].rearrange("c p n -> p c n")),
            inc=(ch_hp[slot], 16), reads=[hpsB], writes=[hpbB[slot]])
        B.op("pool", lambda e, s=slot, n0=n0: e.dma_start(
            out=sab[s].rearrange("p (c n) -> p c n", c=4),
            in_=sa_scr[:, :, n0:n0 + 512].rearrange("c p n -> p c n")),
            inc=(ch_sab[slot], 16), reads=[sasB], writes=[sabB[slot]])

    load_c2(0)
    for nc_ in range(N // 512):
        slot = nc_ % 2
        hpbase = HPB2 + slot * (E * 512)
        if nc_ + 1 < N // 512:
            load_c2(nc_ + 1)
        for j in range(4):
            nt = nc_ * 4 + j
            b0, b1 = (0, 1) if nt % 2 == 0 else (2, 3)
            B.op("sp", lambda e, s=nt % 2, nt=nt: e.dma_start(
                out=xrC[s][:, :], in_=x1_scr[nt * 128:(nt + 1) * 128, :]),
                inc=(ch_xrc[nt % 2], 16), reads=[x1sB], writes=[xrCB[nt % 2]])

            def dmm(e, hpbase=hpbase, j=j, b0=b0, b1=b1, s=slot):
                inst = None
                for c in range(E):
                    st = a1[:, hpbase + c * 512 + j * 128: hpbase + c * 512 + (j + 1) * 128]
                    inst = e.matmul(p[b0][:, :], st, a1[:, c * D: c * D + 512],
                                    start=(c == 0), stop=False)
                    inst = e.matmul(p[b1][:, :], st, a1[:, c * D + 512: (c + 1) * D],
                                    start=(c == 0), stop=False)
                for cs in range(4):
                    st = sab[s][:, cs * 512 + j * 128: cs * 512 + (j + 1) * 128]
                    inst = e.matmul(p[b0][:, :], st,
                                    sdwb[:, cs * D: cs * D + 512],
                                    start=False, stop=(cs == 3))
                    inst = e.matmul(p[b1][:, :], st,
                                    sdwb[:, cs * D + 512: (cs + 1) * D],
                                    start=False, stop=(cs == 3))
                return inst
            B.op("pe", dmm, reads=[hpbB[slot], dwbB, sabB[slot], sdwbB],
                 writes=[pB[b0], pB[b1]])

            def res2(e, b0=b0, b1=b1, s=nt % 2):
                e.scalar_tensor_tensor(
                    out=t2[:, 0:512], in0=p[b0][:, :], scalar=1.0,
                    in1=xrC[s][:, 0:512], op0=ALU.mult, op1=ALU.add,
                    accum_out=sx2[:, 0:1])
                return e.scalar_tensor_tensor(
                    out=t2[:, 512:1024], in0=p[b1][:, :], scalar=1.0,
                    in1=xrC[s][:, 512:1024], op0=ALU.mult, op1=ALU.add,
                    accum_out=sx2[:, 1:2])
            B.op("dve", res2, reads=[pB[b0], pB[b1], xrCB[nt % 2]],
                 writes=[t2B, sx2B2])

            def sqp2(e, b0=b0, b1=b1):
                e.activation(out=p[b0][:, :], in_=t2[:, 0:512],
                             func=AF.Square, accum_out=sq2[:, 0:1])
                return e.activation(out=p[b1][:, :], in_=t2[:, 512:1024],
                                    func=AF.Square, accum_out=sq2[:, 1:2])
            B.op("act", sqp2, reads=[t2B], writes=[sq2B2, pB[b0], pB[b1]])
            B.op("dve", lambda e: e.reduce_sum(
                mn1[:, :], sx2[:, :], axis=AX.X), reads=[sx2B2], writes=[mn1B2])
            B.op("dve", lambda e: e.tensor_scalar(
                out=mn1[:, :], in0=mn1[:, :], scalar1=1.0 / D, scalar2=None,
                op0=ALU.mult), reads=[mn1B2], writes=[mn1B2])
            B.op("dve", lambda e: e.tensor_tensor(
                out=m2t[:, :], in0=mn1[:, :], in1=mn1[:, :], op=ALU.mult),
                reads=[mn1B2], writes=[m2tB2])
            B.op("dve", lambda e: e.reduce_sum(
                rstd[:, :], sq2[:, :], axis=AX.X), reads=[sq2B2], writes=[rstdB2])
            B.op("dve", lambda e: e.scalar_tensor_tensor(
                out=rstd[:, :], in0=rstd[:, :], scalar=1.0 / D,
                in1=m2t[:, :], op0=ALU.mult, op1=ALU.subtract),
                reads=[m2tB2], writes=[rstdB2])
            B.op("dve", lambda e: e.tensor_scalar(
                out=rstd[:, :], in0=rstd[:, :], scalar1=EPS, scalar2=None,
                op0=ALU.add), reads=[rstdB2], writes=[rstdB2])
            B.op("act", lambda e: e.activation(
                out=rstd[:, :], in_=rstd[:, :], func=AF.Ln),
                reads=[rstdB2], writes=[rstdB2])
            B.op("act", lambda e: e.activation(
                out=rstd[:, :], in_=rstd[:, :], func=AF.Exp, scale=-0.5),
                reads=[rstdB2], writes=[rstdB2])
            B.op("dve", lambda e, s=nt % 2: e.tensor_scalar(
                out=outb[s][:, :], in0=t2[:, :], scalar1=mn1[:, :],
                scalar2=rstd[:, :], op0=ALU.subtract, op1=ALU.mult),
                reads=[t2B, mn1B2, rstdB2], writes=[outBB[nt % 2]])
            out_toks.append(B.op("sp", lambda e, s=nt % 2, nt=nt: e.dma_start(
                out=y_out[nt * 128:(nt + 1) * 128, :], in_=outb[s][:, :]),
                inc=(ch_out[nt % 2], 16), reads=[outBB[nt % 2]]))

    B.op("sp", lambda e: e.nop(), deps=out_toks)
    B.replay(nc, es)
    es.close()
    return nc


def _prep_weights(ins):
    f = lambda a: np.ascontiguousarray(np.asarray(a, dtype=np.float32))
    WQ = f(ins["L0_attn_WQ"])
    WK = f(ins["L0_attn_WK"])
    WV = f(ins["L0_attn_WV"])
    per_k = lambda w: np.ascontiguousarray(
        w.reshape(KD, 128, w.shape[1]).transpose(1, 0, 2))
    gate = f(ins["L0_ffn_gate_w"])
    up = f(ins["L0_ffn_up_w"])
    down = f(ins["L0_ffn_down_w"])
    sg = f(ins["L0_ffn_sgate_w"])
    su = f(ins["L0_ffn_sup_w"])
    sd = f(ins["L0_ffn_sdown_w"])
    rwm = f(ins["L0_ffn_router_w"])
    gT = gate.transpose(2, 0, 1).reshape(D, E * 128)
    uT = up.transpose(2, 0, 1).reshape(D, E * 128)
    dT = down.transpose(0, 2, 1).reshape(E * 128, D)
    sgT = sg.T
    suT = su.T
    sdT = sd.T
    rT = rwm.T
    out = {
        "wq": per_k(WQ), "wk": per_k(WK), "wv": per_k(WV),
        "gw": per_k(gT), "uw": per_k(uT),
        "dw": np.ascontiguousarray(dT.reshape(E, 128, D).transpose(1, 0, 2)),
        "sgw": per_k(sgT), "suw": per_k(suT),
        "sdw": np.ascontiguousarray(sdT.reshape(4, 128, D).transpose(1, 0, 2)),
        "rw": per_k(rT),
        "rb": f(ins["L0_ffn_router_bias"]).reshape(1, E),
        "idn": np.eye(128, dtype=np.float32),
        "cm": np.triu(np.full((128, 128), NEG, dtype=np.float32), k=1),
    }
    return out


def kernel(**inputs):
    S = 4096
    x = np.ascontiguousarray(np.asarray(inputs["L0_attn_x"], dtype=np.float32))
    Bfull = x.shape[0]
    ncores = 8
    per = Bfull // ncores
    w = _prep_weights(inputs)
    nc = build_program(S)
    in_maps = []
    for c in range(ncores):
        xs = x[c * per:(c + 1) * per]
        xTs = np.ascontiguousarray(
            xs.reshape(per, S, KD, 128).transpose(0, 2, 3, 1))
        m = dict(w)
        m["x"] = xs
        m["xT"] = xTs
        in_maps.append(m)
    res = run_bass_kernel_spmd(nc, in_maps, core_ids=list(range(ncores)))
    y = np.concatenate(
        [np.asarray(r["y"], dtype=np.float32).reshape(per, S, D)
         for r in res.results], axis=0)
    return y


# version 17
# speedup vs baseline: 1.0149x; 1.0149x over previous
import numpy as np
import concourse.bass as bass
import concourse.mybir as mybir
from contextlib import ExitStack
from concourse.bass_utils import run_bass_kernel_spmd

F32 = mybir.dt.float32
BF16 = mybir.dt.bfloat16
AF = mybir.ActivationFunctionType
ALU = mybir.AluOpType
AX = mybir.AxisListType

D = 1024
KD = 8
E = 32
FS = 512
NB = 2
EPS = 1e-5
NEG = -1.0e30

ENGINES = ("pe", "dve", "act", "pool", "sp")


class _Chan:
    __slots__ = ("name", "h", "n")

    def __init__(self, name):
        self.name = name
        self.h = None
        self.n = 0


class _Op:
    __slots__ = ("eng", "deps", "fn", "chan", "amt")

    def __init__(self, eng, deps, fn, chan, amt):
        self.eng = eng
        self.deps = deps
        self.fn = fn
        self.chan = chan
        self.amt = amt


class _Buf:
    __slots__ = ("w", "r")

    def __init__(self, guard=()):
        self.w = list(guard)
        self.r = []


def _dedup(toks):
    best = {}
    for (c, v) in toks:
        if v > best.get(c, 0):
            best[c] = v
    return tuple(best.items())


class _Builder:
    def __init__(self):
        self.ops = []
        self.chans = []
        self.ech = {e: self.chan("c_" + e) for e in ENGINES}

    def chan(self, name):
        c = _Chan(name)
        self.chans.append(c)
        return c

    def op(self, eng, fn, deps=(), inc=None, reads=(), writes=(), partial=()):
        d = list(deps)
        for b in reads:
            d += b.w
        for b in writes:
            d += b.w + b.r
        for b in partial:
            d += b.r
        if inc is None:
            cc, amt = self.ech[eng], 1
        else:
            cc, amt = inc
        cc.n += amt
        tok = (cc, cc.n)
        self.ops.append(_Op(eng, _dedup(d), fn, cc, amt))
        for b in reads:
            b.r.append(tok)
        for b in writes:
            b.w = [tok]
            b.r = []
        for b in partial:
            b.w.append(tok)
        return tok

    def all_toks(self):
        return [(c, c.n) for c in self.chans if c.n > 0]

    def replay(self, nc, exit_stack):
        for c in self.chans:
            c.h = exit_stack.enter_context(nc.semaphore(c.name))
        hwm = {}
        per_eng = {e: [] for e in ENGINES}
        for o in self.ops:
            per_eng[o.eng].append(o)
        block = exit_stack.enter_context(nc.Block())
        engmap = {"pe": block.tensor, "dve": block.vector, "act": block.scalar,
                  "pool": block.gpsimd, "sp": block.sync}

        def make_body(ename, ops):
            def body(e):
                for o in ops:
                    for (cc, v) in o.deps:
                        if v > 0 and hwm.get((ename, cc), 0) < v:
                            e.wait_ge(cc.h, v)
                            hwm[(ename, cc)] = v
                    o.fn(e).then_inc(o.chan.h, o.amt)
            return body

        for ename in ENGINES:
            if per_eng[ename]:
                engmap[ename](make_body(ename, per_eng[ename]))


def _gp(first, *bufs):
    return (bufs, ()) if first else ((), bufs)


DBG = 0


def build_program(S):
    F = E * 128
    N = NB * S
    NT = N // 128
    TPB = S // 128

    nc = bass.Bass("TRN2", target_bir_lowering=False, debug=False)
    es = ExitStack()

    dr = lambda name, shape, dt=F32, kind="ExternalInput": \
        nc.dram_tensor(name, list(shape), dt, kind=kind).ap()
    x_in = dr("x", (NB, S, D))
    xT_in = dr("xT", (NB, KD, 128, S))
    wq_in = dr("wq", (128, KD, D))
    wk_in = dr("wk", (128, KD, D))
    wv_in = dr("wv", (128, KD, D))
    gw_in = dr("gw", (128, KD, F))
    uw_in = dr("uw", (128, KD, F))
    dw_in = dr("dw", (128, E, D))
    sgw_in = dr("sgw", (128, KD, FS))
    suw_in = dr("suw", (128, KD, FS))
    sdw_in = dr("sdw", (128, 4, D))
    rw_in = dr("rw", (128, KD, E))
    rb_in = dr("rb", (1, E))
    idn_in = dr("idn", (128, 128))
    cm_in = dr("cm", (128, 128))
    y_out = dr("y", (N, D), kind="ExternalOutput")
    if DBG:
        dbg_xatt = dr("dbg_xatt", (N, D), kind="ExternalOutput")
        dbg_li = dr("dbg_li", (NT, 128, 1), kind="ExternalOutput")

    x1_scr = dr("x1s", (N, D), kind="Internal")
    x1t_scr = dr("x1ts", (KD, 128, N), BF16, kind="Internal")
    hp_scr = dr("hps", (E, 128, N), BF16, kind="Internal")
    sa_scr = dr("sas", (4, 128, N), BF16, kind="Internal")
    w_scr = dr("ws", (E, N), kind="Internal")

    sb = lambda name, shape, dt=F32: es.enter_context(
        nc.sbuf_tensor(name, list(shape), dt))

    A1_COLS = max(8 * S + 8 * S, E * D + 2 * E * 512)
    arena1 = sb("arena1", (128, A1_COLS // 2), F32)
    a1 = arena1.bitcast(BF16)
    VB = 8 * S
    HPB = E * D

    A2_COLS = 18944
    arena2 = sb("arena2", (128, A2_COLS), F32)
    a2f = arena2
    a2 = arena2.bitcast(BF16)

    def bfv(c0f, c1f):
        return a2[:, 2 * c0f: 2 * c1f]

    def f32v(c0f, c1f):
        return a2f[:, c0f:c1f]

    idb = sb("idb", (128, 128), BF16)
    cmf = sb("cmf", (128, 128), F32)
    rbb = sb("rbb", (128, E), F32)
    rwb = sb("rwb", (128, KD * E), BF16)
    mxs = sb("mxs", (128, 16), F32)
    nmx = sb("nmx", (128, 16), F32)
    sums = sb("sums", (128, 16), F32)
    al = sb("al", (128, 16), F32)
    al2 = sb("al2", (128, 16), F32)
    gm = sb("gm", (128, 2), F32)
    gmn = sb("gmn", (128, 2), F32)
    lv = sb("lv", (128, 2), F32)
    linv = sb("linv", (128, 2), F32)
    bnb = sb("bnb", (128, 12), F32)
    bna = sb("bna", (128, 2), F32)
    rstd = sb("rstd", (128, 1), F32)
    lt8 = sb("lt8", (128, 8 * E), F32)
    et8 = sb("et8", (128, 8 * E), F32)
    ms8 = sb("ms8", (128, 8 * E), F32)
    mk8 = sb("mk8", (128, 8), F32)
    m18 = sb("m18", (128, 8), F32)
    ri8 = sb("ri8", (128, 8), F32)

    ppair = [es.enter_context(nc.psum_tensor(f"pp{i}", [128, 1024], F32))
             for i in range(3)]
    p = [ppair[i // 2][:, (i % 2) * 512:(i % 2) * 512 + 512] for i in range(6)]
    p6 = es.enter_context(nc.psum_tensor("p6", [128, 1024], BF16))
    p7 = es.enter_context(nc.psum_tensor("p7", [128, 1024], BF16))
    ptr = [p6, p7]

    B = _Builder()
    ch = B.chan

    idbB, cmfB, rbbB, rwbB = _Buf(), _Buf(), _Buf(), _Buf()
    B.op("pool", lambda e: e.dma_start(out=idb[:, :], in_=idn_in),
         inc=(ch("d_idb"), 16), writes=[idbB])
    B.op("pool", lambda e: e.dma_start(
        out=rwb[:, :].rearrange("p (k e) -> p k e", k=KD), in_=rw_in),
        inc=(ch("d_rwb"), 16), writes=[rwbB])
    B.op("sp", lambda e: e.dma_start(out=cmf[:, :], in_=cm_in),
         inc=(ch("d_cmf"), 16), writes=[cmfB])
    B.op("sp", lambda e: e.dma_start(out=rbb[:, :],
                                     in_=rb_in.partition_broadcast(128)),
         inc=(ch("d_rbb"), 16), writes=[rbbB])

    ch_wk, ch_wv, ch_wq = ch("d_wk"), ch("d_wv"), ch("d_wq")
    ch_xtq = [ch("d_xtq0"), ch("d_xtq1"), ch("d_xtq2")]
    ch_xtqB2 = [ch("d_xtqB0"), ch("d_xtqB1")]
    ch_xr = [ch("d_xr0"), ch("d_xr1")]
    ch_x1w, ch_x1tw, ch_ws = ch("d_x1w"), ch("d_x1tw"), ch("d_ws")
    ch_x1tr = [ch("d_x1tr0"), ch("d_x1tr1")]
    ch_gw = [ch("d_gw0"), ch("d_gw1")]
    ch_uw = [ch("d_uw0"), ch("d_uw1")]
    ch_wf = [ch("d_wf0"), ch("d_wf1")]
    ch_hb = [ch("d_hb0"), ch("d_hb1")]
    ch_dw, ch_sdw = ch("d_dw"), ch("d_sdw")
    ch_hp = [ch("d_hp0"), ch("d_hp1")]
    ch_sab = [ch("d_sab0"), ch("d_sab1")]
    ch_xrc = [ch("d_xrc0"), ch("d_xrc1")]
    ch_out = [ch("d_out0"), ch("d_out1")]

    dbg_toks = []
    ch_dbg, ch_dbg2 = ch("d_dbg"), ch("d_dbg2")
    pB = [_Buf() for _ in range(6)]
    p6B = _Buf()
    p7B = _Buf()
    ptrB = [p6B, p7B]
    x1sB, x1tsB, wscrB = _Buf(), _Buf(), _Buf()
    hpsB = _Buf()
    sasB = _Buf()
    out_toks = []

    prev_wqbB = None
    for b in range(NB):
        guard = B.all_toks()
        if prev_wqbB is not None:
            wkbB, wvbB = _Buf(prev_wqbB.w + prev_wqbB.r), _Buf(guard)
        else:
            wkbB, wvbB = _Buf(guard), _Buf(guard)
        xtqB_ = [_Buf(guard), _Buf(guard), _Buf(guard)]
        KTB = [_Buf(guard) for _ in range(KD)]
        VBuf = _Buf(guard)

        wkb = bfv(0, 4096)
        wvb = bfv(4096, 8192)
        xtqA = [bfv(8192, 10240), bfv(10240, 12288), bfv(12288, 14336)]

        B.op("pool", lambda e: e.dma_start(
            out=wkb.rearrange("p (k d) -> p k d", k=KD), in_=wk_in),
            inc=(ch_wk, 16), writes=[wkbB])
        B.op("pool", lambda e: e.dma_start(
            out=wvb.rearrange("p (k d) -> p k d", k=KD), in_=wv_in),
            inc=(ch_wv, 16), writes=[wvbB])

        def xtq_sl(slot, k, lo, hi):
            return xtqA[slot][:, k * 512 + lo: k * 512 + hi]

        def load_xtq(cidx):
            slot = cidx % 3
            B.op("pool", lambda e, s=slot, c=cidx, b=b: e.dma_start(
                out=xtqA[s].rearrange("p (k t) -> p k t", k=KD),
                in_=xT_in[b][:, :, c * 512:(c + 1) * 512].rearrange("k p t -> p k t")),
                inc=(ch_xtq[slot], 16), writes=[xtqB_[slot]])

        nchunks = S // 512
        load_xtq(0)
        load_xtq(1)
        for i in range(nchunks // 2):
            s0, s1 = (2 * i) % 3, (2 * i + 1) % 3
            if 2 * i + 2 < nchunks:
                load_xtq(2 * i + 2)
            for m in range(KD):
                bA, bBk = (0, 1) if m % 2 == 0 else (2, 3)

                def kmm(e, m=m, bA=bA, bBk=bBk, s0=s0, s1=s1):
                    inst = None
                    for k in range(KD):
                        st = wkb[:, k * D + m * 128: k * D + (m + 1) * 128]
                        inst = e.matmul(p[bA][:, :], st, xtq_sl(s0, k, 0, 512),
                                        start=(k == 0), stop=(k == KD - 1))
                        inst = e.matmul(p[bBk][:, :], st, xtq_sl(s1, k, 0, 512),
                                        start=(k == 0), stop=(k == KD - 1))
                    return inst
                B.op("pe", kmm, reads=[wkbB, xtqB_[s0], xtqB_[s1]],
                     writes=[pB[bA], pB[bBk]])
                for (bk, half) in ((bA, 0), (bBk, 1)):
                    c0 = (2 * i + half) * 512
                    B.op("dve", lambda e, bk=bk, m=m, c0=c0: e.tensor_copy(
                        a1[:, m * S + c0: m * S + c0 + 512], p[bk][:, :]),
                        reads=[pB[bk]], partial=[KTB[m]])
            for ci in (2 * i, 2 * i + 1):
                slot = ci % 3
                if ci == 2 * i + 1 and 2 * i + 3 < nchunks:
                    load_xtq(2 * i + 3)
                for j in range(4):
                    tt = ci * 4 + j
                    b0, b1 = (4, 5) if tt % 2 == 0 else (0, 1)

                    def vmm(e, slot=slot, j=j, b0=b0, b1=b1):
                        inst = None
                        for k in range(KD):
                            st = xtq_sl(slot, k, j * 128, (j + 1) * 128)
                            inst = e.matmul(p[b0][:, :], st,
                                            wvb[:, k * D: k * D + 512],
                                            start=(k == 0), stop=(k == KD - 1))
                            inst = e.matmul(p[b1][:, :], st,
                                            wvb[:, k * D + 512: (k + 1) * D],
                                            start=(k == 0), stop=(k == KD - 1))
                        return inst
                    B.op("pe", vmm, reads=[wvbB, xtqB_[slot]],
                         writes=[pB[b0], pB[b1]])
                    for (bk, off) in ((b0, 0), (b1, 512)):
                        B.op("dve", lambda e, bk=bk, tt=tt, off=off: e.tensor_copy(
                            a1[:, VB + tt * 1024 + off: VB + tt * 1024 + off + 512],
                            p[bk][:, :]), reads=[pB[bk]], partial=[VBuf])

        guard2 = B.all_toks()
        wqbB = _Buf(guard2)
        xtqBB2 = [_Buf(guard2), _Buf(guard2)]
        QTB_ = _Buf(guard2)
        PB_ = _Buf(guard2)
        PTB_ = _Buf(guard2)
        xresB = [_Buf(guard2), _Buf(guard2)]
        x1fB, x1cB, x1TB, xtqBB = (_Buf(guard2), _Buf(guard2), _Buf(guard2),
                                   _Buf(guard2))
        mxsB = [_Buf(), _Buf()]
        nmxB = [_Buf(), _Buf()]
        sumsB = [_Buf(), _Buf()]
        alB = [_Buf(), _Buf()]
        al2B = [_Buf(), _Buf()]
        gmB = [_Buf(), _Buf()]
        gmnB = [_Buf(), _Buf()]
        lvB = [_Buf(), _Buf()]
        linvB = [_Buf(), _Buf()]
        PBs = [PB_, _Buf(guard2)]
        bnbB, bnaB, rstdB = _Buf(), _Buf(), _Buf()

        wqb = bfv(0, 4096)
        QT = bfv(4096, 6144)
        Pbs = [bfv(6144, 8192), bfv(16896, 18944)]
        PT = bfv(8192, 8704)
        xres = [f32v(8704, 9728), f32v(9728, 10752)]
        x1tF = f32v(10752, 11776)
        x1tC = bfv(11776, 12288)
        x1TC = bfv(12288, 12800)
        xtqBs = [bfv(12800, 14848), bfv(14848, 16896)]

        B.op("pool", lambda e: e.dma_start(
            out=wqb.rearrange("p (k d) -> p k d", k=KD), in_=wq_in),
            inc=(ch_wq, 16), writes=[wqbB])
        prev_wqbB = wqbB

        pend_x1 = [None]

        def flush_x1():
            if pend_x1[0] is not None:
                pend_x1[0]()
                pend_x1[0] = None

        def load_xtqB(g):
            sl = g % 2
            B.op("pool", lambda e, g=g, b=b, sl=sl: e.dma_start(
                out=xtqBs[sl].rearrange("p (k t) -> p k t", k=KD),
                in_=xT_in[b][:, :, g * 512:(g + 1) * 512].rearrange("k p t -> p k t")),
                inc=(ch_xtqB2[sl], 16), writes=[xtqBB2[sl]])

        def emit_Q(g):
            sl = g % 2
            for m in range(KD):
                bk = m % 4

                def qmm(e, m=m, bk=bk, sl=sl):
                    inst = None
                    for k in range(KD):
                        st = wqb[:, k * D + m * 128: k * D + (m + 1) * 128]
                        inst = e.matmul(p[bk][:, :], st,
                                        xtqBs[sl][:, k * 512:(k + 1) * 512],
                                        start=(k == 0), stop=(k == KD - 1))
                    return inst
                B.op("pe", qmm, reads=[wqbB, xtqBB2[sl]], writes=[pB[bk]])
                w_, pt_ = _gp(m == 0, QTB_)
                B.op("act", lambda e, m=m, bk=bk: e.activation(
                    out=QT[:, m * 512:(m + 1) * 512], in_=p[bk][:, :],
                    func=AF.Copy, scale=1.0 / 32.0),
                    reads=[pB[bk]], writes=w_, partial=pt_)

        def emit_S(T):
            q = T % 2
            j = T % 4
            t_len = 128 * (T + 1)
            npc = (t_len + 511) // 512
            Pq = Pbs[q]
            B.op("sp", lambda e, T=T, b=b: e.dma_start(
                out=xres[q], in_=x_in[b, T * 128:(T + 1) * 128, :]),
                inc=(ch_xr[q], 16), writes=[xresB[q]])
            npc = (t_len + 1023) // 1024
            for c in range(npc):
                w_c = min(1024, t_len - 1024 * c)
                pr = c % 2
                bk0, bk1 = (0, 1) if pr == 0 else (2, 3)
                nsub = (w_c + 511) // 512

                def smm(e, j=j, c=c, w_c=w_c, pr=pr, nsub=nsub):
                    inst = None
                    for k in range(KD):
                        st = QT[:, k * 512 + j * 128: k * 512 + (j + 1) * 128]
                        for sbi in range(nsub):
                            wseg = min(512, w_c - 512 * sbi)
                            inst = e.matmul(
                                ppair[pr][:, 512 * sbi: 512 * sbi + wseg], st,
                                a1[:, k * S + 1024 * c + 512 * sbi:
                                   k * S + 1024 * c + 512 * sbi + wseg],
                                start=(k == 0), stop=(k == KD - 1))
                    return inst
                B.op("pe", smm, reads=[QTB_] + KTB, writes=[pB[bk0], pB[bk1]])
                if c == npc - 1:
                    B.op("dve", lambda e, pr=pr, w_c=w_c: e.tensor_tensor(
                        out=ppair[pr][:, w_c - 128:w_c],
                        in0=ppair[pr][:, w_c - 128:w_c],
                        in1=cmf[:, :], op=ALU.add),
                        reads=[cmfB], writes=[pB[bk0], pB[bk1]])
                w_, pt_ = _gp(c == 0, mxsB[q])
                B.op("dve", lambda e, pr=pr, c=c, w_c=w_c, q=q: e.reduce_max(
                    mxs[:, q * 8 + c:q * 8 + c + 1], ppair[pr][:, 0:w_c],
                    axis=AX.X),
                    reads=[pB[bk0], pB[bk1]], writes=w_, partial=pt_)
                w_, pt_ = _gp(c == 0, nmxB[q])
                B.op("dve", lambda e, c=c, q=q: e.tensor_scalar(
                    out=nmx[:, q * 8 + c:q * 8 + c + 1],
                    in0=mxs[:, q * 8 + c:q * 8 + c + 1], scalar1=-1.0,
                    scalar2=None, op0=ALU.mult),
                    reads=[mxsB[q]], writes=w_, partial=pt_)
                w_, pt_ = _gp(c == 0, PBs[q], sumsB[q])
                B.op("act", lambda e, pr=pr, c=c, w_c=w_c, q=q, Pq=Pq: e.activation(
                    out=Pq[:, 1024 * c: 1024 * c + w_c], in_=ppair[pr][:, 0:w_c],
                    func=AF.Exp, bias=nmx[:, q * 8 + c:q * 8 + c + 1], scale=1.0,
                    accum_out=sums[:, q * 8 + c:q * 8 + c + 1]),
                    reads=[pB[bk0], pB[bk1], nmxB[q]], writes=w_, partial=pt_)

        def emit_C(T):
            q = T % 2
            t_len = 128 * (T + 1)
            npc = (t_len + 1023) // 1024
            Pq = Pbs[q]
            if npc > 1:
                B.op("dve", lambda e, npc=npc, q=q: e.reduce_max(
                    gm[:, q:q + 1], mxs[:, q * 8:q * 8 + npc], axis=AX.X),
                    reads=[mxsB[q]], writes=[gmB[q]])
                B.op("dve", lambda e, q=q: e.tensor_scalar(
                    out=gmn[:, q:q + 1], in0=gm[:, q:q + 1], scalar1=-1.0,
                    scalar2=None, op0=ALU.mult), reads=[gmB[q]], writes=[gmnB[q]])
                B.op("act", lambda e, npc=npc, q=q: e.activation(
                    out=al[:, q * 8:q * 8 + npc], in_=mxs[:, q * 8:q * 8 + npc],
                    func=AF.Exp, bias=gmn[:, q:q + 1], scale=1.0),
                    reads=[mxsB[q], gmnB[q]], writes=[alB[q]])
                B.op("dve", lambda e, npc=npc, q=q: e.tensor_tensor(
                    out=al2[:, q * 8:q * 8 + npc], in0=al[:, q * 8:q * 8 + npc],
                    in1=sums[:, q * 8:q * 8 + npc],
                    op=ALU.mult), reads=[alB[q], sumsB[q]], writes=[al2B[q]])
                B.op("dve", lambda e, npc=npc, q=q: e.reduce_sum(
                    lv[:, q:q + 1], al2[:, q * 8:q * 8 + npc], axis=AX.X),
                    reads=[al2B[q]], writes=[lvB[q]])
                B.op("dve", lambda e, q=q: e.reciprocal(
                    linv[:, q:q + 1], lv[:, q:q + 1]),
                    reads=[lvB[q]], writes=[linvB[q]])
                for c in range(npc):
                    w_c = min(1024, t_len - 1024 * c)
                    B.op("dve", lambda e, c=c, w_c=w_c, q=q, Pq=Pq: e.tensor_scalar(
                        out=Pq[:, 1024 * c:1024 * c + w_c],
                        in0=Pq[:, 1024 * c:1024 * c + w_c],
                        scalar1=al[:, q * 8 + c:q * 8 + c + 1], scalar2=None,
                        op0=ALU.mult),
                        reads=[alB[q], PBs[q]], partial=[PBs[q]])
            else:
                B.op("dve", lambda e, q=q: e.reciprocal(
                    linv[:, q:q + 1], sums[:, q * 8:q * 8 + 1]),
                    reads=[sumsB[q]], writes=[linvB[q]])

        def make_TA(T):
            q = T % 2
            t_len = 128 * (T + 1)
            Pq = Pbs[q]

            def emit():
                flush_x1()
                nblk = t_len // 128
                ngrp = (nblk + 7) // 8

                def emit_tr(grp):
                    blocks = list(range(grp * 8, min(grp * 8 + 8, nblk)))
                    bq = grp % 2

                    def tr(e, blocks=blocks, bq=bq):
                        inst = None
                        for tt in blocks:
                            inst = e.transpose(
                                ptr[bq][:, (tt % 8) * 128:(tt % 8 + 1) * 128],
                                Pq[:, tt * 128:(tt + 1) * 128], idb[:, :])
                        return inst
                    B.op("pe", tr, reads=[PBs[q], idbB], writes=[ptrB[bq]])

                def emit_ev(grp):
                    blocks = list(range(grp * 8, min(grp * 8 + 8, nblk)))
                    bq = grp % 2
                    nb_ = len(blocks)
                    B.op("act", lambda e, nb_=nb_, bq=bq: e.copy(
                        out=PT[:, 0:nb_ * 128], in_=ptr[bq][:, 0:nb_ * 128]),
                        reads=[ptrB[bq]], writes=[PTB_])

                def emit_av(grp):
                    blocks = list(range(grp * 8, min(grp * 8 + 8, nblk)))

                    def av(e, blocks=blocks, first=(grp == 0), nblk=nblk):
                        inst = None
                        for ii, tt in enumerate(blocks):
                            st = PT[:, ii * 128:(ii + 1) * 128]
                            fi = first and ii == 0
                            la = tt == nblk - 1
                            inst = e.matmul(
                                p[4][:, :], st,
                                a1[:, VB + tt * 1024: VB + tt * 1024 + 512],
                                start=fi, stop=la)
                            inst = e.matmul(
                                p[5][:, :], st,
                                a1[:, VB + tt * 1024 + 512: VB + (tt + 1) * 1024],
                                start=fi, stop=la)
                        return inst
                    w_, pt_ = _gp(grp == 0, pB[4], pB[5])
                    B.op("pe", av, reads=[PTB_, VBuf], writes=w_, partial=pt_)

                emit_tr(0)
                emit_ev(0)
                for grp in range(1, ngrp):
                    emit_tr(grp)
                    emit_av(grp - 1)
                    emit_ev(grp)
                emit_av(ngrp - 1)

                def res_add(e, q=q):
                    e.scalar_tensor_tensor(
                        out=x1tF[:, 0:512], in0=p[4][:, :],
                        scalar=linv[:, q:q + 1],
                        in1=xres[q][:, 0:512], op0=ALU.mult, op1=ALU.add)
                    return e.scalar_tensor_tensor(
                        out=x1tF[:, 512:1024], in0=p[5][:, :],
                        scalar=linv[:, q:q + 1],
                        in1=xres[q][:, 512:1024], op0=ALU.mult, op1=ALU.add)
                B.op("dve", res_add, reads=[pB[4], pB[5], linvB[q], xresB[q]],
                     writes=[x1fB])
                if DBG:
                    gnt = b * TPB + T
                    dbg_toks.append(B.op("sp", lambda e, gnt=gnt: e.dma_start(
                        out=dbg_xatt[gnt * 128:(gnt + 1) * 128, :], in_=x1tF[:, :]),
                        inc=(ch_dbg, 16), reads=[x1fB]))
                    dbg_toks.append(B.op("sp", lambda e, gnt=gnt, q=q: e.dma_start(
                        out=dbg_li[gnt], in_=linv[:, q:q + 1]),
                        inc=(ch_dbg2, 16), reads=[linvB[q]]))

                def bns1(e):
                    e.bn_stats(bnb[:, 0:6], x1tF[:, 0:512])
                    return e.bn_stats(bnb[:, 6:12], x1tF[:, 512:1024])
                B.op("dve", bns1, reads=[x1fB], writes=[bnbB])
                B.op("dve", lambda e: e.bn_aggr(bna[:, :], bnb[:, :]),
                     reads=[bnbB], writes=[bnaB])
                B.op("dve", lambda e: e.tensor_scalar(
                    out=rstd[:, :], in0=bna[:, 1:2], scalar1=EPS, scalar2=None,
                    op0=ALU.add), reads=[bnaB], writes=[rstdB])
                B.op("act", lambda e: e.activation(
                    out=rstd[:, :], in_=rstd[:, :], func=AF.Ln),
                    reads=[rstdB], writes=[rstdB])
                B.op("act", lambda e: e.activation(
                    out=rstd[:, :], in_=rstd[:, :], func=AF.Exp, scale=-0.5),
                    reads=[rstdB], writes=[rstdB])
                B.op("dve", lambda e: e.tensor_scalar(
                    out=x1tF[:, :], in0=x1tF[:, :], scalar1=bna[:, 0:1],
                    scalar2=rstd[:, :], op0=ALU.subtract, op1=ALU.mult),
                    reads=[bnaB, rstdB], writes=[x1fB])
                n0 = (b * TPB + T) * 128
                B.op("sp", lambda e, n0=n0: e.dma_start(
                    out=x1_scr[n0:n0 + 128, :], in_=x1tF[:, :]),
                    inc=(ch_x1w, 16), reads=[x1fB], partial=[x1sB])

                def make_pend(n0=n0):
                    def emit2():
                        B.op("pool", lambda e: e.tensor_copy(x1tC[:, :], x1tF[:, :]),
                             reads=[x1fB], writes=[x1cB])

                        def xtr(e):
                            inst = None
                            for k in range(KD):
                                inst = e.transpose(
                                    p7[:, k * 128:(k + 1) * 128],
                                    x1tC[:, k * 128:(k + 1) * 128], idb[:, :])
                            return inst
                        B.op("pe", xtr, reads=[x1cB, idbB], writes=[p7B])
                        B.op("dve", lambda e: e.tensor_copy(x1TC[:, :], p7[:, :]),
                             reads=[p7B], writes=[x1TB])
                        B.op("sp", lambda e: e.dma_start(
                            out=x1t_scr[:, :, n0:n0 + 128].rearrange(
                                "k p j -> p k j"),
                            in_=x1TC[:, :].rearrange("p (k j) -> p k j", k=KD)),
                            inc=(ch_x1tw, 16), reads=[x1TB], partial=[x1tsB])
                    return emit2
                pend_x1[0] = make_pend()
            return emit

        pend_TA = [None]

        def flush_TA():
            if pend_TA[0] is not None:
                pend_TA[0]()
                pend_TA[0] = None

        load_xtqB(0)
        for T in range(TPB):
            g = T // 4
            if T % 4 == 0:
                if g + 1 < S // 512:
                    load_xtqB(g + 1)
                emit_Q(g)
            emit_S(T)
            emit_C(T)
            flush_TA()
            pend_TA[0] = make_TA(T)
        flush_TA()
        flush_x1()

    if DBG:
        B.op("sp", lambda e: e.nop(), deps=dbg_toks + B.all_toks())
        B.replay(nc, es)
        es.close()
        return nc

    NH = N // 2
    HREG = 8 * NH
    guard3 = B.all_toks()
    x1TrB = [_Buf(guard3), _Buf(guard3)]
    lt8B, et8B, ms8B, mk8B, m18B, ri8B = (_Buf(), _Buf(), _Buf(), _Buf(),
                                          _Buf(), _Buf())
    gwcB = [_Buf(guard3), _Buf(guard3)]
    uwcB = [_Buf(guard3), _Buf(guard3)]
    hbB_ = [_Buf(guard3), _Buf(guard3)]
    wfinB = _Buf(guard3)
    wfinbB = _Buf(guard3)
    t0B, t1B = _Buf(guard3), _Buf(guard3)
    wfulB = [_Buf(guard3), _Buf(guard3)]

    gwc = [bfv(0, 512), bfv(512, 1024)]
    uwc = [bfv(1024, 1536), bfv(1536, 2048)]
    hbuf = [bfv(2048, 4096), bfv(4096, 6144)]
    wfin = f32v(6144, 8192)
    t0 = f32v(8192, 9216)
    t1 = f32v(9216, 10240)
    wful = [f32v(10240, 11264), f32v(11264, 12288)]
    wfinb = bfv(12288, 13312)

    def x1tr_sl(h, k, lo, hi):
        base = h * HREG + k * NH
        return a1[:, base + lo: base + hi]

    for h in range(2):
        for k in range(KD):
            B.op("sp", lambda e, h=h, k=k: e.dma_start(
                out=x1tr_sl(h, k, 0, NH),
                in_=x1t_scr[k][:, h * NH:(h + 1) * NH]),
                inc=(ch_x1tr[h], 16), deps=guard3, reads=[x1tsB],
                partial=[x1TrB[h]])

    lt8_3 = lt8.reshape([128, 8, E])
    et8_3 = et8.reshape([128, 8, E])
    ms8_3 = ms8.reshape([128, 8, E])
    mk8_3 = mk8.reshape([128, 8, 1])
    m18_3 = m18.reshape([128, 8, 1])
    ri8_3 = ri8.reshape([128, 8, 1])
    NTH = NT // 2

    def emit_router(h):
        for nt in range(h * NTH, (h + 1) * NTH):
            ntl = nt - h * NTH

            lb = 4 + (nt % 2)

            def lmm(e, h=h, ntl=ntl, lb=lb):
                inst = None
                for k in range(KD):
                    st = x1tr_sl(h, k, ntl * 128, (ntl + 1) * 128)
                    inst = e.matmul(p[lb][:, 0:E], st, rwb[:, k * E:(k + 1) * E],
                                    start=(k == 0), stop=(k == KD - 1))
                return inst
            B.op("pe", lmm, reads=[x1TrB[h], rwbB], writes=[pB[lb]])
            w_, pt_ = _gp(nt == 0, wfinB)
            B.op("dve", lambda e, nt=nt, lb=lb: e.tensor_tensor(
                out=wfin[:, nt * E:(nt + 1) * E], in0=p[lb][:, 0:E], in1=rbb[:, :],
                op=ALU.add), reads=[pB[lb], rbbB], writes=w_, partial=pt_)
        for g8 in range(h * (NTH // 8), (h + 1) * (NTH // 8)):
            wf_sl = wfin[:, g8 * 8 * E:(g8 + 1) * 8 * E]
            wf3 = wf_sl.rearrange("p (s e) -> p s e", s=8)
            B.op("dve", lambda e, wf_sl=wf_sl: e.tensor_copy(lt8[:, :], wf_sl),
                 reads=[wfinB], writes=[lt8B])
            for it in range(4):
                B.op("dve", lambda e: e.reduce_max(
                    mk8_3[:, :, :], lt8_3[:, :, :], axis=AX.X),
                    reads=[lt8B], writes=[mk8B])
                if it == 0:
                    B.op("dve", lambda e: e.tensor_copy(m18[:, :], mk8[:, :]),
                         reads=[mk8B], writes=[m18B])
                if it < 3:
                    B.op("dve", lambda e: e.tensor_tensor(
                        out=ms8_3[:, :, :], in0=lt8_3[:, :, :],
                        in1=mk8_3.ap().to_broadcast([128, 8, E]), op=ALU.is_ge),
                        reads=[lt8B, mk8B], writes=[ms8B])
                    B.op("dve", lambda e: e.scalar_tensor_tensor(
                        out=lt8[:, :], in0=ms8[:, :], scalar=NEG, in1=lt8[:, :],
                        op0=ALU.mult, op1=ALU.add), reads=[ms8B], writes=[lt8B])
            B.op("dve", lambda e, wf3=wf3: e.tensor_tensor(
                out=lt8_3[:, :, :], in0=wf3,
                in1=m18_3.ap().to_broadcast([128, 8, E]), op=ALU.subtract),
                reads=[wfinB, m18B], writes=[lt8B])
            B.op("act", lambda e: e.activation(
                out=et8[:, :], in_=lt8[:, :], func=AF.Exp),
                reads=[lt8B], writes=[et8B])
            B.op("dve", lambda e, wf3=wf3: e.tensor_tensor(
                out=ms8_3[:, :, :], in0=wf3,
                in1=mk8_3.ap().to_broadcast([128, 8, E]), op=ALU.is_ge),
                reads=[wfinB, mk8B], writes=[ms8B])
            B.op("dve", lambda e: e.tensor_tensor(
                out=et8[:, :], in0=et8[:, :], in1=ms8[:, :], op=ALU.mult),
                reads=[ms8B], writes=[et8B])
            B.op("dve", lambda e: e.reduce_sum(
                ri8_3[:, :, :], et8_3[:, :, :], axis=AX.X),
                reads=[et8B], writes=[ri8B])
            B.op("dve", lambda e: e.reciprocal(ri8[:, :], ri8[:, :]),
                 reads=[ri8B], writes=[ri8B])
            B.op("dve", lambda e, wf3=wf3: e.tensor_tensor(
                out=wf3, in0=et8_3[:, :, :],
                in1=ri8_3.ap().to_broadcast([128, 8, E]), op=ALU.mult),
                reads=[et8B, ri8B], partial=[wfinB])
        for g8 in range(h * (NTH // 8), (h + 1) * (NTH // 8)):
            B.op("act", lambda e, g8=g8: e.copy(
                out=wfinb[:, g8 * 8 * E:(g8 + 1) * 8 * E],
                in_=wfin[:, g8 * 8 * E:(g8 + 1) * 8 * E]),
                reads=[wfinB], partial=[wfinbB])

            def wtr(e, g8=g8):
                inst = None
                for i in range(8):
                    inst = e.transpose(p6[0:E, i * 128:(i + 1) * 128],
                                       wfinb[:, (g8 * 8 + i) * E:(g8 * 8 + i + 1) * E],
                                       idb[:, :])
                return inst
            B.op("pe", wtr, reads=[wfinbB, idbB], writes=[p6B])
            B.op("dve", lambda e: e.tensor_copy(t0[0:E, 0:1024], p6[0:E, :]),
                 reads=[p6B], writes=[t0B])
            B.op("sp", lambda e, g8=g8: e.dma_start(
                out=w_scr[:, g8 * 1024:(g8 + 1) * 1024], in_=t0[0:E, 0:1024]),
                inc=(ch_ws, 16), reads=[t0B], partial=[wscrB])

    def emit_main_half(h, hook=None):
        NCHL = NH // 1024
        for ft in range(E + 4):
            if hook is not None and ft % 4 == 2:
                hook()
            shared = ft >= E
            slot = ft % 2
            if shared:
                cs = ft - E
                B.op("pool", lambda e, s=slot, cs=cs: e.dma_start(
                    out=gwc[s].rearrange("p (k f) -> p k f", k=KD),
                    in_=sgw_in[:, :, cs * 128:(cs + 1) * 128]),
                    inc=(ch_gw[slot], 16), writes=[gwcB[slot]])
                B.op("pool", lambda e, s=slot, cs=cs: e.dma_start(
                    out=uwc[s].rearrange("p (k f) -> p k f", k=KD),
                    in_=suw_in[:, :, cs * 128:(cs + 1) * 128]),
                    inc=(ch_uw[slot], 16), writes=[uwcB[slot]])
            else:
                B.op("pool", lambda e, s=slot, ft=ft: e.dma_start(
                    out=gwc[s].rearrange("p (k f) -> p k f", k=KD),
                    in_=gw_in[:, :, ft * 128:(ft + 1) * 128]),
                    inc=(ch_gw[slot], 16), writes=[gwcB[slot]])
                B.op("pool", lambda e, s=slot, ft=ft: e.dma_start(
                    out=uwc[s].rearrange("p (k f) -> p k f", k=KD),
                    in_=uw_in[:, :, ft * 128:(ft + 1) * 128]),
                    inc=(ch_uw[slot], 16), writes=[uwcB[slot]])
            for nc_ in range(NCHL):
                n0l = nc_ * 1024
                n0g = h * NH + n0l
                wslot = nc_ % 2
                if not shared:
                    B.op("pool", lambda e, ws_=wslot, ft=ft, n0g=n0g: e.dma_start(
                        out=wful[ws_][:, :],
                        in_=w_scr[ft:ft + 1, n0g:n0g + 1024].partition_broadcast(128)),
                        inc=(ch_wf[wslot], 16), reads=[wscrB], writes=[wfulB[wslot]])

                def gmm(e, s=slot, h=h, n0l=n0l):
                    inst = None
                    for k in range(KD):
                        st = gwc[s][:, k * 128:(k + 1) * 128]
                        inst = e.matmul(p[0][:, :], st, x1tr_sl(h, k, n0l, n0l + 512),
                                        start=(k == 0), stop=(k == KD - 1))
                        inst = e.matmul(p[1][:, :], st,
                                        x1tr_sl(h, k, n0l + 512, n0l + 1024),
                                        start=(k == 0), stop=(k == KD - 1))
                    return inst
                B.op("pe", gmm, reads=[gwcB[slot], x1TrB[h]], writes=[pB[0], pB[1]])

                def umm(e, s=slot, h=h, n0l=n0l):
                    inst = None
                    for k in range(KD):
                        st = uwc[s][:, k * 128:(k + 1) * 128]
                        inst = e.matmul(p[2][:, :], st, x1tr_sl(h, k, n0l, n0l + 512),
                                        start=(k == 0), stop=(k == KD - 1))
                        inst = e.matmul(p[3][:, :], st,
                                        x1tr_sl(h, k, n0l + 512, n0l + 1024),
                                        start=(k == 0), stop=(k == KD - 1))
                    return inst
                B.op("pe", umm, reads=[uwcB[slot], x1TrB[h]], writes=[pB[2], pB[3]])

                def silu_(e):
                    e.activation(out=t0[:, 0:512], in_=p[0][:, :], func=AF.Silu)
                    return e.activation(out=t0[:, 512:1024], in_=p[1][:, :],
                                        func=AF.Silu)
                B.op("act", silu_, reads=[pB[0], pB[1]], writes=[t0B])

                half_b = nc_ // (NCHL // 2) if NCHL >= 2 else 0
                pos = nc_ % max(NCHL // 2, 1)
                hb = hbuf[half_b]
                if shared:
                    def tt1s(e, hb=hb, pos=pos):
                        e.tensor_tensor(out=hb[:, pos * 1024: pos * 1024 + 512],
                                        in0=t0[:, 0:512], in1=p[2][:, :], op=ALU.mult)
                        return e.tensor_tensor(
                            out=hb[:, pos * 1024 + 512: (pos + 1) * 1024],
                            in0=t0[:, 512:1024], in1=p[3][:, :], op=ALU.mult)
                    B.op("dve", tt1s, reads=[t0B, pB[2], pB[3]], partial=[hbB_[half_b]])
                else:
                    def tt1(e):
                        e.tensor_tensor(out=t1[:, 0:512], in0=t0[:, 0:512],
                                        in1=p[2][:, :], op=ALU.mult)
                        return e.tensor_tensor(out=t1[:, 512:1024], in0=t0[:, 512:1024],
                                               in1=p[3][:, :], op=ALU.mult)
                    B.op("dve", tt1, reads=[t0B, pB[2], pB[3]], writes=[t1B])
                    B.op("dve", lambda e, hb=hb, pos=pos, ws_=wslot: e.tensor_tensor(
                        out=hb[:, pos * 1024:(pos + 1) * 1024], in0=t1[:, :],
                        in1=wful[ws_][:, :], op=ALU.mult),
                        reads=[t1B, wfulB[wslot]], partial=[hbB_[half_b]])
                if pos == max(NCHL // 2, 1) - 1:
                    width = (pos + 1) * 1024
                    base = h * NH + half_b * (NH // 2)
                    dst = (sa_scr[cs] if shared else hp_scr[ft])
                    scrb = sasB if shared else hpsB
                    B.op("sp", lambda e, hb=hb, width=width, base=base, dst=dst:
                         e.dma_start(out=dst[:, base:base + width],
                                     in_=hb[:, 0:width]),
                         inc=(ch_hb[half_b], 16), reads=[hbB_[half_b]], partial=[scrb])

    emit_router(0)
    emit_router(1)
    emit_main_half(0)

    def emit_dw_load(part=None, nparts=8):
        regs = [x1TrB[0]] + ([x1TrB[1]] if E * D > HREG else [])
        if part is None:
            B.op("pool", lambda e: e.dma_start(
                out=a1[:, 0:E * D].rearrange("p (c d) -> p c d", c=E), in_=dw_in),
                inc=(ch_dw, 16), writes=regs)
            return
        ecs = E // nparts
        B.op("pool", lambda e, part=part, ecs=ecs: e.dma_start(
            out=a1[:, part * ecs * D:(part + 1) * ecs * D].rearrange(
                "p (c d) -> p c d", c=ecs),
            in_=dw_in[:, part * ecs:(part + 1) * ecs, :]),
            inc=(ch_dw, 16),
            deps=[t for b_ in regs for t in (b_.w + b_.r)],
            partial=[regs[0]])

    if E * D <= HREG:
        _dwp = [0]

        def emit_dw_part():
            if _dwp[0] < 8:
                emit_dw_load(part=_dwp[0])
                _dwp[0] += 1
        emit_main_half(1, hook=emit_dw_part)
        while _dwp[0] < 8:
            emit_dw_load(part=_dwp[0])
            _dwp[0] += 1
    else:
        emit_main_half(1)
        emit_dw_load()
    dwbB = x1TrB[0]

    guard4 = B.all_toks()
    HPB2 = max(HREG, E * D)
    sdwbB = _Buf(guard4)
    hpbB = [_Buf(x1TrB[1].w + x1TrB[1].r), _Buf(x1TrB[1].w + x1TrB[1].r)]
    sabB = [_Buf(guard4), _Buf(guard4)]
    xrCB = [_Buf(guard4), _Buf(guard4)]
    outBB = [_Buf(guard4), _Buf(guard4)]
    t2B = _Buf(guard4)
    bnbB2, bnaB2, rstdB2 = _Buf(), _Buf(), _Buf()

    sdwb = bfv(0, 2048)
    sab = [bfv(2048, 3072), bfv(3072, 4096)]
    xrC = [f32v(4096, 5120), f32v(5120, 6144)]
    outb = [f32v(6144, 7168), f32v(7168, 8192)]
    t2 = f32v(8192, 9216)

    B.op("pool", lambda e: e.dma_start(
        out=sdwb.rearrange("p (c d) -> p c d", c=4), in_=sdw_in),
        inc=(ch_sdw, 16), deps=guard4, writes=[sdwbB])

    def load_c2(nc_):
        n0 = nc_ * 512
        slot = nc_ % 2
        hpbase = HPB2 + slot * (E * 512)
        B.op("pool", lambda e, hpbase=hpbase, n0=n0: e.dma_start(
            out=a1[:, hpbase: hpbase + E * 512].rearrange("p (c n) -> p c n", c=E),
            in_=hp_scr[:, :, n0:n0 + 512].rearrange("c p n -> p c n")),
            inc=(ch_hp[slot], 16), reads=[hpsB], writes=[hpbB[slot]])
        B.op("pool", lambda e, s=slot, n0=n0: e.dma_start(
            out=sab[s].rearrange("p (c n) -> p c n", c=4),
            in_=sa_scr[:, :, n0:n0 + 512].rearrange("c p n -> p c n")),
            inc=(ch_sab[slot], 16), reads=[sasB], writes=[sabB[slot]])

    load_c2(0)
    for nc_ in range(N // 512):
        slot = nc_ % 2
        hpbase = HPB2 + slot * (E * 512)
        if nc_ + 1 < N // 512:
            load_c2(nc_ + 1)
        for j in range(4):
            nt = nc_ * 4 + j
            b0, b1 = (0, 1) if nt % 2 == 0 else (2, 3)
            B.op("sp", lambda e, s=nt % 2, nt=nt: e.dma_start(
                out=xrC[s][:, :], in_=x1_scr[nt * 128:(nt + 1) * 128, :]),
                inc=(ch_xrc[nt % 2], 16), reads=[x1sB], writes=[xrCB[nt % 2]])

            def dmm(e, hpbase=hpbase, j=j, b0=b0, b1=b1, s=slot):
                inst = None
                for c in range(E):
                    st = a1[:, hpbase + c * 512 + j * 128: hpbase + c * 512 + (j + 1) * 128]
                    inst = e.matmul(p[b0][:, :], st, a1[:, c * D: c * D + 512],
                                    start=(c == 0), stop=False)
                    inst = e.matmul(p[b1][:, :], st, a1[:, c * D + 512: (c + 1) * D],
                                    start=(c == 0), stop=False)
                for cs in range(4):
                    st = sab[s][:, cs * 512 + j * 128: cs * 512 + (j + 1) * 128]
                    inst = e.matmul(p[b0][:, :], st,
                                    sdwb[:, cs * D: cs * D + 512],
                                    start=False, stop=(cs == 3))
                    inst = e.matmul(p[b1][:, :], st,
                                    sdwb[:, cs * D + 512: (cs + 1) * D],
                                    start=False, stop=(cs == 3))
                return inst
            B.op("pe", dmm, reads=[hpbB[slot], dwbB, sabB[slot], sdwbB],
                 writes=[pB[b0], pB[b1]])

            def res2(e, b0=b0, b1=b1, s=nt % 2):
                e.tensor_tensor(out=t2[:, 0:512], in0=p[b0][:, :],
                                in1=xrC[s][:, 0:512], op=ALU.add)
                return e.tensor_tensor(out=t2[:, 512:1024], in0=p[b1][:, :],
                                       in1=xrC[s][:, 512:1024], op=ALU.add)
            B.op("dve", res2, reads=[pB[b0], pB[b1], xrCB[nt % 2]], writes=[t2B])

            def bns2(e):
                e.bn_stats(bnb[:, 0:6], t2[:, 0:512])
                return e.bn_stats(bnb[:, 6:12], t2[:, 512:1024])
            B.op("dve", bns2, reads=[t2B], writes=[bnbB2])
            B.op("dve", lambda e: e.bn_aggr(bna[:, :], bnb[:, :]),
                 reads=[bnbB2], writes=[bnaB2])
            B.op("dve", lambda e: e.tensor_scalar(
                out=rstd[:, :], in0=bna[:, 1:2], scalar1=EPS, scalar2=None,
                op0=ALU.add), reads=[bnaB2], writes=[rstdB2])
            B.op("act", lambda e: e.activation(
                out=rstd[:, :], in_=rstd[:, :], func=AF.Ln),
                reads=[rstdB2], writes=[rstdB2])
            B.op("act", lambda e: e.activation(
                out=rstd[:, :], in_=rstd[:, :], func=AF.Exp, scale=-0.5),
                reads=[rstdB2], writes=[rstdB2])
            B.op("dve", lambda e, s=nt % 2: e.tensor_scalar(
                out=outb[s][:, :], in0=t2[:, :], scalar1=bna[:, 0:1],
                scalar2=rstd[:, :], op0=ALU.subtract, op1=ALU.mult),
                reads=[t2B, bnaB2, rstdB2], writes=[outBB[nt % 2]])
            out_toks.append(B.op("sp", lambda e, s=nt % 2, nt=nt: e.dma_start(
                out=y_out[nt * 128:(nt + 1) * 128, :], in_=outb[s][:, :]),
                inc=(ch_out[nt % 2], 16), reads=[outBB[nt % 2]]))

    B.op("sp", lambda e: e.nop(), deps=out_toks)
    B.replay(nc, es)
    es.close()
    return nc


def _prep_weights(ins):
    f = lambda a: np.ascontiguousarray(np.asarray(a, dtype=np.float32))
    WQ = f(ins["L0_attn_WQ"])
    WK = f(ins["L0_attn_WK"])
    WV = f(ins["L0_attn_WV"])
    per_k = lambda w: np.ascontiguousarray(
        w.reshape(KD, 128, w.shape[1]).transpose(1, 0, 2))
    gate = f(ins["L0_ffn_gate_w"])
    up = f(ins["L0_ffn_up_w"])
    down = f(ins["L0_ffn_down_w"])
    sg = f(ins["L0_ffn_sgate_w"])
    su = f(ins["L0_ffn_sup_w"])
    sd = f(ins["L0_ffn_sdown_w"])
    rwm = f(ins["L0_ffn_router_w"])
    gT = gate.transpose(2, 0, 1).reshape(D, E * 128)
    uT = up.transpose(2, 0, 1).reshape(D, E * 128)
    dT = down.transpose(0, 2, 1).reshape(E * 128, D)
    sgT = sg.T
    suT = su.T
    sdT = sd.T
    rT = rwm.T
    out = {
        "wq": per_k(WQ), "wk": per_k(WK), "wv": per_k(WV),
        "gw": per_k(gT), "uw": per_k(uT),
        "dw": np.ascontiguousarray(dT.reshape(E, 128, D).transpose(1, 0, 2)),
        "sgw": per_k(sgT), "suw": per_k(suT),
        "sdw": np.ascontiguousarray(sdT.reshape(4, 128, D).transpose(1, 0, 2)),
        "rw": per_k(rT),
        "rb": f(ins["L0_ffn_router_bias"]).reshape(1, E),
        "idn": np.eye(128, dtype=np.float32),
        "cm": np.triu(np.full((128, 128), NEG, dtype=np.float32), k=1),
    }
    return out


def kernel(**inputs):
    S = 4096
    x = np.ascontiguousarray(np.asarray(inputs["L0_attn_x"], dtype=np.float32))
    Bfull = x.shape[0]
    ncores = 8
    per = Bfull // ncores
    w = _prep_weights(inputs)
    nc = build_program(S)
    in_maps = []
    for c in range(ncores):
        xs = x[c * per:(c + 1) * per]
        xTs = np.ascontiguousarray(
            xs.reshape(per, S, KD, 128).transpose(0, 2, 3, 1))
        m = dict(w)
        m["x"] = xs
        m["xT"] = xTs
        in_maps.append(m)
    res = run_bass_kernel_spmd(nc, in_maps, core_ids=list(range(ncores)))
    y = np.concatenate(
        [np.asarray(r["y"], dtype=np.float32).reshape(per, S, D)
         for r in res.results], axis=0)
    return y


# version 18
# speedup vs baseline: 1.0167x; 1.0018x over previous
import numpy as np
import concourse.bass as bass
import concourse.mybir as mybir
from contextlib import ExitStack
from concourse.bass_utils import run_bass_kernel_spmd

F32 = mybir.dt.float32
BF16 = mybir.dt.bfloat16
AF = mybir.ActivationFunctionType
ALU = mybir.AluOpType
AX = mybir.AxisListType

D = 1024
KD = 8
E = 32
FS = 512
NB = 2
EPS = 1e-5
NEG = -1.0e30

ENGINES = ("pe", "dve", "act", "pool", "sp")


class _Chan:
    __slots__ = ("name", "h", "n")

    def __init__(self, name):
        self.name = name
        self.h = None
        self.n = 0


class _Op:
    __slots__ = ("eng", "deps", "fn", "chan", "amt")

    def __init__(self, eng, deps, fn, chan, amt):
        self.eng = eng
        self.deps = deps
        self.fn = fn
        self.chan = chan
        self.amt = amt


class _Buf:
    __slots__ = ("w", "r")

    def __init__(self, guard=()):
        self.w = list(guard)
        self.r = []


def _dedup(toks):
    best = {}
    for (c, v) in toks:
        if v > best.get(c, 0):
            best[c] = v
    return tuple(best.items())


class _Builder:
    def __init__(self):
        self.ops = []
        self.chans = []
        self.ech = {e: self.chan("c_" + e) for e in ENGINES}

    def chan(self, name):
        c = _Chan(name)
        self.chans.append(c)
        return c

    def op(self, eng, fn, deps=(), inc=None, reads=(), writes=(), partial=()):
        d = list(deps)
        for b in reads:
            d += b.w
        for b in writes:
            d += b.w + b.r
        for b in partial:
            d += b.r
        if inc is None:
            cc, amt = self.ech[eng], 1
        else:
            cc, amt = inc
        cc.n += amt
        tok = (cc, cc.n)
        self.ops.append(_Op(eng, _dedup(d), fn, cc, amt))
        for b in reads:
            b.r.append(tok)
        for b in writes:
            b.w = [tok]
            b.r = []
        for b in partial:
            b.w.append(tok)
        return tok

    def all_toks(self):
        return [(c, c.n) for c in self.chans if c.n > 0]

    def replay(self, nc, exit_stack):
        for c in self.chans:
            c.h = exit_stack.enter_context(nc.semaphore(c.name))
        hwm = {}
        per_eng = {e: [] for e in ENGINES}
        for o in self.ops:
            per_eng[o.eng].append(o)
        block = exit_stack.enter_context(nc.Block())
        engmap = {"pe": block.tensor, "dve": block.vector, "act": block.scalar,
                  "pool": block.gpsimd, "sp": block.sync}

        def make_body(ename, ops):
            def body(e):
                for o in ops:
                    for (cc, v) in o.deps:
                        if v > 0 and hwm.get((ename, cc), 0) < v:
                            e.wait_ge(cc.h, v)
                            hwm[(ename, cc)] = v
                    o.fn(e).then_inc(o.chan.h, o.amt)
            return body

        for ename in ENGINES:
            if per_eng[ename]:
                engmap[ename](make_body(ename, per_eng[ename]))


def _gp(first, *bufs):
    return (bufs, ()) if first else ((), bufs)


DBG = 0


def build_program(S):
    F = E * 128
    N = NB * S
    NT = N // 128
    TPB = S // 128

    nc = bass.Bass("TRN2", target_bir_lowering=False, debug=False)
    es = ExitStack()

    dr = lambda name, shape, dt=F32, kind="ExternalInput": \
        nc.dram_tensor(name, list(shape), dt, kind=kind).ap()
    x_in = dr("x", (NB, S, D))
    xT_in = dr("xT", (NB, KD, 128, S))
    wq_in = dr("wq", (128, KD, D))
    wk_in = dr("wk", (128, KD, D))
    wv_in = dr("wv", (128, KD, D))
    gw_in = dr("gw", (128, KD, F))
    uw_in = dr("uw", (128, KD, F))
    dw_in = dr("dw", (128, E, D))
    sgw_in = dr("sgw", (128, KD, FS))
    suw_in = dr("suw", (128, KD, FS))
    sdw_in = dr("sdw", (128, 4, D))
    rw_in = dr("rw", (128, KD, E))
    rb_in = dr("rb", (1, E))
    idn_in = dr("idn", (128, 128))
    cm_in = dr("cm", (128, 128))
    y_out = dr("y", (N, D), kind="ExternalOutput")
    if DBG:
        dbg_xatt = dr("dbg_xatt", (N, D), kind="ExternalOutput")
        dbg_li = dr("dbg_li", (NT, 128, 1), kind="ExternalOutput")

    x1_scr = dr("x1s", (N, D), kind="Internal")
    x1t_scr = dr("x1ts", (KD, 128, N), BF16, kind="Internal")
    hp_scr = dr("hps", (E, 128, N), BF16, kind="Internal")
    sa_scr = dr("sas", (4, 128, N), BF16, kind="Internal")
    w_scr = dr("ws", (E, N), kind="Internal")

    sb = lambda name, shape, dt=F32: es.enter_context(
        nc.sbuf_tensor(name, list(shape), dt))

    A1_COLS = max(8 * S + 8 * S, E * D + 2 * E * 512)
    arena1 = sb("arena1", (128, A1_COLS // 2), F32)
    a1 = arena1.bitcast(BF16)
    VB = 8 * S
    HPB = E * D

    A2_COLS = 18944
    arena2 = sb("arena2", (128, A2_COLS), F32)
    a2f = arena2
    a2 = arena2.bitcast(BF16)

    def bfv(c0f, c1f):
        return a2[:, 2 * c0f: 2 * c1f]

    def f32v(c0f, c1f):
        return a2f[:, c0f:c1f]

    idb = sb("idb", (128, 128), BF16)
    cmf = sb("cmf", (128, 128), F32)
    rbb = sb("rbb", (128, E), F32)
    rwb = sb("rwb", (128, KD * E), BF16)
    mxs = sb("mxs", (128, 16), F32)
    nmx = sb("nmx", (128, 16), F32)
    sums = sb("sums", (128, 16), F32)
    al = sb("al", (128, 16), F32)
    al2 = sb("al2", (128, 16), F32)
    gm = sb("gm", (128, 2), F32)
    gmn = sb("gmn", (128, 2), F32)
    lv = sb("lv", (128, 2), F32)
    linv = sb("linv", (128, 2), F32)
    bnb = sb("bnb", (128, 12), F32)
    bna = sb("bna", (128, 2), F32)
    rstd = sb("rstd", (128, 1), F32)
    lt8 = sb("lt8", (128, 8 * E), F32)
    et8 = sb("et8", (128, 8 * E), F32)
    ms8 = sb("ms8", (128, 8 * E), F32)
    mk8 = sb("mk8", (128, 8), F32)
    m18 = sb("m18", (128, 8), F32)
    ri8 = sb("ri8", (128, 8), F32)

    ppair = [es.enter_context(nc.psum_tensor(f"pp{i}", [128, 1024], F32))
             for i in range(3)]
    p = [ppair[i // 2][:, (i % 2) * 512:(i % 2) * 512 + 512] for i in range(6)]
    p6 = es.enter_context(nc.psum_tensor("p6", [128, 1024], BF16))
    p7 = es.enter_context(nc.psum_tensor("p7", [128, 1024], BF16))
    ptr = [p6, p7]

    B = _Builder()
    ch = B.chan

    idbB, cmfB, rbbB, rwbB = _Buf(), _Buf(), _Buf(), _Buf()
    B.op("pool", lambda e: e.dma_start(out=idb[:, :], in_=idn_in),
         inc=(ch("d_idb"), 16), writes=[idbB])
    B.op("pool", lambda e: e.dma_start(
        out=rwb[:, :].rearrange("p (k e) -> p k e", k=KD), in_=rw_in),
        inc=(ch("d_rwb"), 16), writes=[rwbB])
    B.op("sp", lambda e: e.dma_start(out=cmf[:, :], in_=cm_in),
         inc=(ch("d_cmf"), 16), writes=[cmfB])
    B.op("sp", lambda e: e.dma_start(out=rbb[:, :],
                                     in_=rb_in.partition_broadcast(128)),
         inc=(ch("d_rbb"), 16), writes=[rbbB])

    ch_wk, ch_wv, ch_wq = ch("d_wk"), ch("d_wv"), ch("d_wq")
    ch_xtq = [ch("d_xtq0"), ch("d_xtq1"), ch("d_xtq2")]
    ch_xtqB2 = [ch("d_xtqB0"), ch("d_xtqB1")]
    ch_xr = [ch("d_xr0"), ch("d_xr1")]
    ch_x1w, ch_x1tw, ch_ws = ch("d_x1w"), ch("d_x1tw"), ch("d_ws")
    ch_x1tr = [ch("d_x1tr0"), ch("d_x1tr1")]
    ch_gw = [ch("d_gw0"), ch("d_gw1")]
    ch_uw = [ch("d_uw0"), ch("d_uw1")]
    ch_wf = [ch("d_wf0"), ch("d_wf1")]
    ch_hb = [ch("d_hb0"), ch("d_hb1")]
    ch_dw, ch_sdw = ch("d_dw"), ch("d_sdw")
    ch_hp = [ch("d_hp0"), ch("d_hp1")]
    ch_sab = [ch("d_sab0"), ch("d_sab1")]
    ch_xrc = [ch("d_xrc0"), ch("d_xrc1")]
    ch_out = [ch("d_out0"), ch("d_out1")]

    dbg_toks = []
    ch_dbg, ch_dbg2 = ch("d_dbg"), ch("d_dbg2")
    pB = [_Buf() for _ in range(6)]
    p6B = _Buf()
    p7B = _Buf()
    ptrB = [p6B, p7B]
    x1sB, x1tsB, wscrB = _Buf(), _Buf(), _Buf()
    hpsB = _Buf()
    sasB = _Buf()
    out_toks = []

    prev_wqbB = None
    for b in range(NB):
        guard = B.all_toks()
        if prev_wqbB is not None:
            wkbB, wvbB = _Buf(prev_wqbB.w + prev_wqbB.r), _Buf(guard)
        else:
            wkbB, wvbB = _Buf(guard), _Buf(guard)
        xtqB_ = [_Buf(guard), _Buf(guard), _Buf(guard)]
        KTB = [_Buf(guard) for _ in range(KD)]
        VBuf = _Buf(guard)

        wkb = bfv(0, 4096)
        wvb = bfv(4096, 8192)
        xtqA = [bfv(8192, 10240), bfv(10240, 12288), bfv(12288, 14336)]

        B.op("pool", lambda e: e.dma_start(
            out=wkb.rearrange("p (k d) -> p k d", k=KD), in_=wk_in),
            inc=(ch_wk, 16), writes=[wkbB])
        B.op("pool", lambda e: e.dma_start(
            out=wvb.rearrange("p (k d) -> p k d", k=KD), in_=wv_in),
            inc=(ch_wv, 16), writes=[wvbB])

        def xtq_sl(slot, k, lo, hi):
            return xtqA[slot][:, k * 512 + lo: k * 512 + hi]

        def load_xtq(cidx):
            slot = cidx % 3
            B.op("pool", lambda e, s=slot, c=cidx, b=b: e.dma_start(
                out=xtqA[s].rearrange("p (k t) -> p k t", k=KD),
                in_=xT_in[b][:, :, c * 512:(c + 1) * 512].rearrange("k p t -> p k t")),
                inc=(ch_xtq[slot], 16), writes=[xtqB_[slot]])

        nchunks = S // 512
        load_xtq(0)
        load_xtq(1)
        for i in range(nchunks // 2):
            s0, s1 = (2 * i) % 3, (2 * i + 1) % 3
            if 2 * i + 2 < nchunks:
                load_xtq(2 * i + 2)
            for m in range(KD):
                bA, bBk = (0, 1) if m % 2 == 0 else (2, 3)

                def kmm(e, m=m, bA=bA, bBk=bBk, s0=s0, s1=s1):
                    inst = None
                    for k in range(KD):
                        st = wkb[:, k * D + m * 128: k * D + (m + 1) * 128]
                        inst = e.matmul(p[bA][:, :], st, xtq_sl(s0, k, 0, 512),
                                        start=(k == 0), stop=(k == KD - 1))
                        inst = e.matmul(p[bBk][:, :], st, xtq_sl(s1, k, 0, 512),
                                        start=(k == 0), stop=(k == KD - 1))
                    return inst
                B.op("pe", kmm, reads=[wkbB, xtqB_[s0], xtqB_[s1]],
                     writes=[pB[bA], pB[bBk]])
                for (bk, half) in ((bA, 0), (bBk, 1)):
                    c0 = (2 * i + half) * 512
                    B.op("dve", lambda e, bk=bk, m=m, c0=c0: e.tensor_copy(
                        a1[:, m * S + c0: m * S + c0 + 512], p[bk][:, :]),
                        reads=[pB[bk]], partial=[KTB[m]])
            for ci in (2 * i, 2 * i + 1):
                slot = ci % 3
                if ci == 2 * i + 1 and 2 * i + 3 < nchunks:
                    load_xtq(2 * i + 3)
                for j in range(4):
                    tt = ci * 4 + j
                    b0, b1 = (4, 5) if tt % 2 == 0 else (0, 1)

                    def vmm(e, slot=slot, j=j, b0=b0, b1=b1):
                        inst = None
                        for k in range(KD):
                            st = xtq_sl(slot, k, j * 128, (j + 1) * 128)
                            inst = e.matmul(p[b0][:, :], st,
                                            wvb[:, k * D: k * D + 512],
                                            start=(k == 0), stop=(k == KD - 1))
                            inst = e.matmul(p[b1][:, :], st,
                                            wvb[:, k * D + 512: (k + 1) * D],
                                            start=(k == 0), stop=(k == KD - 1))
                        return inst
                    B.op("pe", vmm, reads=[wvbB, xtqB_[slot]],
                         writes=[pB[b0], pB[b1]])
                    for (bk, off) in ((b0, 0), (b1, 512)):
                        B.op("dve", lambda e, bk=bk, tt=tt, off=off: e.tensor_copy(
                            a1[:, VB + tt * 1024 + off: VB + tt * 1024 + off + 512],
                            p[bk][:, :]), reads=[pB[bk]], partial=[VBuf])

        guard2 = B.all_toks()
        wqbB = _Buf(guard2)
        xtqBB2 = [_Buf(guard2), _Buf(guard2)]
        QTB_ = _Buf(guard2)
        PB_ = _Buf(guard2)
        PTB_ = _Buf(guard2)
        xresB = [_Buf(guard2), _Buf(guard2)]
        x1fB, x1cB, x1TB, xtqBB = (_Buf(guard2), _Buf(guard2), _Buf(guard2),
                                   _Buf(guard2))
        mxsB = [_Buf(), _Buf()]
        nmxB = [_Buf(), _Buf()]
        sumsB = [_Buf(), _Buf()]
        alB = [_Buf(), _Buf()]
        al2B = [_Buf(), _Buf()]
        gmB = [_Buf(), _Buf()]
        gmnB = [_Buf(), _Buf()]
        lvB = [_Buf(), _Buf()]
        linvB = [_Buf(), _Buf()]
        PBs = [PB_, _Buf(guard2)]
        bnbB, bnaB, rstdB = _Buf(), _Buf(), _Buf()

        wqb = bfv(0, 4096)
        QT = bfv(4096, 6144)
        Pbs = [bfv(6144, 8192), bfv(16896, 18944)]
        PT = bfv(8192, 8704)
        xres = [f32v(8704, 9728), f32v(9728, 10752)]
        x1tF = f32v(10752, 11776)
        x1tC = bfv(11776, 12288)
        x1TC = bfv(12288, 12800)
        xtqBs = [bfv(12800, 14848), bfv(14848, 16896)]

        B.op("pool", lambda e: e.dma_start(
            out=wqb.rearrange("p (k d) -> p k d", k=KD), in_=wq_in),
            inc=(ch_wq, 16), writes=[wqbB])
        prev_wqbB = wqbB

        pend_x1 = [None]

        def flush_x1():
            if pend_x1[0] is not None:
                pend_x1[0]()
                pend_x1[0] = None

        def load_xtqB(g):
            sl = g % 2
            B.op("pool", lambda e, g=g, b=b, sl=sl: e.dma_start(
                out=xtqBs[sl].rearrange("p (k t) -> p k t", k=KD),
                in_=xT_in[b][:, :, g * 512:(g + 1) * 512].rearrange("k p t -> p k t")),
                inc=(ch_xtqB2[sl], 16), writes=[xtqBB2[sl]])

        def emit_Q(g):
            sl = g % 2
            for m in range(KD):
                bk = m % 4

                def qmm(e, m=m, bk=bk, sl=sl):
                    inst = None
                    for k in range(KD):
                        st = wqb[:, k * D + m * 128: k * D + (m + 1) * 128]
                        inst = e.matmul(p[bk][:, :], st,
                                        xtqBs[sl][:, k * 512:(k + 1) * 512],
                                        start=(k == 0), stop=(k == KD - 1))
                    return inst
                B.op("pe", qmm, reads=[wqbB, xtqBB2[sl]], writes=[pB[bk]])
                w_, pt_ = _gp(m == 0, QTB_)
                B.op("act", lambda e, m=m, bk=bk: e.activation(
                    out=QT[:, m * 512:(m + 1) * 512], in_=p[bk][:, :],
                    func=AF.Copy, scale=1.0 / 32.0),
                    reads=[pB[bk]], writes=w_, partial=pt_)

        def emit_S(T):
            q = T % 2
            j = T % 4
            t_len = 128 * (T + 1)
            npc = (t_len + 511) // 512
            Pq = Pbs[q]
            B.op("sp", lambda e, T=T, b=b: e.dma_start(
                out=xres[q], in_=x_in[b, T * 128:(T + 1) * 128, :]),
                inc=(ch_xr[q], 16), writes=[xresB[q]])
            npc = (t_len + 1023) // 1024
            for c in range(npc):
                w_c = min(1024, t_len - 1024 * c)
                pr = c % 2
                bk0, bk1 = (0, 1) if pr == 0 else (2, 3)
                nsub = (w_c + 511) // 512

                def smm(e, j=j, c=c, w_c=w_c, pr=pr, nsub=nsub):
                    inst = None
                    for k in range(KD):
                        st = QT[:, k * 512 + j * 128: k * 512 + (j + 1) * 128]
                        for sbi in range(nsub):
                            wseg = min(512, w_c - 512 * sbi)
                            inst = e.matmul(
                                ppair[pr][:, 512 * sbi: 512 * sbi + wseg], st,
                                a1[:, k * S + 1024 * c + 512 * sbi:
                                   k * S + 1024 * c + 512 * sbi + wseg],
                                start=(k == 0), stop=(k == KD - 1))
                    return inst
                B.op("pe", smm, reads=[QTB_] + KTB, writes=[pB[bk0], pB[bk1]])
                if c == npc - 1:
                    B.op("dve", lambda e, pr=pr, w_c=w_c: e.tensor_tensor(
                        out=ppair[pr][:, w_c - 128:w_c],
                        in0=ppair[pr][:, w_c - 128:w_c],
                        in1=cmf[:, :], op=ALU.add),
                        reads=[cmfB], writes=[pB[bk0], pB[bk1]])
                w_, pt_ = _gp(c == 0, mxsB[q])
                B.op("dve", lambda e, pr=pr, c=c, w_c=w_c, q=q: e.reduce_max(
                    mxs[:, q * 8 + c:q * 8 + c + 1], ppair[pr][:, 0:w_c],
                    axis=AX.X),
                    reads=[pB[bk0], pB[bk1]], writes=w_, partial=pt_)
                w_, pt_ = _gp(c == 0, nmxB[q])
                B.op("dve", lambda e, c=c, q=q: e.tensor_scalar(
                    out=nmx[:, q * 8 + c:q * 8 + c + 1],
                    in0=mxs[:, q * 8 + c:q * 8 + c + 1], scalar1=-1.0,
                    scalar2=None, op0=ALU.mult),
                    reads=[mxsB[q]], writes=w_, partial=pt_)
                w_, pt_ = _gp(c == 0, PBs[q], sumsB[q])
                B.op("act", lambda e, pr=pr, c=c, w_c=w_c, q=q, Pq=Pq: e.activation(
                    out=Pq[:, 1024 * c: 1024 * c + w_c], in_=ppair[pr][:, 0:w_c],
                    func=AF.Exp, bias=nmx[:, q * 8 + c:q * 8 + c + 1], scale=1.0,
                    accum_out=sums[:, q * 8 + c:q * 8 + c + 1]),
                    reads=[pB[bk0], pB[bk1], nmxB[q]], writes=w_, partial=pt_)

        def emit_C(T):
            q = T % 2
            t_len = 128 * (T + 1)
            npc = (t_len + 1023) // 1024
            Pq = Pbs[q]
            if npc > 1:
                B.op("dve", lambda e, npc=npc, q=q: e.reduce_max(
                    gm[:, q:q + 1], mxs[:, q * 8:q * 8 + npc], axis=AX.X),
                    reads=[mxsB[q]], writes=[gmB[q]])
                B.op("dve", lambda e, q=q: e.tensor_scalar(
                    out=gmn[:, q:q + 1], in0=gm[:, q:q + 1], scalar1=-1.0,
                    scalar2=None, op0=ALU.mult), reads=[gmB[q]], writes=[gmnB[q]])
                B.op("act", lambda e, npc=npc, q=q: e.activation(
                    out=al[:, q * 8:q * 8 + npc], in_=mxs[:, q * 8:q * 8 + npc],
                    func=AF.Exp, bias=gmn[:, q:q + 1], scale=1.0),
                    reads=[mxsB[q], gmnB[q]], writes=[alB[q]])
                B.op("dve", lambda e, npc=npc, q=q: e.tensor_tensor(
                    out=al2[:, q * 8:q * 8 + npc], in0=al[:, q * 8:q * 8 + npc],
                    in1=sums[:, q * 8:q * 8 + npc],
                    op=ALU.mult), reads=[alB[q], sumsB[q]], writes=[al2B[q]])
                B.op("dve", lambda e, npc=npc, q=q: e.reduce_sum(
                    lv[:, q:q + 1], al2[:, q * 8:q * 8 + npc], axis=AX.X),
                    reads=[al2B[q]], writes=[lvB[q]])
                B.op("dve", lambda e, q=q: e.reciprocal(
                    linv[:, q:q + 1], lv[:, q:q + 1]),
                    reads=[lvB[q]], writes=[linvB[q]])
                for c in range(npc):
                    w_c = min(1024, t_len - 1024 * c)
                    B.op("dve", lambda e, c=c, w_c=w_c, q=q, Pq=Pq: e.tensor_scalar(
                        out=Pq[:, 1024 * c:1024 * c + w_c],
                        in0=Pq[:, 1024 * c:1024 * c + w_c],
                        scalar1=al[:, q * 8 + c:q * 8 + c + 1], scalar2=None,
                        op0=ALU.mult),
                        reads=[alB[q], PBs[q]], partial=[PBs[q]])
            else:
                B.op("dve", lambda e, q=q: e.reciprocal(
                    linv[:, q:q + 1], sums[:, q * 8:q * 8 + 1]),
                    reads=[sumsB[q]], writes=[linvB[q]])

        def make_TA(T):
            q = T % 2
            t_len = 128 * (T + 1)
            Pq = Pbs[q]

            def emit():
                flush_x1()
                nblk = t_len // 128
                ngrp = (nblk + 7) // 8

                def emit_tr(grp):
                    blocks = list(range(grp * 8, min(grp * 8 + 8, nblk)))
                    bq = grp % 2

                    def tr(e, blocks=blocks, bq=bq):
                        inst = None
                        for tt in blocks:
                            inst = e.transpose(
                                ptr[bq][:, (tt % 8) * 128:(tt % 8 + 1) * 128],
                                Pq[:, tt * 128:(tt + 1) * 128], idb[:, :])
                        return inst
                    B.op("pe", tr, reads=[PBs[q], idbB], writes=[ptrB[bq]])

                def emit_ev(grp):
                    blocks = list(range(grp * 8, min(grp * 8 + 8, nblk)))
                    bq = grp % 2
                    nb_ = len(blocks)
                    B.op("act", lambda e, nb_=nb_, bq=bq: e.copy(
                        out=PT[:, 0:nb_ * 128], in_=ptr[bq][:, 0:nb_ * 128]),
                        reads=[ptrB[bq]], writes=[PTB_])

                def emit_av(grp):
                    blocks = list(range(grp * 8, min(grp * 8 + 8, nblk)))

                    def av(e, blocks=blocks, first=(grp == 0), nblk=nblk):
                        inst = None
                        for ii, tt in enumerate(blocks):
                            st = PT[:, ii * 128:(ii + 1) * 128]
                            fi = first and ii == 0
                            la = tt == nblk - 1
                            inst = e.matmul(
                                p[4][:, :], st,
                                a1[:, VB + tt * 1024: VB + tt * 1024 + 512],
                                start=fi, stop=la)
                            inst = e.matmul(
                                p[5][:, :], st,
                                a1[:, VB + tt * 1024 + 512: VB + (tt + 1) * 1024],
                                start=fi, stop=la)
                        return inst
                    w_, pt_ = _gp(grp == 0, pB[4], pB[5])
                    B.op("pe", av, reads=[PTB_, VBuf], writes=w_, partial=pt_)

                emit_tr(0)
                emit_ev(0)
                for grp in range(1, ngrp):
                    emit_tr(grp)
                    emit_av(grp - 1)
                    emit_ev(grp)
                emit_av(ngrp - 1)

                def res_add(e, q=q):
                    e.scalar_tensor_tensor(
                        out=x1tF[:, 0:512], in0=p[4][:, :],
                        scalar=linv[:, q:q + 1],
                        in1=xres[q][:, 0:512], op0=ALU.mult, op1=ALU.add)
                    return e.scalar_tensor_tensor(
                        out=x1tF[:, 512:1024], in0=p[5][:, :],
                        scalar=linv[:, q:q + 1],
                        in1=xres[q][:, 512:1024], op0=ALU.mult, op1=ALU.add)
                B.op("dve", res_add, reads=[pB[4], pB[5], linvB[q], xresB[q]],
                     writes=[x1fB])
                if DBG:
                    gnt = b * TPB + T
                    dbg_toks.append(B.op("sp", lambda e, gnt=gnt: e.dma_start(
                        out=dbg_xatt[gnt * 128:(gnt + 1) * 128, :], in_=x1tF[:, :]),
                        inc=(ch_dbg, 16), reads=[x1fB]))
                    dbg_toks.append(B.op("sp", lambda e, gnt=gnt, q=q: e.dma_start(
                        out=dbg_li[gnt], in_=linv[:, q:q + 1]),
                        inc=(ch_dbg2, 16), reads=[linvB[q]]))

                def bns1(e):
                    e.bn_stats(bnb[:, 0:6], x1tF[:, 0:512])
                    return e.bn_stats(bnb[:, 6:12], x1tF[:, 512:1024])
                B.op("dve", bns1, reads=[x1fB], writes=[bnbB])
                B.op("dve", lambda e: e.bn_aggr(bna[:, :], bnb[:, :]),
                     reads=[bnbB], writes=[bnaB])
                B.op("dve", lambda e: e.tensor_scalar(
                    out=rstd[:, :], in0=bna[:, 1:2], scalar1=EPS, scalar2=None,
                    op0=ALU.add), reads=[bnaB], writes=[rstdB])
                B.op("act", lambda e: e.activation(
                    out=rstd[:, :], in_=rstd[:, :], func=AF.Ln),
                    reads=[rstdB], writes=[rstdB])
                B.op("act", lambda e: e.activation(
                    out=rstd[:, :], in_=rstd[:, :], func=AF.Exp, scale=-0.5),
                    reads=[rstdB], writes=[rstdB])
                B.op("dve", lambda e: e.tensor_scalar(
                    out=x1tF[:, :], in0=x1tF[:, :], scalar1=bna[:, 0:1],
                    scalar2=rstd[:, :], op0=ALU.subtract, op1=ALU.mult),
                    reads=[bnaB, rstdB], writes=[x1fB])
                n0 = (b * TPB + T) * 128
                B.op("sp", lambda e, n0=n0: e.dma_start(
                    out=x1_scr[n0:n0 + 128, :], in_=x1tF[:, :]),
                    inc=(ch_x1w, 16), reads=[x1fB], partial=[x1sB])

                def make_pend(n0=n0):
                    def emit2():
                        B.op("pool", lambda e: e.tensor_copy(x1tC[:, :], x1tF[:, :]),
                             reads=[x1fB], writes=[x1cB])

                        def xtr(e):
                            inst = None
                            for k in range(KD):
                                inst = e.transpose(
                                    p7[:, k * 128:(k + 1) * 128],
                                    x1tC[:, k * 128:(k + 1) * 128], idb[:, :])
                            return inst
                        B.op("pe", xtr, reads=[x1cB, idbB], writes=[p7B])
                        B.op("dve", lambda e: e.tensor_copy(x1TC[:, :], p7[:, :]),
                             reads=[p7B], writes=[x1TB])
                        B.op("sp", lambda e: e.dma_start(
                            out=x1t_scr[:, :, n0:n0 + 128].rearrange(
                                "k p j -> p k j"),
                            in_=x1TC[:, :].rearrange("p (k j) -> p k j", k=KD)),
                            inc=(ch_x1tw, 16), reads=[x1TB], partial=[x1tsB])
                    return emit2
                pend_x1[0] = make_pend()
            return emit

        pend_TA = [None]

        def flush_TA():
            if pend_TA[0] is not None:
                pend_TA[0]()
                pend_TA[0] = None

        load_xtqB(0)
        for T in range(TPB):
            g = T // 4
            if T % 4 == 0:
                if g + 1 < S // 512:
                    load_xtqB(g + 1)
                emit_Q(g)
            emit_S(T)
            emit_C(T)
            flush_TA()
            pend_TA[0] = make_TA(T)
        flush_TA()
        flush_x1()

    if DBG:
        B.op("sp", lambda e: e.nop(), deps=dbg_toks + B.all_toks())
        B.replay(nc, es)
        es.close()
        return nc

    NH = N // 2
    HREG = 8 * NH
    guard3 = B.all_toks()
    x1TrB = [_Buf(guard3), _Buf(guard3)]
    lt8B, et8B, ms8B, mk8B, m18B, ri8B = (_Buf(), _Buf(), _Buf(), _Buf(),
                                          _Buf(), _Buf())
    gwcB = [_Buf(guard3), _Buf(guard3)]
    uwcB = [_Buf(guard3), _Buf(guard3)]
    hbB_ = [_Buf(guard3), _Buf(guard3)]
    wfinB = _Buf(guard3)
    wfinbB = _Buf(guard3)
    t0B, t1B = _Buf(guard3), _Buf(guard3)
    wfulB = [_Buf(guard3), _Buf(guard3)]

    gwc = [bfv(0, 512), bfv(512, 1024)]
    uwc = [bfv(1024, 1536), bfv(1536, 2048)]
    hbuf = [bfv(2048, 4096), bfv(4096, 6144)]
    wfin = f32v(6144, 8192)
    t0 = f32v(8192, 9216)
    t1 = f32v(9216, 10240)
    wful = [f32v(10240, 11264), f32v(11264, 12288)]
    wfinb = bfv(12288, 13312)

    def x1tr_sl(h, k, lo, hi):
        base = h * HREG + k * NH
        return a1[:, base + lo: base + hi]

    for h in range(2):
        for k in range(KD):
            B.op("sp", lambda e, h=h, k=k: e.dma_start(
                out=x1tr_sl(h, k, 0, NH),
                in_=x1t_scr[k][:, h * NH:(h + 1) * NH]),
                inc=(ch_x1tr[h], 16), deps=guard3, reads=[x1tsB],
                partial=[x1TrB[h]])

    lt8_3 = lt8.reshape([128, 8, E])
    et8_3 = et8.reshape([128, 8, E])
    ms8_3 = ms8.reshape([128, 8, E])
    mk8_3 = mk8.reshape([128, 8, 1])
    m18_3 = m18.reshape([128, 8, 1])
    ri8_3 = ri8.reshape([128, 8, 1])
    NTH = NT // 2

    def emit_router(h):
        for nt in range(h * NTH, (h + 1) * NTH):
            ntl = nt - h * NTH

            lb = 4 + (nt % 2)

            def lmm(e, h=h, ntl=ntl, lb=lb):
                inst = None
                for k in range(KD):
                    st = x1tr_sl(h, k, ntl * 128, (ntl + 1) * 128)
                    inst = e.matmul(p[lb][:, 0:E], st, rwb[:, k * E:(k + 1) * E],
                                    start=(k == 0), stop=(k == KD - 1))
                return inst
            B.op("pe", lmm, reads=[x1TrB[h], rwbB], writes=[pB[lb]])
            w_, pt_ = _gp(nt == 0, wfinB)
            B.op("dve", lambda e, nt=nt, lb=lb: e.tensor_tensor(
                out=wfin[:, nt * E:(nt + 1) * E], in0=p[lb][:, 0:E], in1=rbb[:, :],
                op=ALU.add), reads=[pB[lb], rbbB], writes=w_, partial=pt_)
        for g8 in range(h * (NTH // 8), (h + 1) * (NTH // 8)):
            wf_sl = wfin[:, g8 * 8 * E:(g8 + 1) * 8 * E]
            wf3 = wf_sl.rearrange("p (s e) -> p s e", s=8)
            B.op("dve", lambda e, wf_sl=wf_sl: e.tensor_copy(lt8[:, :], wf_sl),
                 reads=[wfinB], writes=[lt8B])
            for it in range(4):
                B.op("dve", lambda e: e.reduce_max(
                    mk8_3[:, :, :], lt8_3[:, :, :], axis=AX.X),
                    reads=[lt8B], writes=[mk8B])
                if it == 0:
                    B.op("dve", lambda e: e.tensor_copy(m18[:, :], mk8[:, :]),
                         reads=[mk8B], writes=[m18B])
                if it < 3:
                    B.op("dve", lambda e: e.tensor_tensor(
                        out=ms8_3[:, :, :], in0=lt8_3[:, :, :],
                        in1=mk8_3.ap().to_broadcast([128, 8, E]), op=ALU.is_ge),
                        reads=[lt8B, mk8B], writes=[ms8B])
                    B.op("dve", lambda e: e.scalar_tensor_tensor(
                        out=lt8[:, :], in0=ms8[:, :], scalar=NEG, in1=lt8[:, :],
                        op0=ALU.mult, op1=ALU.add), reads=[ms8B], writes=[lt8B])
            B.op("dve", lambda e, wf3=wf3: e.tensor_tensor(
                out=lt8_3[:, :, :], in0=wf3,
                in1=m18_3.ap().to_broadcast([128, 8, E]), op=ALU.subtract),
                reads=[wfinB, m18B], writes=[lt8B])
            B.op("act", lambda e: e.activation(
                out=et8[:, :], in_=lt8[:, :], func=AF.Exp),
                reads=[lt8B], writes=[et8B])
            B.op("dve", lambda e, wf3=wf3: e.tensor_tensor(
                out=ms8_3[:, :, :], in0=wf3,
                in1=mk8_3.ap().to_broadcast([128, 8, E]), op=ALU.is_ge),
                reads=[wfinB, mk8B], writes=[ms8B])
            B.op("dve", lambda e: e.tensor_tensor(
                out=et8[:, :], in0=et8[:, :], in1=ms8[:, :], op=ALU.mult),
                reads=[ms8B], writes=[et8B])
            B.op("dve", lambda e: e.reduce_sum(
                ri8_3[:, :, :], et8_3[:, :, :], axis=AX.X),
                reads=[et8B], writes=[ri8B])
            B.op("dve", lambda e: e.reciprocal(ri8[:, :], ri8[:, :]),
                 reads=[ri8B], writes=[ri8B])
            B.op("dve", lambda e, wf3=wf3: e.tensor_tensor(
                out=wf3, in0=et8_3[:, :, :],
                in1=ri8_3.ap().to_broadcast([128, 8, E]), op=ALU.mult),
                reads=[et8B, ri8B], partial=[wfinB])
    def emit_router_wtr(h):
        for g8 in range(h * (NTH // 8), (h + 1) * (NTH // 8)):
            B.op("act", lambda e, g8=g8: e.copy(
                out=wfinb[:, g8 * 8 * E:(g8 + 1) * 8 * E],
                in_=wfin[:, g8 * 8 * E:(g8 + 1) * 8 * E]),
                reads=[wfinB], partial=[wfinbB])

            def wtr(e, g8=g8):
                inst = None
                for i in range(8):
                    inst = e.transpose(p6[0:E, i * 128:(i + 1) * 128],
                                       wfinb[:, (g8 * 8 + i) * E:(g8 * 8 + i + 1) * E],
                                       idb[:, :])
                return inst
            B.op("pe", wtr, reads=[wfinbB, idbB], writes=[p6B])
            B.op("dve", lambda e: e.tensor_copy(t0[0:E, 0:1024], p6[0:E, :]),
                 reads=[p6B], writes=[t0B])
            B.op("sp", lambda e, g8=g8: e.dma_start(
                out=w_scr[:, g8 * 1024:(g8 + 1) * 1024], in_=t0[0:E, 0:1024]),
                inc=(ch_ws, 16), reads=[t0B], partial=[wscrB])

    def emit_main_half(h, hook=None, part="experts"):
        NCHL = NH // 1024
        ftlist = list(range(E, E + 4)) if part == "shared" else list(range(E))
        for ft in ftlist:
            if hook is not None and ft % 4 == 2:
                hook()
            shared = ft >= E
            slot = ft % 2
            if shared:
                cs = ft - E
                B.op("pool", lambda e, s=slot, cs=cs: e.dma_start(
                    out=gwc[s].rearrange("p (k f) -> p k f", k=KD),
                    in_=sgw_in[:, :, cs * 128:(cs + 1) * 128]),
                    inc=(ch_gw[slot], 16), writes=[gwcB[slot]])
                B.op("pool", lambda e, s=slot, cs=cs: e.dma_start(
                    out=uwc[s].rearrange("p (k f) -> p k f", k=KD),
                    in_=suw_in[:, :, cs * 128:(cs + 1) * 128]),
                    inc=(ch_uw[slot], 16), writes=[uwcB[slot]])
            else:
                B.op("pool", lambda e, s=slot, ft=ft: e.dma_start(
                    out=gwc[s].rearrange("p (k f) -> p k f", k=KD),
                    in_=gw_in[:, :, ft * 128:(ft + 1) * 128]),
                    inc=(ch_gw[slot], 16), writes=[gwcB[slot]])
                B.op("pool", lambda e, s=slot, ft=ft: e.dma_start(
                    out=uwc[s].rearrange("p (k f) -> p k f", k=KD),
                    in_=uw_in[:, :, ft * 128:(ft + 1) * 128]),
                    inc=(ch_uw[slot], 16), writes=[uwcB[slot]])
            for nc_ in range(NCHL):
                n0l = nc_ * 1024
                n0g = h * NH + n0l
                wslot = nc_ % 2
                if not shared:
                    B.op("pool", lambda e, ws_=wslot, ft=ft, n0g=n0g: e.dma_start(
                        out=wful[ws_][:, :],
                        in_=w_scr[ft:ft + 1, n0g:n0g + 1024].partition_broadcast(128)),
                        inc=(ch_wf[wslot], 16), reads=[wscrB], writes=[wfulB[wslot]])

                def gmm(e, s=slot, h=h, n0l=n0l):
                    inst = None
                    for k in range(KD):
                        st = gwc[s][:, k * 128:(k + 1) * 128]
                        inst = e.matmul(p[0][:, :], st, x1tr_sl(h, k, n0l, n0l + 512),
                                        start=(k == 0), stop=(k == KD - 1))
                        inst = e.matmul(p[1][:, :], st,
                                        x1tr_sl(h, k, n0l + 512, n0l + 1024),
                                        start=(k == 0), stop=(k == KD - 1))
                    return inst
                B.op("pe", gmm, reads=[gwcB[slot], x1TrB[h]], writes=[pB[0], pB[1]])

                def umm(e, s=slot, h=h, n0l=n0l):
                    inst = None
                    for k in range(KD):
                        st = uwc[s][:, k * 128:(k + 1) * 128]
                        inst = e.matmul(p[2][:, :], st, x1tr_sl(h, k, n0l, n0l + 512),
                                        start=(k == 0), stop=(k == KD - 1))
                        inst = e.matmul(p[3][:, :], st,
                                        x1tr_sl(h, k, n0l + 512, n0l + 1024),
                                        start=(k == 0), stop=(k == KD - 1))
                    return inst
                B.op("pe", umm, reads=[uwcB[slot], x1TrB[h]], writes=[pB[2], pB[3]])

                def silu_(e):
                    e.activation(out=t0[:, 0:512], in_=p[0][:, :], func=AF.Silu)
                    return e.activation(out=t0[:, 512:1024], in_=p[1][:, :],
                                        func=AF.Silu)
                B.op("act", silu_, reads=[pB[0], pB[1]], writes=[t0B])

                half_b = nc_ // (NCHL // 2) if NCHL >= 2 else 0
                pos = nc_ % max(NCHL // 2, 1)
                hb = hbuf[half_b]
                if shared:
                    def tt1s(e, hb=hb, pos=pos):
                        e.tensor_tensor(out=hb[:, pos * 1024: pos * 1024 + 512],
                                        in0=t0[:, 0:512], in1=p[2][:, :], op=ALU.mult)
                        return e.tensor_tensor(
                            out=hb[:, pos * 1024 + 512: (pos + 1) * 1024],
                            in0=t0[:, 512:1024], in1=p[3][:, :], op=ALU.mult)
                    B.op("dve", tt1s, reads=[t0B, pB[2], pB[3]], partial=[hbB_[half_b]])
                else:
                    def tt1(e):
                        e.tensor_tensor(out=t1[:, 0:512], in0=t0[:, 0:512],
                                        in1=p[2][:, :], op=ALU.mult)
                        return e.tensor_tensor(out=t1[:, 512:1024], in0=t0[:, 512:1024],
                                               in1=p[3][:, :], op=ALU.mult)
                    B.op("dve", tt1, reads=[t0B, pB[2], pB[3]], writes=[t1B])
                    B.op("dve", lambda e, hb=hb, pos=pos, ws_=wslot: e.tensor_tensor(
                        out=hb[:, pos * 1024:(pos + 1) * 1024], in0=t1[:, :],
                        in1=wful[ws_][:, :], op=ALU.mult),
                        reads=[t1B, wfulB[wslot]], partial=[hbB_[half_b]])
                if pos == max(NCHL // 2, 1) - 1:
                    width = (pos + 1) * 1024
                    base = h * NH + half_b * (NH // 2)
                    dst = (sa_scr[cs] if shared else hp_scr[ft])
                    scrb = sasB if shared else hpsB
                    B.op("sp", lambda e, hb=hb, width=width, base=base, dst=dst:
                         e.dma_start(out=dst[:, base:base + width],
                                     in_=hb[:, 0:width]),
                         inc=(ch_hb[half_b], 16), reads=[hbB_[half_b]], partial=[scrb])

    emit_router(0)
    emit_main_half(0, part="shared")
    emit_router_wtr(0)
    emit_router(1)
    emit_main_half(1, part="shared")
    emit_router_wtr(1)
    emit_main_half(0)

    def emit_dw_load(part=None, nparts=8):
        regs = [x1TrB[0]] + ([x1TrB[1]] if E * D > HREG else [])
        if part is None:
            B.op("pool", lambda e: e.dma_start(
                out=a1[:, 0:E * D].rearrange("p (c d) -> p c d", c=E), in_=dw_in),
                inc=(ch_dw, 16), writes=regs)
            return
        ecs = E // nparts
        B.op("pool", lambda e, part=part, ecs=ecs: e.dma_start(
            out=a1[:, part * ecs * D:(part + 1) * ecs * D].rearrange(
                "p (c d) -> p c d", c=ecs),
            in_=dw_in[:, part * ecs:(part + 1) * ecs, :]),
            inc=(ch_dw, 16),
            deps=[t for b_ in regs for t in (b_.w + b_.r)],
            partial=[regs[0]])

    if E * D <= HREG:
        _dwp = [0]

        def emit_dw_part():
            if _dwp[0] < 8:
                emit_dw_load(part=_dwp[0])
                _dwp[0] += 1
        emit_main_half(1, hook=emit_dw_part)
        while _dwp[0] < 8:
            emit_dw_load(part=_dwp[0])
            _dwp[0] += 1
    else:
        emit_main_half(1)
        emit_dw_load()
    dwbB = x1TrB[0]

    guard4 = B.all_toks()
    HPB2 = max(HREG, E * D)
    sdwbB = _Buf(guard4)
    hpbB = [_Buf(x1TrB[1].w + x1TrB[1].r), _Buf(x1TrB[1].w + x1TrB[1].r)]
    sabB = [_Buf(guard4), _Buf(guard4)]
    xrCB = [_Buf(guard4), _Buf(guard4)]
    outBB = [_Buf(guard4), _Buf(guard4)]
    t2B = _Buf(guard4)
    bnbB2, bnaB2, rstdB2 = _Buf(), _Buf(), _Buf()

    sdwb = bfv(0, 2048)
    sab = [bfv(2048, 3072), bfv(3072, 4096)]
    xrC = [f32v(4096, 5120), f32v(5120, 6144)]
    outb = [f32v(6144, 7168), f32v(7168, 8192)]
    t2 = f32v(8192, 9216)

    B.op("pool", lambda e: e.dma_start(
        out=sdwb.rearrange("p (c d) -> p c d", c=4), in_=sdw_in),
        inc=(ch_sdw, 16), deps=guard4, writes=[sdwbB])

    def load_c2(nc_):
        n0 = nc_ * 512
        slot = nc_ % 2
        hpbase = HPB2 + slot * (E * 512)
        B.op("pool", lambda e, hpbase=hpbase, n0=n0: e.dma_start(
            out=a1[:, hpbase: hpbase + E * 512].rearrange("p (c n) -> p c n", c=E),
            in_=hp_scr[:, :, n0:n0 + 512].rearrange("c p n -> p c n")),
            inc=(ch_hp[slot], 16), reads=[hpsB], writes=[hpbB[slot]])
        B.op("pool", lambda e, s=slot, n0=n0: e.dma_start(
            out=sab[s].rearrange("p (c n) -> p c n", c=4),
            in_=sa_scr[:, :, n0:n0 + 512].rearrange("c p n -> p c n")),
            inc=(ch_sab[slot], 16), reads=[sasB], writes=[sabB[slot]])

    load_c2(0)
    for nc_ in range(N // 512):
        slot = nc_ % 2
        hpbase = HPB2 + slot * (E * 512)
        if nc_ + 1 < N // 512:
            load_c2(nc_ + 1)
        for j in range(4):
            nt = nc_ * 4 + j
            b0, b1 = (0, 1) if nt % 2 == 0 else (2, 3)
            B.op("sp", lambda e, s=nt % 2, nt=nt: e.dma_start(
                out=xrC[s][:, :], in_=x1_scr[nt * 128:(nt + 1) * 128, :]),
                inc=(ch_xrc[nt % 2], 16), reads=[x1sB], writes=[xrCB[nt % 2]])

            def dmm(e, hpbase=hpbase, j=j, b0=b0, b1=b1, s=slot):
                inst = None
                for c in range(E):
                    st = a1[:, hpbase + c * 512 + j * 128: hpbase + c * 512 + (j + 1) * 128]
                    inst = e.matmul(p[b0][:, :], st, a1[:, c * D: c * D + 512],
                                    start=(c == 0), stop=False)
                    inst = e.matmul(p[b1][:, :], st, a1[:, c * D + 512: (c + 1) * D],
                                    start=(c == 0), stop=False)
                for cs in range(4):
                    st = sab[s][:, cs * 512 + j * 128: cs * 512 + (j + 1) * 128]
                    inst = e.matmul(p[b0][:, :], st,
                                    sdwb[:, cs * D: cs * D + 512],
                                    start=False, stop=(cs == 3))
                    inst = e.matmul(p[b1][:, :], st,
                                    sdwb[:, cs * D + 512: (cs + 1) * D],
                                    start=False, stop=(cs == 3))
                return inst
            B.op("pe", dmm, reads=[hpbB[slot], dwbB, sabB[slot], sdwbB],
                 writes=[pB[b0], pB[b1]])

            def res2(e, b0=b0, b1=b1, s=nt % 2):
                e.tensor_tensor(out=t2[:, 0:512], in0=p[b0][:, :],
                                in1=xrC[s][:, 0:512], op=ALU.add)
                return e.tensor_tensor(out=t2[:, 512:1024], in0=p[b1][:, :],
                                       in1=xrC[s][:, 512:1024], op=ALU.add)
            B.op("dve", res2, reads=[pB[b0], pB[b1], xrCB[nt % 2]], writes=[t2B])

            def bns2(e):
                e.bn_stats(bnb[:, 0:6], t2[:, 0:512])
                return e.bn_stats(bnb[:, 6:12], t2[:, 512:1024])
            B.op("dve", bns2, reads=[t2B], writes=[bnbB2])
            B.op("dve", lambda e: e.bn_aggr(bna[:, :], bnb[:, :]),
                 reads=[bnbB2], writes=[bnaB2])
            B.op("dve", lambda e: e.tensor_scalar(
                out=rstd[:, :], in0=bna[:, 1:2], scalar1=EPS, scalar2=None,
                op0=ALU.add), reads=[bnaB2], writes=[rstdB2])
            B.op("act", lambda e: e.activation(
                out=rstd[:, :], in_=rstd[:, :], func=AF.Ln),
                reads=[rstdB2], writes=[rstdB2])
            B.op("act", lambda e: e.activation(
                out=rstd[:, :], in_=rstd[:, :], func=AF.Exp, scale=-0.5),
                reads=[rstdB2], writes=[rstdB2])
            B.op("dve", lambda e, s=nt % 2: e.tensor_scalar(
                out=outb[s][:, :], in0=t2[:, :], scalar1=bna[:, 0:1],
                scalar2=rstd[:, :], op0=ALU.subtract, op1=ALU.mult),
                reads=[t2B, bnaB2, rstdB2], writes=[outBB[nt % 2]])
            out_toks.append(B.op("sp", lambda e, s=nt % 2, nt=nt: e.dma_start(
                out=y_out[nt * 128:(nt + 1) * 128, :], in_=outb[s][:, :]),
                inc=(ch_out[nt % 2], 16), reads=[outBB[nt % 2]]))

    B.op("sp", lambda e: e.nop(), deps=out_toks)
    B.replay(nc, es)
    es.close()
    return nc


def _prep_weights(ins):
    f = lambda a: np.ascontiguousarray(np.asarray(a, dtype=np.float32))
    WQ = f(ins["L0_attn_WQ"])
    WK = f(ins["L0_attn_WK"])
    WV = f(ins["L0_attn_WV"])
    per_k = lambda w: np.ascontiguousarray(
        w.reshape(KD, 128, w.shape[1]).transpose(1, 0, 2))
    gate = f(ins["L0_ffn_gate_w"])
    up = f(ins["L0_ffn_up_w"])
    down = f(ins["L0_ffn_down_w"])
    sg = f(ins["L0_ffn_sgate_w"])
    su = f(ins["L0_ffn_sup_w"])
    sd = f(ins["L0_ffn_sdown_w"])
    rwm = f(ins["L0_ffn_router_w"])
    gT = gate.transpose(2, 0, 1).reshape(D, E * 128)
    uT = up.transpose(2, 0, 1).reshape(D, E * 128)
    dT = down.transpose(0, 2, 1).reshape(E * 128, D)
    sgT = sg.T
    suT = su.T
    sdT = sd.T
    rT = rwm.T
    out = {
        "wq": per_k(WQ), "wk": per_k(WK), "wv": per_k(WV),
        "gw": per_k(gT), "uw": per_k(uT),
        "dw": np.ascontiguousarray(dT.reshape(E, 128, D).transpose(1, 0, 2)),
        "sgw": per_k(sgT), "suw": per_k(suT),
        "sdw": np.ascontiguousarray(sdT.reshape(4, 128, D).transpose(1, 0, 2)),
        "rw": per_k(rT),
        "rb": f(ins["L0_ffn_router_bias"]).reshape(1, E),
        "idn": np.eye(128, dtype=np.float32),
        "cm": np.triu(np.full((128, 128), NEG, dtype=np.float32), k=1),
    }
    return out


def kernel(**inputs):
    S = 4096
    x = np.ascontiguousarray(np.asarray(inputs["L0_attn_x"], dtype=np.float32))
    Bfull = x.shape[0]
    ncores = 8
    per = Bfull // ncores
    w = _prep_weights(inputs)
    nc = build_program(S)
    in_maps = []
    for c in range(ncores):
        xs = x[c * per:(c + 1) * per]
        xTs = np.ascontiguousarray(
            xs.reshape(per, S, KD, 128).transpose(0, 2, 3, 1))
        m = dict(w)
        m["x"] = xs
        m["xT"] = xTs
        in_maps.append(m)
    res = run_bass_kernel_spmd(nc, in_maps, core_ids=list(range(ncores)))
    y = np.concatenate(
        [np.asarray(r["y"], dtype=np.float32).reshape(per, S, D)
         for r in res.results], axis=0)
    return y


# version 19
# speedup vs baseline: 1.0216x; 1.0048x over previous
import numpy as np
import concourse.bass as bass
import concourse.mybir as mybir
from contextlib import ExitStack
from concourse.bass_utils import run_bass_kernel_spmd

F32 = mybir.dt.float32
BF16 = mybir.dt.bfloat16
AF = mybir.ActivationFunctionType
ALU = mybir.AluOpType
AX = mybir.AxisListType

D = 1024
KD = 8
E = 32
FS = 512
NB = 2
EPS = 1e-5
NEG = -1.0e30

ENGINES = ("pe", "dve", "act", "pool", "sp")


class _Chan:
    __slots__ = ("name", "h", "n")

    def __init__(self, name):
        self.name = name
        self.h = None
        self.n = 0


class _Op:
    __slots__ = ("eng", "deps", "fn", "chan", "amt")

    def __init__(self, eng, deps, fn, chan, amt):
        self.eng = eng
        self.deps = deps
        self.fn = fn
        self.chan = chan
        self.amt = amt


class _Buf:
    __slots__ = ("w", "r")

    def __init__(self, guard=()):
        self.w = list(guard)
        self.r = []


def _dedup(toks):
    best = {}
    for (c, v) in toks:
        if v > best.get(c, 0):
            best[c] = v
    return tuple(best.items())


class _Builder:
    def __init__(self):
        self.ops = []
        self.chans = []
        self.ech = {e: self.chan("c_" + e) for e in ENGINES}

    def chan(self, name):
        c = _Chan(name)
        self.chans.append(c)
        return c

    def op(self, eng, fn, deps=(), inc=None, reads=(), writes=(), partial=()):
        d = list(deps)
        for b in reads:
            d += b.w
        for b in writes:
            d += b.w + b.r
        for b in partial:
            d += b.r
        if inc is None:
            cc, amt = self.ech[eng], 1
        else:
            cc, amt = inc
        cc.n += amt
        tok = (cc, cc.n)
        self.ops.append(_Op(eng, _dedup(d), fn, cc, amt))
        for b in reads:
            b.r.append(tok)
        for b in writes:
            b.w = [tok]
            b.r = []
        for b in partial:
            b.w.append(tok)
        return tok

    def all_toks(self):
        return [(c, c.n) for c in self.chans if c.n > 0]

    def replay(self, nc, exit_stack):
        for c in self.chans:
            c.h = exit_stack.enter_context(nc.semaphore(c.name))
        hwm = {}
        per_eng = {e: [] for e in ENGINES}
        for o in self.ops:
            per_eng[o.eng].append(o)
        block = exit_stack.enter_context(nc.Block())
        engmap = {"pe": block.tensor, "dve": block.vector, "act": block.scalar,
                  "pool": block.gpsimd, "sp": block.sync}

        def make_body(ename, ops):
            def body(e):
                for o in ops:
                    for (cc, v) in o.deps:
                        if v > 0 and hwm.get((ename, cc), 0) < v:
                            e.wait_ge(cc.h, v)
                            hwm[(ename, cc)] = v
                    o.fn(e).then_inc(o.chan.h, o.amt)
            return body

        for ename in ENGINES:
            if per_eng[ename]:
                engmap[ename](make_body(ename, per_eng[ename]))


def _gp(first, *bufs):
    return (bufs, ()) if first else ((), bufs)


DBG = 0


def build_program(S):
    F = E * 128
    N = NB * S
    NT = N // 128
    TPB = S // 128

    nc = bass.Bass("TRN2", target_bir_lowering=False, debug=False)
    es = ExitStack()

    dr = lambda name, shape, dt=F32, kind="ExternalInput": \
        nc.dram_tensor(name, list(shape), dt, kind=kind).ap()
    x_in = dr("x", (NB, S, D))
    xT_in = dr("xT", (NB, KD, 128, S))
    wq_in = dr("wq", (128, KD, D))
    wk_in = dr("wk", (128, KD, D))
    wv_in = dr("wv", (128, KD, D))
    gw_in = dr("gw", (128, KD, F))
    uw_in = dr("uw", (128, KD, F))
    dw_in = dr("dw", (128, E, D))
    sgw_in = dr("sgw", (128, KD, FS))
    suw_in = dr("suw", (128, KD, FS))
    sdw_in = dr("sdw", (128, 4, D))
    rw_in = dr("rw", (128, KD, E))
    rb_in = dr("rb", (1, E))
    idn_in = dr("idn", (128, 128))
    cm_in = dr("cm", (128, 128))
    y_out = dr("y", (N, D), kind="ExternalOutput")
    if DBG:
        dbg_xatt = dr("dbg_xatt", (N, D), kind="ExternalOutput")
        dbg_li = dr("dbg_li", (NT, 128, 1), kind="ExternalOutput")

    x1_scr = dr("x1s", (N, D), kind="Internal")
    x1t_scr = dr("x1ts", (KD, 128, N), BF16, kind="Internal")
    hp_scr = dr("hps", (E, 128, N), BF16, kind="Internal")
    sa_scr = dr("sas", (4, 128, N), BF16, kind="Internal")
    w_scr = dr("ws", (E, N), kind="Internal")

    sb = lambda name, shape, dt=F32: es.enter_context(
        nc.sbuf_tensor(name, list(shape), dt))

    A1_COLS = max(8 * S + 8 * S, E * D + 2 * E * 512)
    arena1 = sb("arena1", (128, A1_COLS // 2), F32)
    a1 = arena1.bitcast(BF16)
    VB = 8 * S
    HPB = E * D

    A2_COLS = 18944
    arena2 = sb("arena2", (128, A2_COLS), F32)
    a2f = arena2
    a2 = arena2.bitcast(BF16)

    def bfv(c0f, c1f):
        return a2[:, 2 * c0f: 2 * c1f]

    def f32v(c0f, c1f):
        return a2f[:, c0f:c1f]

    idb = sb("idb", (128, 128), BF16)
    cmf = sb("cmf", (128, 128), F32)
    rbb = sb("rbb", (128, E), F32)
    rwb = sb("rwb", (128, KD * E), BF16)
    mxs = sb("mxs", (128, 16), F32)
    nmx = sb("nmx", (128, 16), F32)
    sums = sb("sums", (128, 16), F32)
    al = sb("al", (128, 16), F32)
    al2 = sb("al2", (128, 16), F32)
    gm = sb("gm", (128, 2), F32)
    gmn = sb("gmn", (128, 2), F32)
    lv = sb("lv", (128, 2), F32)
    linv = sb("linv", (128, 2), F32)
    bnb = sb("bnb", (128, 12), F32)
    bna = sb("bna", (128, 2), F32)
    rstd = sb("rstd", (128, 1), F32)
    lt8 = sb("lt8", (128, 8 * E), F32)
    et8 = sb("et8", (128, 8 * E), F32)
    ms8 = sb("ms8", (128, 8 * E), F32)
    mk8 = sb("mk8", (128, 8), F32)
    m18 = sb("m18", (128, 8), F32)
    ri8 = sb("ri8", (128, 8), F32)

    ppair = [es.enter_context(nc.psum_tensor(f"pp{i}", [128, 1024], F32))
             for i in range(3)]
    p = [ppair[i // 2][:, (i % 2) * 512:(i % 2) * 512 + 512] for i in range(6)]
    p6 = es.enter_context(nc.psum_tensor("p6", [128, 1024], BF16))
    p7 = es.enter_context(nc.psum_tensor("p7", [128, 1024], BF16))
    ptr = [p6, p7]

    B = _Builder()
    ch = B.chan

    idbB, cmfB, rbbB, rwbB = _Buf(), _Buf(), _Buf(), _Buf()
    B.op("pool", lambda e: e.dma_start(out=idb[:, :], in_=idn_in),
         inc=(ch("d_idb"), 16), writes=[idbB])
    B.op("pool", lambda e: e.dma_start(
        out=rwb[:, :].rearrange("p (k e) -> p k e", k=KD), in_=rw_in),
        inc=(ch("d_rwb"), 16), writes=[rwbB])
    B.op("sp", lambda e: e.dma_start(out=cmf[:, :], in_=cm_in),
         inc=(ch("d_cmf"), 16), writes=[cmfB])
    B.op("sp", lambda e: e.dma_start(out=rbb[:, :],
                                     in_=rb_in.partition_broadcast(128)),
         inc=(ch("d_rbb"), 16), writes=[rbbB])

    ch_wk, ch_wv, ch_wq = ch("d_wk"), ch("d_wv"), ch("d_wq")
    ch_xtq = [ch("d_xtq0"), ch("d_xtq1"), ch("d_xtq2")]
    ch_xtqB2 = [ch("d_xtqB0"), ch("d_xtqB1")]
    ch_xr = [ch("d_xr0"), ch("d_xr1")]
    ch_x1w, ch_x1tw, ch_ws = ch("d_x1w"), ch("d_x1tw"), ch("d_ws")
    ch_x1tr = [[ch(f"d_x1tr{h}p{pz}") for pz in range(4)] for h in range(2)]
    ch_gw = [ch("d_gw0"), ch("d_gw1")]
    ch_uw = [ch("d_uw0"), ch("d_uw1")]
    ch_wf = [ch("d_wf0"), ch("d_wf1")]
    ch_hb = [ch("d_hb0"), ch("d_hb1")]
    ch_dw, ch_sdw = ch("d_dw"), ch("d_sdw")
    ch_hp = [ch("d_hp0"), ch("d_hp1")]
    ch_sab = [ch("d_sab0"), ch("d_sab1")]
    ch_xrc = [ch("d_xrc0"), ch("d_xrc1")]
    ch_out = [ch("d_out0"), ch("d_out1")]

    dbg_toks = []
    ch_dbg, ch_dbg2 = ch("d_dbg"), ch("d_dbg2")
    pB = [_Buf() for _ in range(6)]
    p6B = _Buf()
    p7B = _Buf()
    ptrB = [p6B, p7B]
    x1sB, x1tsB, wscrB = _Buf(), _Buf(), _Buf()
    hpsB = _Buf()
    sasB = _Buf()
    out_toks = []

    prev_wqbB = None
    for b in range(NB):
        guard = B.all_toks()
        if prev_wqbB is not None:
            wkbB, wvbB = _Buf(prev_wqbB.w + prev_wqbB.r), _Buf(guard)
        else:
            wkbB, wvbB = _Buf(guard), _Buf(guard)
        xtqB_ = [_Buf(guard), _Buf(guard), _Buf(guard)]
        KTB = [_Buf(guard) for _ in range(KD)]
        VBuf = _Buf(guard)

        wkb = bfv(0, 4096)
        wvb = bfv(4096, 8192)
        xtqA = [bfv(8192, 10240), bfv(10240, 12288), bfv(12288, 14336)]

        B.op("pool", lambda e: e.dma_start(
            out=wkb.rearrange("p (k d) -> p k d", k=KD), in_=wk_in),
            inc=(ch_wk, 16), writes=[wkbB])
        B.op("pool", lambda e: e.dma_start(
            out=wvb.rearrange("p (k d) -> p k d", k=KD), in_=wv_in),
            inc=(ch_wv, 16), writes=[wvbB])

        def xtq_sl(slot, k, lo, hi):
            return xtqA[slot][:, k * 512 + lo: k * 512 + hi]

        def load_xtq(cidx):
            slot = cidx % 3
            B.op("pool", lambda e, s=slot, c=cidx, b=b: e.dma_start(
                out=xtqA[s].rearrange("p (k t) -> p k t", k=KD),
                in_=xT_in[b][:, :, c * 512:(c + 1) * 512].rearrange("k p t -> p k t")),
                inc=(ch_xtq[slot], 16), writes=[xtqB_[slot]])

        nchunks = S // 512
        load_xtq(0)
        load_xtq(1)
        for i in range(nchunks // 2):
            s0, s1 = (2 * i) % 3, (2 * i + 1) % 3
            if 2 * i + 2 < nchunks:
                load_xtq(2 * i + 2)
            for m in range(KD):
                bA, bBk = (0, 1) if m % 2 == 0 else (2, 3)

                def kmm(e, m=m, bA=bA, bBk=bBk, s0=s0, s1=s1):
                    inst = None
                    for k in range(KD):
                        st = wkb[:, k * D + m * 128: k * D + (m + 1) * 128]
                        inst = e.matmul(p[bA][:, :], st, xtq_sl(s0, k, 0, 512),
                                        start=(k == 0), stop=(k == KD - 1))
                        inst = e.matmul(p[bBk][:, :], st, xtq_sl(s1, k, 0, 512),
                                        start=(k == 0), stop=(k == KD - 1))
                    return inst
                B.op("pe", kmm, reads=[wkbB, xtqB_[s0], xtqB_[s1]],
                     writes=[pB[bA], pB[bBk]])
                for (bk, half) in ((bA, 0), (bBk, 1)):
                    c0 = (2 * i + half) * 512
                    B.op("dve", lambda e, bk=bk, m=m, c0=c0: e.tensor_copy(
                        a1[:, m * S + c0: m * S + c0 + 512], p[bk][:, :]),
                        reads=[pB[bk]], partial=[KTB[m]])
            for ci in (2 * i, 2 * i + 1):
                slot = ci % 3
                if ci == 2 * i + 1 and 2 * i + 3 < nchunks:
                    load_xtq(2 * i + 3)
                for j in range(4):
                    tt = ci * 4 + j
                    b0, b1 = (4, 5) if tt % 2 == 0 else (0, 1)

                    def vmm(e, slot=slot, j=j, b0=b0, b1=b1):
                        inst = None
                        for k in range(KD):
                            st = xtq_sl(slot, k, j * 128, (j + 1) * 128)
                            inst = e.matmul(p[b0][:, :], st,
                                            wvb[:, k * D: k * D + 512],
                                            start=(k == 0), stop=(k == KD - 1))
                            inst = e.matmul(p[b1][:, :], st,
                                            wvb[:, k * D + 512: (k + 1) * D],
                                            start=(k == 0), stop=(k == KD - 1))
                        return inst
                    B.op("pe", vmm, reads=[wvbB, xtqB_[slot]],
                         writes=[pB[b0], pB[b1]])
                    for (bk, off) in ((b0, 0), (b1, 512)):
                        B.op("dve", lambda e, bk=bk, tt=tt, off=off: e.tensor_copy(
                            a1[:, VB + tt * 1024 + off: VB + tt * 1024 + off + 512],
                            p[bk][:, :]), reads=[pB[bk]], partial=[VBuf])

        guard2 = B.all_toks()
        wqbB = _Buf(guard2)
        xtqBB2 = [_Buf(guard2), _Buf(guard2)]
        QTB_ = _Buf(guard2)
        PB_ = _Buf(guard2)
        PTB_ = _Buf(guard2)
        xresB = [_Buf(guard2), _Buf(guard2)]
        x1fB, x1cB, x1TB, xtqBB = (_Buf(guard2), _Buf(guard2), _Buf(guard2),
                                   _Buf(guard2))
        mxsB = [_Buf(), _Buf()]
        nmxB = [_Buf(), _Buf()]
        sumsB = [_Buf(), _Buf()]
        alB = [_Buf(), _Buf()]
        al2B = [_Buf(), _Buf()]
        gmB = [_Buf(), _Buf()]
        gmnB = [_Buf(), _Buf()]
        lvB = [_Buf(), _Buf()]
        linvB = [_Buf(), _Buf()]
        PBs = [PB_, _Buf(guard2)]
        bnbB, bnaB, rstdB = _Buf(), _Buf(), _Buf()

        wqb = bfv(0, 4096)
        QT = bfv(4096, 6144)
        Pbs = [bfv(6144, 8192), bfv(16896, 18944)]
        PT = bfv(8192, 8704)
        xres = [f32v(8704, 9728), f32v(9728, 10752)]
        x1tF = f32v(10752, 11776)
        x1tC = bfv(11776, 12288)
        x1TC = bfv(12288, 12800)
        xtqBs = [bfv(12800, 14848), bfv(14848, 16896)]

        B.op("pool", lambda e: e.dma_start(
            out=wqb.rearrange("p (k d) -> p k d", k=KD), in_=wq_in),
            inc=(ch_wq, 16), writes=[wqbB])
        prev_wqbB = wqbB

        pend_x1 = [None]

        def flush_x1():
            if pend_x1[0] is not None:
                pend_x1[0]()
                pend_x1[0] = None

        def load_xtqB(g):
            sl = g % 2
            B.op("pool", lambda e, g=g, b=b, sl=sl: e.dma_start(
                out=xtqBs[sl].rearrange("p (k t) -> p k t", k=KD),
                in_=xT_in[b][:, :, g * 512:(g + 1) * 512].rearrange("k p t -> p k t")),
                inc=(ch_xtqB2[sl], 16), writes=[xtqBB2[sl]])

        def emit_Q(g):
            sl = g % 2
            for m in range(KD):
                bk = m % 4

                def qmm(e, m=m, bk=bk, sl=sl):
                    inst = None
                    for k in range(KD):
                        st = wqb[:, k * D + m * 128: k * D + (m + 1) * 128]
                        inst = e.matmul(p[bk][:, :], st,
                                        xtqBs[sl][:, k * 512:(k + 1) * 512],
                                        start=(k == 0), stop=(k == KD - 1))
                    return inst
                B.op("pe", qmm, reads=[wqbB, xtqBB2[sl]], writes=[pB[bk]])
                w_, pt_ = _gp(m == 0, QTB_)
                B.op("act", lambda e, m=m, bk=bk: e.activation(
                    out=QT[:, m * 512:(m + 1) * 512], in_=p[bk][:, :],
                    func=AF.Copy, scale=1.0 / 32.0),
                    reads=[pB[bk]], writes=w_, partial=pt_)

        def emit_S(T):
            q = T % 2
            j = T % 4
            t_len = 128 * (T + 1)
            npc = (t_len + 511) // 512
            Pq = Pbs[q]
            B.op("sp", lambda e, T=T, b=b: e.dma_start(
                out=xres[q], in_=x_in[b, T * 128:(T + 1) * 128, :]),
                inc=(ch_xr[q], 16), writes=[xresB[q]])
            npc = (t_len + 1023) // 1024
            for c in range(npc):
                w_c = min(1024, t_len - 1024 * c)
                pr = c % 2
                bk0, bk1 = (0, 1) if pr == 0 else (2, 3)
                nsub = (w_c + 511) // 512

                def smm(e, j=j, c=c, w_c=w_c, pr=pr, nsub=nsub):
                    inst = None
                    for k in range(KD):
                        st = QT[:, k * 512 + j * 128: k * 512 + (j + 1) * 128]
                        for sbi in range(nsub):
                            wseg = min(512, w_c - 512 * sbi)
                            inst = e.matmul(
                                ppair[pr][:, 512 * sbi: 512 * sbi + wseg], st,
                                a1[:, k * S + 1024 * c + 512 * sbi:
                                   k * S + 1024 * c + 512 * sbi + wseg],
                                start=(k == 0), stop=(k == KD - 1))
                    return inst
                B.op("pe", smm, reads=[QTB_] + KTB, writes=[pB[bk0], pB[bk1]])
                if c == npc - 1:
                    B.op("dve", lambda e, pr=pr, w_c=w_c: e.tensor_tensor(
                        out=ppair[pr][:, w_c - 128:w_c],
                        in0=ppair[pr][:, w_c - 128:w_c],
                        in1=cmf[:, :], op=ALU.add),
                        reads=[cmfB], writes=[pB[bk0], pB[bk1]])
                w_, pt_ = _gp(c == 0, mxsB[q])
                B.op("dve", lambda e, pr=pr, c=c, w_c=w_c, q=q: e.reduce_max(
                    mxs[:, q * 8 + c:q * 8 + c + 1], ppair[pr][:, 0:w_c],
                    axis=AX.X),
                    reads=[pB[bk0], pB[bk1]], writes=w_, partial=pt_)
                w_, pt_ = _gp(c == 0, nmxB[q])
                B.op("dve", lambda e, c=c, q=q: e.tensor_scalar(
                    out=nmx[:, q * 8 + c:q * 8 + c + 1],
                    in0=mxs[:, q * 8 + c:q * 8 + c + 1], scalar1=-1.0,
                    scalar2=None, op0=ALU.mult),
                    reads=[mxsB[q]], writes=w_, partial=pt_)
                w_, pt_ = _gp(c == 0, PBs[q], sumsB[q])
                B.op("act", lambda e, pr=pr, c=c, w_c=w_c, q=q, Pq=Pq: e.activation(
                    out=Pq[:, 1024 * c: 1024 * c + w_c], in_=ppair[pr][:, 0:w_c],
                    func=AF.Exp, bias=nmx[:, q * 8 + c:q * 8 + c + 1], scale=1.0,
                    accum_out=sums[:, q * 8 + c:q * 8 + c + 1]),
                    reads=[pB[bk0], pB[bk1], nmxB[q]], writes=w_, partial=pt_)

        def emit_C(T):
            q = T % 2
            t_len = 128 * (T + 1)
            npc = (t_len + 1023) // 1024
            Pq = Pbs[q]
            if npc > 1:
                B.op("dve", lambda e, npc=npc, q=q: e.reduce_max(
                    gm[:, q:q + 1], mxs[:, q * 8:q * 8 + npc], axis=AX.X),
                    reads=[mxsB[q]], writes=[gmB[q]])
                B.op("dve", lambda e, q=q: e.tensor_scalar(
                    out=gmn[:, q:q + 1], in0=gm[:, q:q + 1], scalar1=-1.0,
                    scalar2=None, op0=ALU.mult), reads=[gmB[q]], writes=[gmnB[q]])
                B.op("act", lambda e, npc=npc, q=q: e.activation(
                    out=al[:, q * 8:q * 8 + npc], in_=mxs[:, q * 8:q * 8 + npc],
                    func=AF.Exp, bias=gmn[:, q:q + 1], scale=1.0),
                    reads=[mxsB[q], gmnB[q]], writes=[alB[q]])
                B.op("dve", lambda e, npc=npc, q=q: e.tensor_tensor(
                    out=al2[:, q * 8:q * 8 + npc], in0=al[:, q * 8:q * 8 + npc],
                    in1=sums[:, q * 8:q * 8 + npc],
                    op=ALU.mult), reads=[alB[q], sumsB[q]], writes=[al2B[q]])
                B.op("dve", lambda e, npc=npc, q=q: e.reduce_sum(
                    lv[:, q:q + 1], al2[:, q * 8:q * 8 + npc], axis=AX.X),
                    reads=[al2B[q]], writes=[lvB[q]])
                B.op("dve", lambda e, q=q: e.reciprocal(
                    linv[:, q:q + 1], lv[:, q:q + 1]),
                    reads=[lvB[q]], writes=[linvB[q]])
                for c in range(npc):
                    w_c = min(1024, t_len - 1024 * c)
                    B.op("dve", lambda e, c=c, w_c=w_c, q=q, Pq=Pq: e.tensor_scalar(
                        out=Pq[:, 1024 * c:1024 * c + w_c],
                        in0=Pq[:, 1024 * c:1024 * c + w_c],
                        scalar1=al[:, q * 8 + c:q * 8 + c + 1], scalar2=None,
                        op0=ALU.mult),
                        reads=[alB[q], PBs[q]], partial=[PBs[q]])
            else:
                B.op("dve", lambda e, q=q: e.reciprocal(
                    linv[:, q:q + 1], sums[:, q * 8:q * 8 + 1]),
                    reads=[sumsB[q]], writes=[linvB[q]])

        def make_TA(T):
            q = T % 2
            t_len = 128 * (T + 1)
            Pq = Pbs[q]

            def emit():
                flush_x1()
                nblk = t_len // 128
                ngrp = (nblk + 7) // 8

                def emit_tr(grp):
                    blocks = list(range(grp * 8, min(grp * 8 + 8, nblk)))
                    bq = grp % 2

                    def tr(e, blocks=blocks, bq=bq):
                        inst = None
                        for tt in blocks:
                            inst = e.transpose(
                                ptr[bq][:, (tt % 8) * 128:(tt % 8 + 1) * 128],
                                Pq[:, tt * 128:(tt + 1) * 128], idb[:, :])
                        return inst
                    B.op("pe", tr, reads=[PBs[q], idbB], writes=[ptrB[bq]])

                def emit_ev(grp):
                    blocks = list(range(grp * 8, min(grp * 8 + 8, nblk)))
                    bq = grp % 2
                    nb_ = len(blocks)
                    B.op("act", lambda e, nb_=nb_, bq=bq: e.copy(
                        out=PT[:, 0:nb_ * 128], in_=ptr[bq][:, 0:nb_ * 128]),
                        reads=[ptrB[bq]], writes=[PTB_])

                def emit_av(grp):
                    blocks = list(range(grp * 8, min(grp * 8 + 8, nblk)))

                    def av(e, blocks=blocks, first=(grp == 0), nblk=nblk):
                        inst = None
                        for ii, tt in enumerate(blocks):
                            st = PT[:, ii * 128:(ii + 1) * 128]
                            fi = first and ii == 0
                            la = tt == nblk - 1
                            inst = e.matmul(
                                p[4][:, :], st,
                                a1[:, VB + tt * 1024: VB + tt * 1024 + 512],
                                start=fi, stop=la)
                            inst = e.matmul(
                                p[5][:, :], st,
                                a1[:, VB + tt * 1024 + 512: VB + (tt + 1) * 1024],
                                start=fi, stop=la)
                        return inst
                    w_, pt_ = _gp(grp == 0, pB[4], pB[5])
                    B.op("pe", av, reads=[PTB_, VBuf], writes=w_, partial=pt_)

                emit_tr(0)
                emit_ev(0)
                for grp in range(1, ngrp):
                    emit_tr(grp)
                    emit_av(grp - 1)
                    emit_ev(grp)
                emit_av(ngrp - 1)

                def res_add(e, q=q):
                    e.scalar_tensor_tensor(
                        out=x1tF[:, 0:512], in0=p[4][:, :],
                        scalar=linv[:, q:q + 1],
                        in1=xres[q][:, 0:512], op0=ALU.mult, op1=ALU.add)
                    return e.scalar_tensor_tensor(
                        out=x1tF[:, 512:1024], in0=p[5][:, :],
                        scalar=linv[:, q:q + 1],
                        in1=xres[q][:, 512:1024], op0=ALU.mult, op1=ALU.add)
                B.op("dve", res_add, reads=[pB[4], pB[5], linvB[q], xresB[q]],
                     writes=[x1fB])
                if DBG:
                    gnt = b * TPB + T
                    dbg_toks.append(B.op("sp", lambda e, gnt=gnt: e.dma_start(
                        out=dbg_xatt[gnt * 128:(gnt + 1) * 128, :], in_=x1tF[:, :]),
                        inc=(ch_dbg, 16), reads=[x1fB]))
                    dbg_toks.append(B.op("sp", lambda e, gnt=gnt, q=q: e.dma_start(
                        out=dbg_li[gnt], in_=linv[:, q:q + 1]),
                        inc=(ch_dbg2, 16), reads=[linvB[q]]))

                def bns1(e):
                    e.bn_stats(bnb[:, 0:6], x1tF[:, 0:512])
                    return e.bn_stats(bnb[:, 6:12], x1tF[:, 512:1024])
                B.op("dve", bns1, reads=[x1fB], writes=[bnbB])
                B.op("dve", lambda e: e.bn_aggr(bna[:, :], bnb[:, :]),
                     reads=[bnbB], writes=[bnaB])
                B.op("dve", lambda e: e.tensor_scalar(
                    out=rstd[:, :], in0=bna[:, 1:2], scalar1=EPS, scalar2=None,
                    op0=ALU.add), reads=[bnaB], writes=[rstdB])
                B.op("act", lambda e: e.activation(
                    out=rstd[:, :], in_=rstd[:, :], func=AF.Ln),
                    reads=[rstdB], writes=[rstdB])
                B.op("act", lambda e: e.activation(
                    out=rstd[:, :], in_=rstd[:, :], func=AF.Exp, scale=-0.5),
                    reads=[rstdB], writes=[rstdB])
                B.op("dve", lambda e: e.tensor_scalar(
                    out=x1tF[:, :], in0=x1tF[:, :], scalar1=bna[:, 0:1],
                    scalar2=rstd[:, :], op0=ALU.subtract, op1=ALU.mult),
                    reads=[bnaB, rstdB], writes=[x1fB])
                n0 = (b * TPB + T) * 128
                B.op("sp", lambda e, n0=n0: e.dma_start(
                    out=x1_scr[n0:n0 + 128, :], in_=x1tF[:, :]),
                    inc=(ch_x1w, 16), reads=[x1fB], partial=[x1sB])

                def make_pend(n0=n0):
                    def emit2():
                        B.op("pool", lambda e: e.tensor_copy(x1tC[:, :], x1tF[:, :]),
                             reads=[x1fB], writes=[x1cB])

                        def xtr(e):
                            inst = None
                            for k in range(KD):
                                inst = e.transpose(
                                    p7[:, k * 128:(k + 1) * 128],
                                    x1tC[:, k * 128:(k + 1) * 128], idb[:, :])
                            return inst
                        B.op("pe", xtr, reads=[x1cB, idbB], writes=[p7B])
                        B.op("dve", lambda e: e.tensor_copy(x1TC[:, :], p7[:, :]),
                             reads=[p7B], writes=[x1TB])
                        B.op("sp", lambda e: e.dma_start(
                            out=x1t_scr[:, :, n0:n0 + 128].rearrange(
                                "k p j -> p k j"),
                            in_=x1TC[:, :].rearrange("p (k j) -> p k j", k=KD)),
                            inc=(ch_x1tw, 16), reads=[x1TB], partial=[x1tsB])
                    return emit2
                pend_x1[0] = make_pend()
            return emit

        pend_TA = [None]

        def flush_TA():
            if pend_TA[0] is not None:
                pend_TA[0]()
                pend_TA[0] = None

        load_xtqB(0)
        for T in range(TPB):
            g = T // 4
            if T % 4 == 0:
                if g + 1 < S // 512:
                    load_xtqB(g + 1)
                emit_Q(g)
            emit_S(T)
            emit_C(T)
            flush_TA()
            pend_TA[0] = make_TA(T)
        flush_TA()
        flush_x1()

    if DBG:
        B.op("sp", lambda e: e.nop(), deps=dbg_toks + B.all_toks())
        B.replay(nc, es)
        es.close()
        return nc

    NH = N // 2
    HREG = 8 * NH
    guard3 = B.all_toks()
    x1TrB = [_Buf(guard3), _Buf(guard3)]
    lt8B, et8B, ms8B, mk8B, m18B, ri8B = (_Buf(), _Buf(), _Buf(), _Buf(),
                                          _Buf(), _Buf())
    gwcB = [_Buf(guard3), _Buf(guard3)]
    uwcB = [_Buf(guard3), _Buf(guard3)]
    hbB_ = [_Buf(guard3), _Buf(guard3)]
    wfinB = _Buf(guard3)
    wfinbB = _Buf(guard3)
    t0B, t1B = _Buf(guard3), _Buf(guard3)
    wfulB = [_Buf(guard3), _Buf(guard3)]

    gwc = [bfv(0, 512), bfv(512, 1024)]
    uwc = [bfv(1024, 1536), bfv(1536, 2048)]
    hbuf = [bfv(2048, 4096), bfv(4096, 6144)]
    wfin = f32v(6144, 8192)
    t0 = f32v(8192, 9216)
    t1 = f32v(9216, 10240)
    wful = [f32v(10240, 11264), f32v(11264, 12288)]
    wfinb = bfv(12288, 13312)

    def x1tr_sl(h, k, lo, hi):
        base = h * HREG + k * NH
        return a1[:, base + lo: base + hi]

    NPIECE = 4
    PIECE = NH // NPIECE
    x1tr_tok = {}
    for h in range(2):
        for pz in range(NPIECE):
            for k in range(KD):
                x1tr_tok[(h, k, pz)] = B.op(
                    "sp", lambda e, h=h, k=k, pz=pz: e.dma_start(
                        out=x1tr_sl(h, k, pz * PIECE, (pz + 1) * PIECE),
                        in_=x1t_scr[k][:, h * NH + pz * PIECE:
                                       h * NH + (pz + 1) * PIECE]),
                    inc=(ch_x1tr[h][pz], 16), deps=guard3, reads=[x1tsB],
                    partial=[x1TrB[h]])

    lt8_3 = lt8.reshape([128, 8, E])
    et8_3 = et8.reshape([128, 8, E])
    ms8_3 = ms8.reshape([128, 8, E])
    mk8_3 = mk8.reshape([128, 8, 1])
    m18_3 = m18.reshape([128, 8, 1])
    ri8_3 = ri8.reshape([128, 8, 1])
    NTH = NT // 2

    def emit_router(h):
        def emit_tile(nt):
            ntl = nt - h * NTH

            lb = 4 + (nt % 2)

            def lmm(e, h=h, ntl=ntl, lb=lb):
                inst = None
                for k in range(KD):
                    st = x1tr_sl(h, k, ntl * 128, (ntl + 1) * 128)
                    inst = e.matmul(p[lb][:, 0:E], st, rwb[:, k * E:(k + 1) * E],
                                    start=(k == 0), stop=(k == KD - 1))
                return inst
            B.op("pe", lmm,
                 deps=[x1tr_tok[(h, k, (ntl * 128) // PIECE)]
                       for k in range(KD)],
                 reads=[rwbB], writes=[pB[lb]])
            w_, pt_ = _gp(nt == 0, wfinB)
            B.op("dve", lambda e, nt=nt, lb=lb: e.tensor_tensor(
                out=wfin[:, nt * E:(nt + 1) * E], in0=p[lb][:, 0:E], in1=rbb[:, :],
                op=ALU.add), reads=[pB[lb], rbbB], writes=w_, partial=pt_)

        for g8 in range(h * (NTH // 8), (h + 1) * (NTH // 8)):
            for nt in range(g8 * 8, (g8 + 1) * 8):
                emit_tile(nt)
            wf_sl = wfin[:, g8 * 8 * E:(g8 + 1) * 8 * E]
            wf3 = wf_sl.rearrange("p (s e) -> p s e", s=8)
            B.op("dve", lambda e, wf_sl=wf_sl: e.tensor_copy(lt8[:, :], wf_sl),
                 reads=[wfinB], writes=[lt8B])
            for it in range(4):
                B.op("dve", lambda e: e.reduce_max(
                    mk8_3[:, :, :], lt8_3[:, :, :], axis=AX.X),
                    reads=[lt8B], writes=[mk8B])
                if it == 0:
                    B.op("dve", lambda e: e.tensor_copy(m18[:, :], mk8[:, :]),
                         reads=[mk8B], writes=[m18B])
                if it < 3:
                    B.op("dve", lambda e: e.tensor_tensor(
                        out=ms8_3[:, :, :], in0=lt8_3[:, :, :],
                        in1=mk8_3.ap().to_broadcast([128, 8, E]), op=ALU.is_ge),
                        reads=[lt8B, mk8B], writes=[ms8B])
                    B.op("dve", lambda e: e.scalar_tensor_tensor(
                        out=lt8[:, :], in0=ms8[:, :], scalar=NEG, in1=lt8[:, :],
                        op0=ALU.mult, op1=ALU.add), reads=[ms8B], writes=[lt8B])
            B.op("dve", lambda e, wf3=wf3: e.tensor_tensor(
                out=lt8_3[:, :, :], in0=wf3,
                in1=m18_3.ap().to_broadcast([128, 8, E]), op=ALU.subtract),
                reads=[wfinB, m18B], writes=[lt8B])
            B.op("act", lambda e: e.activation(
                out=et8[:, :], in_=lt8[:, :], func=AF.Exp),
                reads=[lt8B], writes=[et8B])
            B.op("dve", lambda e, wf3=wf3: e.tensor_tensor(
                out=ms8_3[:, :, :], in0=wf3,
                in1=mk8_3.ap().to_broadcast([128, 8, E]), op=ALU.is_ge),
                reads=[wfinB, mk8B], writes=[ms8B])
            B.op("dve", lambda e: e.tensor_tensor(
                out=et8[:, :], in0=et8[:, :], in1=ms8[:, :], op=ALU.mult),
                reads=[ms8B], writes=[et8B])
            B.op("dve", lambda e: e.reduce_sum(
                ri8_3[:, :, :], et8_3[:, :, :], axis=AX.X),
                reads=[et8B], writes=[ri8B])
            B.op("dve", lambda e: e.reciprocal(ri8[:, :], ri8[:, :]),
                 reads=[ri8B], writes=[ri8B])
            B.op("dve", lambda e, wf3=wf3: e.tensor_tensor(
                out=wf3, in0=et8_3[:, :, :],
                in1=ri8_3.ap().to_broadcast([128, 8, E]), op=ALU.mult),
                reads=[et8B, ri8B], partial=[wfinB])
        for g8 in range(h * (NTH // 8), (h + 1) * (NTH // 8)):
            B.op("act", lambda e, g8=g8: e.copy(
                out=wfinb[:, g8 * 8 * E:(g8 + 1) * 8 * E],
                in_=wfin[:, g8 * 8 * E:(g8 + 1) * 8 * E]),
                reads=[wfinB], partial=[wfinbB])

            def wtr(e, g8=g8):
                inst = None
                for i in range(8):
                    inst = e.transpose(p6[0:E, i * 128:(i + 1) * 128],
                                       wfinb[:, (g8 * 8 + i) * E:(g8 * 8 + i + 1) * E],
                                       idb[:, :])
                return inst
            B.op("pe", wtr, reads=[wfinbB, idbB], writes=[p6B])
            B.op("dve", lambda e: e.tensor_copy(t0[0:E, 0:1024], p6[0:E, :]),
                 reads=[p6B], writes=[t0B])
            B.op("sp", lambda e, g8=g8: e.dma_start(
                out=w_scr[:, g8 * 1024:(g8 + 1) * 1024], in_=t0[0:E, 0:1024]),
                inc=(ch_ws, 16), reads=[t0B], partial=[wscrB])

    def emit_main_half(h, hook=None):
        NCHL = NH // 1024
        for ft in range(E + 4):
            if hook is not None and ft % 4 == 2:
                hook()
            shared = ft >= E
            slot = ft % 2
            if shared:
                cs = ft - E
                B.op("pool", lambda e, s=slot, cs=cs: e.dma_start(
                    out=gwc[s].rearrange("p (k f) -> p k f", k=KD),
                    in_=sgw_in[:, :, cs * 128:(cs + 1) * 128]),
                    inc=(ch_gw[slot], 16), writes=[gwcB[slot]])
                B.op("pool", lambda e, s=slot, cs=cs: e.dma_start(
                    out=uwc[s].rearrange("p (k f) -> p k f", k=KD),
                    in_=suw_in[:, :, cs * 128:(cs + 1) * 128]),
                    inc=(ch_uw[slot], 16), writes=[uwcB[slot]])
            else:
                B.op("pool", lambda e, s=slot, ft=ft: e.dma_start(
                    out=gwc[s].rearrange("p (k f) -> p k f", k=KD),
                    in_=gw_in[:, :, ft * 128:(ft + 1) * 128]),
                    inc=(ch_gw[slot], 16), writes=[gwcB[slot]])
                B.op("pool", lambda e, s=slot, ft=ft: e.dma_start(
                    out=uwc[s].rearrange("p (k f) -> p k f", k=KD),
                    in_=uw_in[:, :, ft * 128:(ft + 1) * 128]),
                    inc=(ch_uw[slot], 16), writes=[uwcB[slot]])
            for nc_ in range(NCHL):
                n0l = nc_ * 1024
                n0g = h * NH + n0l
                wslot = nc_ % 2
                if not shared:
                    B.op("pool", lambda e, ws_=wslot, ft=ft, n0g=n0g: e.dma_start(
                        out=wful[ws_][:, :],
                        in_=w_scr[ft:ft + 1, n0g:n0g + 1024].partition_broadcast(128)),
                        inc=(ch_wf[wslot], 16), reads=[wscrB], writes=[wfulB[wslot]])

                def gmm(e, s=slot, h=h, n0l=n0l):
                    inst = None
                    for k in range(KD):
                        st = gwc[s][:, k * 128:(k + 1) * 128]
                        inst = e.matmul(p[0][:, :], st, x1tr_sl(h, k, n0l, n0l + 512),
                                        start=(k == 0), stop=(k == KD - 1))
                        inst = e.matmul(p[1][:, :], st,
                                        x1tr_sl(h, k, n0l + 512, n0l + 1024),
                                        start=(k == 0), stop=(k == KD - 1))
                    return inst
                B.op("pe", gmm, reads=[gwcB[slot], x1TrB[h]], writes=[pB[0], pB[1]])

                def umm(e, s=slot, h=h, n0l=n0l):
                    inst = None
                    for k in range(KD):
                        st = uwc[s][:, k * 128:(k + 1) * 128]
                        inst = e.matmul(p[2][:, :], st, x1tr_sl(h, k, n0l, n0l + 512),
                                        start=(k == 0), stop=(k == KD - 1))
                        inst = e.matmul(p[3][:, :], st,
                                        x1tr_sl(h, k, n0l + 512, n0l + 1024),
                                        start=(k == 0), stop=(k == KD - 1))
                    return inst
                B.op("pe", umm, reads=[uwcB[slot], x1TrB[h]], writes=[pB[2], pB[3]])

                def silu_(e):
                    e.activation(out=t0[:, 0:512], in_=p[0][:, :], func=AF.Silu)
                    return e.activation(out=t0[:, 512:1024], in_=p[1][:, :],
                                        func=AF.Silu)
                B.op("act", silu_, reads=[pB[0], pB[1]], writes=[t0B])

                half_b = nc_ // (NCHL // 2) if NCHL >= 2 else 0
                pos = nc_ % max(NCHL // 2, 1)
                hb = hbuf[half_b]
                if shared:
                    def tt1s(e, hb=hb, pos=pos):
                        e.tensor_tensor(out=hb[:, pos * 1024: pos * 1024 + 512],
                                        in0=t0[:, 0:512], in1=p[2][:, :], op=ALU.mult)
                        return e.tensor_tensor(
                            out=hb[:, pos * 1024 + 512: (pos + 1) * 1024],
                            in0=t0[:, 512:1024], in1=p[3][:, :], op=ALU.mult)
                    B.op("dve", tt1s, reads=[t0B, pB[2], pB[3]], partial=[hbB_[half_b]])
                else:
                    def tt1(e):
                        e.tensor_tensor(out=t1[:, 0:512], in0=t0[:, 0:512],
                                        in1=p[2][:, :], op=ALU.mult)
                        return e.tensor_tensor(out=t1[:, 512:1024], in0=t0[:, 512:1024],
                                               in1=p[3][:, :], op=ALU.mult)
                    B.op("dve", tt1, reads=[t0B, pB[2], pB[3]], writes=[t1B])
                    B.op("dve", lambda e, hb=hb, pos=pos, ws_=wslot: e.tensor_tensor(
                        out=hb[:, pos * 1024:(pos + 1) * 1024], in0=t1[:, :],
                        in1=wful[ws_][:, :], op=ALU.mult),
                        reads=[t1B, wfulB[wslot]], partial=[hbB_[half_b]])
                if pos == max(NCHL // 2, 1) - 1:
                    width = (pos + 1) * 1024
                    base = h * NH + half_b * (NH // 2)
                    dst = (sa_scr[cs] if shared else hp_scr[ft])
                    scrb = sasB if shared else hpsB
                    B.op("sp", lambda e, hb=hb, width=width, base=base, dst=dst:
                         e.dma_start(out=dst[:, base:base + width],
                                     in_=hb[:, 0:width]),
                         inc=(ch_hb[half_b], 16), reads=[hbB_[half_b]], partial=[scrb])

    emit_router(0)
    emit_router(1)
    emit_main_half(0)

    def emit_dw_load(part=None, nparts=8):
        regs = [x1TrB[0]] + ([x1TrB[1]] if E * D > HREG else [])
        if part is None:
            B.op("pool", lambda e: e.dma_start(
                out=a1[:, 0:E * D].rearrange("p (c d) -> p c d", c=E), in_=dw_in),
                inc=(ch_dw, 16), writes=regs)
            return
        ecs = E // nparts
        B.op("pool", lambda e, part=part, ecs=ecs: e.dma_start(
            out=a1[:, part * ecs * D:(part + 1) * ecs * D].rearrange(
                "p (c d) -> p c d", c=ecs),
            in_=dw_in[:, part * ecs:(part + 1) * ecs, :]),
            inc=(ch_dw, 16),
            deps=[t for b_ in regs for t in (b_.w + b_.r)],
            partial=[regs[0]])

    if E * D <= HREG:
        _dwp = [0]

        def emit_dw_part():
            if _dwp[0] < 8:
                emit_dw_load(part=_dwp[0])
                _dwp[0] += 1
        emit_main_half(1, hook=emit_dw_part)
        while _dwp[0] < 8:
            emit_dw_load(part=_dwp[0])
            _dwp[0] += 1
    else:
        emit_main_half(1)
        emit_dw_load()
    dwbB = x1TrB[0]

    guard4 = B.all_toks()
    HPB2 = max(HREG, E * D)
    sdwbB = _Buf(guard4)
    hpbB = [_Buf(x1TrB[1].w + x1TrB[1].r), _Buf(x1TrB[1].w + x1TrB[1].r)]
    sabB = [_Buf(guard4), _Buf(guard4)]
    xrCB = [_Buf(guard4), _Buf(guard4)]
    outBB = [_Buf(guard4), _Buf(guard4)]
    t2B = _Buf(guard4)
    bnbB2, bnaB2, rstdB2 = _Buf(), _Buf(), _Buf()

    sdwb = bfv(0, 2048)
    sab = [bfv(2048, 3072), bfv(3072, 4096)]
    xrC = [f32v(4096, 5120), f32v(5120, 6144)]
    outb = [f32v(6144, 7168), f32v(7168, 8192)]
    t2 = f32v(8192, 9216)

    B.op("pool", lambda e: e.dma_start(
        out=sdwb.rearrange("p (c d) -> p c d", c=4), in_=sdw_in),
        inc=(ch_sdw, 16), deps=guard4, writes=[sdwbB])

    def load_c2(nc_):
        n0 = nc_ * 512
        slot = nc_ % 2
        hpbase = HPB2 + slot * (E * 512)
        B.op("pool", lambda e, hpbase=hpbase, n0=n0: e.dma_start(
            out=a1[:, hpbase: hpbase + E * 512].rearrange("p (c n) -> p c n", c=E),
            in_=hp_scr[:, :, n0:n0 + 512].rearrange("c p n -> p c n")),
            inc=(ch_hp[slot], 16), reads=[hpsB], writes=[hpbB[slot]])
        B.op("pool", lambda e, s=slot, n0=n0: e.dma_start(
            out=sab[s].rearrange("p (c n) -> p c n", c=4),
            in_=sa_scr[:, :, n0:n0 + 512].rearrange("c p n -> p c n")),
            inc=(ch_sab[slot], 16), reads=[sasB], writes=[sabB[slot]])

    load_c2(0)
    for nc_ in range(N // 512):
        slot = nc_ % 2
        hpbase = HPB2 + slot * (E * 512)
        if nc_ + 1 < N // 512:
            load_c2(nc_ + 1)
        for j in range(4):
            nt = nc_ * 4 + j
            b0, b1 = (0, 1) if nt % 2 == 0 else (2, 3)
            B.op("sp", lambda e, s=nt % 2, nt=nt: e.dma_start(
                out=xrC[s][:, :], in_=x1_scr[nt * 128:(nt + 1) * 128, :]),
                inc=(ch_xrc[nt % 2], 16), reads=[x1sB], writes=[xrCB[nt % 2]])

            def dmm(e, hpbase=hpbase, j=j, b0=b0, b1=b1, s=slot):
                inst = None
                for c in range(E):
                    st = a1[:, hpbase + c * 512 + j * 128: hpbase + c * 512 + (j + 1) * 128]
                    inst = e.matmul(p[b0][:, :], st, a1[:, c * D: c * D + 512],
                                    start=(c == 0), stop=False)
                    inst = e.matmul(p[b1][:, :], st, a1[:, c * D + 512: (c + 1) * D],
                                    start=(c == 0), stop=False)
                for cs in range(4):
                    st = sab[s][:, cs * 512 + j * 128: cs * 512 + (j + 1) * 128]
                    inst = e.matmul(p[b0][:, :], st,
                                    sdwb[:, cs * D: cs * D + 512],
                                    start=False, stop=(cs == 3))
                    inst = e.matmul(p[b1][:, :], st,
                                    sdwb[:, cs * D + 512: (cs + 1) * D],
                                    start=False, stop=(cs == 3))
                return inst
            B.op("pe", dmm, reads=[hpbB[slot], dwbB, sabB[slot], sdwbB],
                 writes=[pB[b0], pB[b1]])

            def res2(e, b0=b0, b1=b1, s=nt % 2):
                e.tensor_tensor(out=t2[:, 0:512], in0=p[b0][:, :],
                                in1=xrC[s][:, 0:512], op=ALU.add)
                return e.tensor_tensor(out=t2[:, 512:1024], in0=p[b1][:, :],
                                       in1=xrC[s][:, 512:1024], op=ALU.add)
            B.op("dve", res2, reads=[pB[b0], pB[b1], xrCB[nt % 2]], writes=[t2B])

            def bns2(e):
                e.bn_stats(bnb[:, 0:6], t2[:, 0:512])
                return e.bn_stats(bnb[:, 6:12], t2[:, 512:1024])
            B.op("dve", bns2, reads=[t2B], writes=[bnbB2])
            B.op("dve", lambda e: e.bn_aggr(bna[:, :], bnb[:, :]),
                 reads=[bnbB2], writes=[bnaB2])
            B.op("dve", lambda e: e.tensor_scalar(
                out=rstd[:, :], in0=bna[:, 1:2], scalar1=EPS, scalar2=None,
                op0=ALU.add), reads=[bnaB2], writes=[rstdB2])
            B.op("act", lambda e: e.activation(
                out=rstd[:, :], in_=rstd[:, :], func=AF.Ln),
                reads=[rstdB2], writes=[rstdB2])
            B.op("act", lambda e: e.activation(
                out=rstd[:, :], in_=rstd[:, :], func=AF.Exp, scale=-0.5),
                reads=[rstdB2], writes=[rstdB2])
            B.op("dve", lambda e, s=nt % 2: e.tensor_scalar(
                out=outb[s][:, :], in0=t2[:, :], scalar1=bna[:, 0:1],
                scalar2=rstd[:, :], op0=ALU.subtract, op1=ALU.mult),
                reads=[t2B, bnaB2, rstdB2], writes=[outBB[nt % 2]])
            out_toks.append(B.op("sp", lambda e, s=nt % 2, nt=nt: e.dma_start(
                out=y_out[nt * 128:(nt + 1) * 128, :], in_=outb[s][:, :]),
                inc=(ch_out[nt % 2], 16), reads=[outBB[nt % 2]]))

    B.op("sp", lambda e: e.nop(), deps=out_toks)
    B.replay(nc, es)
    es.close()
    return nc


def _prep_weights(ins):
    f = lambda a: np.ascontiguousarray(np.asarray(a, dtype=np.float32))
    WQ = f(ins["L0_attn_WQ"])
    WK = f(ins["L0_attn_WK"])
    WV = f(ins["L0_attn_WV"])
    per_k = lambda w: np.ascontiguousarray(
        w.reshape(KD, 128, w.shape[1]).transpose(1, 0, 2))
    gate = f(ins["L0_ffn_gate_w"])
    up = f(ins["L0_ffn_up_w"])
    down = f(ins["L0_ffn_down_w"])
    sg = f(ins["L0_ffn_sgate_w"])
    su = f(ins["L0_ffn_sup_w"])
    sd = f(ins["L0_ffn_sdown_w"])
    rwm = f(ins["L0_ffn_router_w"])
    gT = gate.transpose(2, 0, 1).reshape(D, E * 128)
    uT = up.transpose(2, 0, 1).reshape(D, E * 128)
    dT = down.transpose(0, 2, 1).reshape(E * 128, D)
    sgT = sg.T
    suT = su.T
    sdT = sd.T
    rT = rwm.T
    out = {
        "wq": per_k(WQ), "wk": per_k(WK), "wv": per_k(WV),
        "gw": per_k(gT), "uw": per_k(uT),
        "dw": np.ascontiguousarray(dT.reshape(E, 128, D).transpose(1, 0, 2)),
        "sgw": per_k(sgT), "suw": per_k(suT),
        "sdw": np.ascontiguousarray(sdT.reshape(4, 128, D).transpose(1, 0, 2)),
        "rw": per_k(rT),
        "rb": f(ins["L0_ffn_router_bias"]).reshape(1, E),
        "idn": np.eye(128, dtype=np.float32),
        "cm": np.triu(np.full((128, 128), NEG, dtype=np.float32), k=1),
    }
    return out


def kernel(**inputs):
    S = 4096
    x = np.ascontiguousarray(np.asarray(inputs["L0_attn_x"], dtype=np.float32))
    Bfull = x.shape[0]
    ncores = 8
    per = Bfull // ncores
    w = _prep_weights(inputs)
    nc = build_program(S)
    in_maps = []
    for c in range(ncores):
        xs = x[c * per:(c + 1) * per]
        xTs = np.ascontiguousarray(
            xs.reshape(per, S, KD, 128).transpose(0, 2, 3, 1))
        m = dict(w)
        m["x"] = xs
        m["xT"] = xTs
        in_maps.append(m)
    res = run_bass_kernel_spmd(nc, in_maps, core_ids=list(range(ncores)))
    y = np.concatenate(
        [np.asarray(r["y"], dtype=np.float32).reshape(per, S, D)
         for r in res.results], axis=0)
    return y
